# version 02
# baseline (speedup 1.0000x reference)
import numpy as np
import ml_dtypes
from contextlib import ExitStack

import concourse.bass as bass
import concourse.mybir as mybir
from concourse.bass_utils import run_bass_kernel_spmd

F32 = mybir.dt.float32
BF16 = mybir.dt.bfloat16
ALU = mybir.AluOpType
ACT = mybir.ActivationFunctionType
AXX = mybir.AxisListType.X
NPBF = ml_dtypes.bfloat16

B, T, HS = 2, 8192, 1024
H, R, D = 16, 16, 64
PER = 2048
C = 128
NCH = PER // C
GRP = 512
NG = PER // GRP
E = 8
DFF = 4096
EPS = 1e-5
LN_SCALE = float(np.log(0.125))


class Ctr:
    __slots__ = ("h", "n", "name")

    def __init__(self, h, name):
        self.h = h
        self.n = 0
        self.name = name


class Builder:
    ENGS = ("tensor", "vector", "scalar", "gpsimd", "sync")

    def __init__(self, nc):
        self.nc = nc
        self.prog = {e: [] for e in self.ENGS}
        self.waited = {}

    def sem(self, es, name):
        return Ctr(es.enter_context(self.nc.semaphore(name)), name)

    def op(self, eng, fn, waits=(), inc=None, inck=1, drain=False):
        wl = []
        for ctr, val in waits:
            key = (eng, ctr.name)
            if val > 0 and self.waited.get(key, 0) < val:
                wl.append((ctr.h, val))
                self.waited[key] = val
        if inc is not None:
            inc.n += inck
        semh = inc.h if inc is not None else None

        if eng in ("vector", "scalar") and semh is not None:
            drain = True

        def closure(e, fn=fn, wl=wl, semh=semh, inck=inck, drain=drain):
            for h, v in wl:
                e.wait_ge(h, v)
            inst = fn(e)
            if drain:
                inst = e.drain()
            if semh is not None:
                if inst is None:
                    inst = e.nop()
                inst.then_inc(semh, inck)
        self.prog[eng].append(closure)

    def emit(self, block):
        for eng in self.ENGS:
            ops = self.prog[eng]
            if not ops:
                continue
            getattr(block, eng)(lambda e, ops=ops: [f(e) for f in ops])


def host_prepare(inputs):
    f = np.float32
    qw = np.asarray(inputs["L0_attn_q_w"], f)
    kw = np.asarray(inputs["L0_attn_k_w"], f)
    vw = np.asarray(inputs["L0_attn_v_w"], f)
    ow = np.asarray(inputs["L0_attn_out_w"], f)
    gw = np.asarray(inputs["L0_attn_g_w"], f)
    Wq = np.asarray(inputs["L0_attn_Wq"], f)
    Wk = np.asarray(inputs["L0_attn_Wk"], f)
    gb = np.asarray(inputs["L0_attn_g_b"], f)
    bdl = np.asarray(inputs["L0_attn_base_decay_logits"], f)
    ob = np.asarray(inputs["L0_attn_out_b"], f)
    lng = np.asarray(inputs["L0_attn_ln_g"], f)
    lnb = np.asarray(inputs["L0_attn_ln_b"], f)
    W1 = np.asarray(inputs["L0_ffn_W1"], f)
    b1 = np.asarray(inputs["L0_ffn_b1"], f)
    W2 = np.asarray(inputs["L0_ffn_W2"], f)
    b2 = np.asarray(inputs["L0_ffn_b2"], f)
    Wr = np.asarray(inputs["L0_ffn_Wr"], f)
    br = np.asarray(inputs["L0_ffn_br"], f)
    x = np.asarray(inputs["L0_attn_x"], f)
    cc = np.ascontiguousarray

    shared = {
        "qwT": cc(qw.T), "kwT": cc(kw.T), "vwT": cc(vw.T), "owT": cc(ow.T),
        "w1": cc(W1), "w2": cc(W2), "b2": cc(b2),
    }
    gwTp = np.zeros((HS, 512), f)
    for h in range(H):
        gwTp[:, 32 * h:32 * h + R] = gw[16 * h:16 * h + R, :].T
    shared["gwTp"] = gwTp
    wqbd = np.zeros((8, 128, 64), f)
    wkbd = np.zeros((8, 128, 64), f)
    for j in range(8):
        for u in range(2):
            h = 2 * j + u
            wqbd[j, 64 * u:64 * u + D, 32 * u:32 * u + R] = Wq[h].T
            wkbd[j, 64 * u:64 * u + D, 32 * u:32 * u + R] = Wk[h].T
    shared["wqbd"] = wqbd
    shared["wkbd"] = wkbd
    wr8 = np.zeros((128, 64), f)
    for k in range(8):
        wr8[:, 8 * k:8 * (k + 1)] = Wr[128 * k:128 * (k + 1), :]
    shared["wr8"] = wr8
    b1f = np.zeros((128, 256), f)
    for e in range(E):
        for dc in range(4):
            for m in range(8):
                b1f[:, e * 32 + dc * 8 + m] = b1[e, dc * 1024 + m * 128:dc * 1024 + (m + 1) * 128]
    shared["b1feat"] = b1f
    shared["brRep"] = np.broadcast_to(br, (128, E)).copy()
    shared["obRep"] = np.broadcast_to(ob, (128, HS)).copy()
    shared["lngRep"] = np.broadcast_to(lng, (128, HS)).copy()
    shared["lnbRep"] = np.broadcast_to(lnb, (128, HS)).copy()
    gbpadv = np.zeros((512,), f)
    for h in range(H):
        gbpadv[32 * h:32 * h + R] = gb[16 * h:16 * h + R]
    shared["gbPadT"] = np.broadcast_to(gbpadv, (128, 512)).copy()
    lgp = np.zeros((128, 4), f)
    gbp = np.zeros((128, 4), f)
    for i in range(4):
        for u in range(4):
            h = 4 * i + u
            lgp[32 * u:32 * u + R, i] = bdl[h]
            gbp[32 * u:32 * u + R, i] = gb[16 * h:16 * h + R]
    shared["lgp"] = lgp
    shared["gbPadF"] = gbp

    halfD = D // 2
    invf = (1.0 / 10000.0 ** (np.arange(halfD, dtype=f) / halfD)).astype(f)
    posg = np.arange(T, dtype=f)
    frq = posg[:, None] * invf[None, :]
    cosT = np.cos(frq).astype(f)
    sinT = np.sin(frq).astype(f)
    cext = np.zeros((T, D), f)
    sext = np.zeros((T, D), f)
    cext[:, 0::2] = cosT
    cext[:, 1::2] = cosT
    sext[:, 0::2] = -sinT
    sext[:, 1::2] = sinT
    swapm = np.zeros((128, 128), f)
    for i in range(0, 128, 2):
        swapm[i, i + 1] = 1.0
        swapm[i + 1, i] = 1.0
    ident = np.eye(128, dtype=f)
    lt = np.arange(C, dtype=f)
    maskst = (lt[None, :] >= lt[:, None]).astype(f)
    idx128 = np.broadcast_to(lt, (128, C)).copy()

    cores = []
    for c in range(8):
        b = c // 4
        t0 = (c % 4) * PER
        ce = np.zeros((128, PER), f)
        se = np.zeros((128, PER), f)
        ce[0:64] = cext[t0:t0 + PER].T
        ce[64:128] = ce[0:64]
        se[0:64] = sext[t0:t0 + PER].T
        se[64:128] = se[0:64]
        offs = np.full((4,), 1e9, f)
        p = c % 4
        for m in range(p):
            offs[m] = float(PER * (p - 1 - m))
        core = dict(shared)
        core.update({
            "x": cc(x[b, t0:t0 + PER]),
            "cext": ce.astype(NPBF), "sext": se.astype(NPBF),
            "swapm": swapm.astype(NPBF), "ident": ident.astype(NPBF),
            "maskst": maskst.astype(NPBF),
            "maskt4": np.tile(maskst, (1, 4)).astype(NPBF),
            "idx128": idx128,
            "offs": np.broadcast_to(offs, (128, 4)).copy(),
        })
        cores.append(core)
    return cores


DEBUG = False
_LAST_SEMS = None


def build_nc():
    nc = bass.Bass("TRN2", target_bir_lowering=False)
    es = ExitStack()
    esA = ExitStack()
    esB = ExitStack()

    def dram(name, shape, dt=F32, kind="ExternalInput"):
        return nc.dram_tensor(name, shape, dt, kind=kind).ap()

    X = dram("x", [PER, HS])
    QWT = dram("qwT", [HS, HS]); KWT = dram("kwT", [HS, HS])
    VWT = dram("vwT", [HS, HS]); OWT = dram("owT", [HS, HS])
    GWTP = dram("gwTp", [HS, 512])
    WQBD = dram("wqbd", [8, 128, 64]); WKBD = dram("wkbd", [8, 128, 64])
    W1D = dram("w1", [E, HS, DFF]); W2D = dram("w2", [E, DFF, HS])
    WR8 = dram("wr8", [128, 64]); B2D = dram("b2", [E, HS])
    B1FD = dram("b1feat", [128, 256])
    BRREP = dram("brRep", [128, E])
    OBREP = dram("obRep", [128, HS]); LNGREP = dram("lngRep", [128, HS])
    LNBREP = dram("lnbRep", [128, HS]); GBPADT = dram("gbPadT", [128, 512])
    LGP = dram("lgp", [128, 4]); GBPADF = dram("gbPadF", [128, 4])
    CEXT = dram("cext", [128, PER], BF16); SEXT = dram("sext", [128, PER], BF16)
    SWAPM = dram("swapm", [128, 128], BF16); IDENT = dram("ident", [128, 128], BF16)
    MASKST = dram("maskst", [128, 128], BF16)
    MASKT4 = dram("maskt4", [128, 512], BF16)
    IDX128 = dram("idx128", [128, C])
    OFFS = dram("offs", [128, 4])
    OUT = dram("out", [PER, HS], kind="ExternalOutput")
    if DEBUG:
        DQT = dram("dqt", [128, 4 * PER], BF16, kind="ExternalOutput")
        DYF = dram("dyf", [128, 8 * PER], BF16, kind="ExternalOutput")
        DMST = dram("dmst", [128, 256], F32, kind="ExternalOutput")
        DTAB = dram("dtab", [128, 4 * C * 3 + 512 + 4 * 3], F32, kind="ExternalOutput")
        DAT = dram("dat", [128, 4 * GRP], BF16, kind="ExternalOutput")
        DAK = dram("dak", [128, 4 * 512], BF16, kind="ExternalOutput")
        DVW = dram("dvw", [128, 4096], BF16, kind="ExternalOutput")
        DXF = dram("dxf", [128, 8 * GRP], BF16, kind="ExternalOutput")
        DH1 = dram("dh1", [PER, HS], F32, kind="ExternalOutput")
        DU = dram("du", [PER, HS], F32, kind="ExternalOutput")
        DST2 = dram("dst2", [128, 16 * NCH], F32, kind="ExternalOutput")
        DWTK = dram("dwtk", [128, NCH * E], F32, kind="ExternalOutput")
        DWFE = dram("dwfe", 8, [PER] if False else [8, PER], BF16, kind="ExternalOutput") if False else dram("dwfe", [8, PER], BF16, kind="ExternalOutput")
        DYA = dram("dya", [128, 8192], F32, kind="ExternalOutput")
        DLGT = dram("dlgt", [128, NCH * E], F32, kind="ExternalOutput")
        DGTS = dram("dgts", [128, NCH * 8], F32, kind="ExternalOutput")
        DMSTE = dram("dmste", [128, 1024], BF16, kind="ExternalOutput")
        DSGT = dram("dsgt", [128, 1024], F32, kind="ExternalOutput")
        DOBR = dram("dobr", [128, HS], BF16, kind="ExternalOutput")
        DSTA = dram("dsta", [128, 16 * NCH], F32, kind="ExternalOutput")
    X2TOK = nc.dram_tensor("x2tok", [PER, HS], F32,
                           kind="ExternalOutput" if DEBUG else "Internal").ap()
    CCIN = nc.dram_tensor("ccin", [128, 256], F32)
    CCOUT = nc.dram_tensor("ccout", [512, 256], F32)

    def sbp(stack, name, p, n, dt, side=None):
        kw = {} if side is None else {"side": side}
        return stack.enter_context(nc.sbuf_tensor(name, [p, n], dt, **kw))

    QT = sbp(es, "qt", 128, 4 * PER, BF16)
    XT0 = sbp(es, "xt0", 128, HS, F32)
    XT1 = sbp(es, "xt1", 128, HS, F32)
    H1T = sbp(es, "h1t", 128, HS, F32)
    OUTT0 = sbp(es, "outt0", 128, HS, F32)
    OUTT1 = sbp(es, "outt1", 128, HS, F32)
    X2B = sbp(es, "x2b", 128, HS, BF16)
    IDT = sbp(es, "idt", 128, 128, BF16)
    OBR = sbp(es, "obr", 128, HS, BF16)
    LNGR = sbp(es, "lngr", 128, HS, BF16)
    LNBR = sbp(es, "lnbr", 128, HS, BF16)
    BRR = sbp(es, "brr", 128, E, F32)
    B2B = sbp(es, "b2b", 8, HS, BF16)
    WR8B = sbp(es, "wr8b", 128, 64, BF16)
    WFE = sbp(es, "wfe", 8, PER, BF16)
    WTK = sbp(es, "wtk", 128, NCH * E, F32)
    STA = sbp(es, "sta", 128, 16, F32)
    LGT = sbp(es, "lgt", 128, E, F32)
    GTS = sbp(es, "gts", 128, 8, F32)
    LAMP = sbp(es, "lamp", 128, 4, F32)
    DECP = sbp(es, "decp", 128, 4, F32)
    DECC = sbp(es, "decc", 128, 4, F32)
    DPOW = sbp(es, "dpow", 128, 4 * NCH, F32)
    WCMB = sbp(es, "wcmb", 128, 16, F32)
    MST = sbp(es, "mst", 128, 256, F32)
    MEXP = sbp(es, "mexp", 128, 16 * 64, BF16)
    MSTE = sbp(es, "mste", 128, 16 * 64, BF16)
    SGT = sbp(es, "sgt", 128, 4 * 256, F32)
    XR0 = sbp(es, "xr0", 128, HS, F32)
    XR1 = sbp(es, "xr1", 128, HS, F32)
    LNSC = sbp(es, "lnsc", 128, 1, F32)
    EX8 = sbp(es, "ex8", 128, 8, F32)
    EPSC = sbp(es, "epsc", 128, 1, F32)
    DBGL = sbp(es, "dbgl", 128, NCH * E, F32)
    DBGG = sbp(es, "dbgg", 128, NCH * 8, F32)
    SPCV = sbp(es, "spcv", 128, 1, F32)

    YF = sbp(esB, "yf", 128, 8 * PER, BF16, side="right")

    WSL = [sbp(esA, "wsl0", 128, 8192, BF16, side="right"),
           sbp(esA, "wsl1", 128, 8192, BF16, side="right")]
    GWTB = sbp(esA, "gwtb", 128, 4096, BF16, side="right")
    BDQ = sbp(esA, "bdq", 128, 512, BF16, side="right")
    BDK = sbp(esA, "bdk", 128, 512, BF16, side="right")
    XFW = sbp(esA, "xfw", 128, 8 * GRP, BF16, side="right")
    VW = sbp(esA, "vw", 128, 4096, BF16, side="right")
    ATW = sbp(esA, "atw", 128, 4 * GRP, BF16, side="right")
    AKW = sbp(esA, "akw", 128, 4 * 512, BF16, side="right")
    KRW = sbp(esA, "krw", 128, 8 * GRP, BF16, side="right")
    GFW = sbp(esA, "gfw", 128, 4 * GRP, BF16, side="right")
    GTW = sbp(esA, "gtw", 128, 4 * 512, BF16, side="right")
    RPW = sbp(esA, "rpw", 128, 2 * GRP, BF16, side="right")
    SPW = sbp(esA, "spw", 128, 16 * C, BF16, side="right")
    QHW = sbp(esA, "qhw", 128, 4 * C, BF16, side="right")
    CEXW = sbp(esA, "cexw", 128, 2 * GRP, BF16, side="right")
    SEXW = sbp(esA, "sexw", 128, 2 * GRP, BF16, side="right")
    SWP = sbp(esA, "swp", 128, 128, BF16, side="right")
    MSK = sbp(esA, "msk", 128, 128, BF16, side="right")
    MSKT4 = sbp(esA, "mskt4", 128, 512, BF16, side="right")
    IDXT = sbp(esA, "idxt", 128, C, F32, side="right")
    GBT = sbp(esA, "gbt", 128, 512, F32, side="right")
    GBPF = sbp(esA, "gbpf", 128, 4, F32, side="right")
    LGPT = sbp(esA, "lgpt", 128, 4, F32, side="right")
    OFFT = sbp(esA, "offt", 128, 4, F32, side="right")
    PPOS = sbp(esA, "ppos", 128, 4 * C, F32, side="right")
    PNEG = sbp(esA, "pneg", 128, 4 * C, F32, side="right")
    PREV = sbp(esA, "prev", 128, 4 * C, F32, side="right")
    PRVT = sbp(esA, "prvt", 128, 512, F32, side="right")

    PP0 = es.enter_context(nc.psum_tensor("pp0", [128, 512], F32))
    PP1 = es.enter_context(nc.psum_tensor("pp1", [128, 512], F32))
    PS0 = es.enter_context(nc.psum_tensor("ps0", [128, 512], F32))
    PS1 = es.enter_context(nc.psum_tensor("ps1", [128, 512], F32))
    PY0 = es.enter_context(nc.psum_tensor("py0", [128, 512], F32))
    PY1 = es.enter_context(nc.psum_tensor("py1", [128, 512], F32))
    PU = es.enter_context(nc.psum_tensor("pu", [128, 512], F32))
    PT = es.enter_context(nc.psum_tensor("pt", [128, 1024], BF16))

    bld = Builder(nc)
    sW = bld.sem(es, "sW")
    sIO = bld.sem(es, "sIO")
    sT = bld.sem(es, "sT")
    sV = bld.sem(es, "sV")
    sS = bld.sem(es, "sS")
    sCC = bld.sem(es, "sCC")
    O = bld.op

    def acc_act(s, *a, **kw):
        s.activation(*a, **kw)
        return s.nop()

    def smargin():
        O("scalar", lambda s: s.activation(SPCV[:, :], LNSC[:, :], ACT.Identity), [], sS)
        return sS.n

    def gdma(dst, src, waits=()):
        O("gpsimd", lambda g, d=dst, s=src: g.dma_start(out=d, in_=s), waits, sW, 16)
        return sW.n

    def sdma(dst, src, waits=()):
        O("sync", lambda s_, d=dst, s=src: s_.dma_start(out=d, in_=s), waits, sIO, 16)
        return sIO.n

    sdma(SWP[:, :], SWAPM); sdma(IDT[:, :], IDENT); sdma(MSK[:, :], MASKST)
    sdma(MSKT4[:, :], MASKT4)
    sdma(IDXT[:, :], IDX128)
    sdma(GBT[:, :], GBPADT); sdma(BRR[:, :], BRREP)
    sdma(OFFT[:, :], OFFS); sdma(LGPT[:, :], LGP); sdma(GBPF[:, :], GBPADF)
    gdma(OBR[:, :], OBREP)
    gdma(LNGR[:, :], LNGREP)
    gdma(LNBR[:, :], LNBREP)
    biascast_v = 0
    biasld_w = sW.n

    WSRC = [QWT, KWT, VWT]

    def load_wslot(seq, waits):
        src = WSRC[seq % 3]
        sl = WSL[seq % 2]
        for k in range(8):
            gdma(sl[:, 1024 * k:1024 * (k + 1)], src[128 * k:128 * (k + 1), :], waits)
        return sW.n

    wrdy = {}
    wconsumed_t = {}
    wrdy[0] = load_wslot(0, [])
    wrdy[1] = load_wslot(1, [])
    for k in range(8):
        gdma(GWTB[:, 512 * k:512 * (k + 1)], GWTP[128 * k:128 * (k + 1), :])
        gdma(BDQ[:, 64 * k:64 * (k + 1)], WQBD[k, :, :])
        gdma(BDK[:, 64 * k:64 * (k + 1)], WKBD[k, :, :])
    gdma(WR8B[:, :], WR8)
    gdma(B2B[:, :], B2D)
    wsmall_done = sW.n

    O("vector", lambda v: v.memset(LNSC[:, :], LN_SCALE), [], sV)
    O("vector", lambda v: v.memset(EPSC[:, :], EPS), [], sV)
    O("vector", lambda v: v.memset(SPCV[:, :], 0.0), [], sV)
    lnsc_v = sV.n
    epsc_v = sV.n
    O("vector", lambda v: v.memset(MEXP[:, :], 0.0), [], sV)
    O("vector", lambda v: v.memset(MST[:, :], 0.0), [], sV)
    O("vector", lambda v: v.memset(MSTE[:, :], 0.0), [], sV)

    lam_in = sIO.n
    O("scalar", lambda s: s.activation(STA[:, 0:4], LGPT[:, :], ACT.Exp, scale=-1.0),
      [(sIO, lam_in)], sS, drain=True)
    O("vector", lambda v: v.tensor_scalar(STA[:, 4:8], STA[:, 0:4], 1.0, None, ALU.add),
      [(sS, sS.n)], sV, drain=True)
    O("scalar", lambda s: s.activation(STA[:, 8:12], STA[:, 4:8], ACT.Ln), [(sV, sV.n)], sS,
      drain=True)
    O("vector", lambda v: v.tensor_scalar(LAMP[:, :], STA[:, 8:12], -1.0, None, ALU.mult),
      [(sS, sS.n)], sV, drain=True)
    O("vector", lambda v: v.memset(SPCV[:, :], 0.0), [], sV)
    O("vector", lambda v: v.memset(SPCV[:, :], 0.0), [], sV)
    lam_v = sV.n
    O("scalar", lambda s: s.activation(DECP[:, :], LAMP[:, :], ACT.Exp), [(sV, lam_v)], sS)
    O("scalar", lambda s: s.activation(DECC[:, :], LAMP[:, :], ACT.Exp, scale=float(C)),
      [], sS)
    for i in range(4):
        O("vector", lambda v, i=i: v.tensor_scalar(PPOS[:, C * i:C * (i + 1)], IDXT[:, :],
                                                   LAMP[:, i:i + 1], None, ALU.mult),
          [(sV, lam_v)], sV, drain=True)
        sv = sV.n
        O("scalar", lambda s, i=i: s.activation(PPOS[:, C * i:C * (i + 1)],
                                                PPOS[:, C * i:C * (i + 1)], ACT.Exp),
          [(sV, sv)], sS)
        O("vector", lambda v, i=i: v.tensor_scalar(PNEG[:, C * i:C * (i + 1)], IDXT[:, :],
                                                   LAMP[:, i:i + 1], None, ALU.mult), [], sV,
          drain=True)
        sv = sV.n
        O("scalar", lambda s, i=i: s.activation(PNEG[:, C * i:C * (i + 1)],
                                                PNEG[:, C * i:C * (i + 1)], ACT.Exp,
                                                scale=-1.0, bias=LNSC[:, 0:1]),
          [(sV, sv), (sV, lnsc_v)], sS)
        O("vector", lambda v, i=i: v.tensor_scalar(STA[:, 12:13], LAMP[:, i:i + 1],
                                                   127.0, LN_SCALE, ALU.mult, ALU.add),
          [(sS, sS.n)], sV, drain=True)
        O("vector", lambda v, i=i: v.tensor_scalar(PREV[:, C * i:C * (i + 1)], IDXT[:, :],
                                                   LAMP[:, i:i + 1], None, ALU.mult), [], sV,
          drain=True)
        sv = sV.n
        O("scalar", lambda s, i=i: s.activation(PREV[:, C * i:C * (i + 1)],
                                                PREV[:, C * i:C * (i + 1)], ACT.Exp,
                                                scale=-1.0, bias=STA[:, 12:13]),
          [(sV, sv)], sS)
        for cc_ in range(NCH):
            O("vector", lambda v, i=i, cc_=cc_: v.tensor_scalar(
                DPOW[:, NCH * i + cc_:NCH * i + cc_ + 1], LAMP[:, i:i + 1],
                float(C * cc_ + 1), None, ALU.mult), [], sV,
              drain=(cc_ == NCH - 1))
        sv = sV.n
        O("scalar", lambda s, i=i: s.activation(DPOW[:, NCH * i:NCH * (i + 1)],
                                                DPOW[:, NCH * i:NCH * (i + 1)], ACT.Exp),
          [(sV, sv)], sS)
        O("vector", lambda v, i=i: v.tensor_scalar(WCMB[:, 4 * i:4 * (i + 1)], OFFT[:, :],
                                                   LAMP[:, i:i + 1], None, ALU.mult), [], sV,
          drain=True)
        sv = sV.n
        O("scalar", lambda s, i=i: s.activation(WCMB[:, 4 * i:4 * (i + 1)],
                                                WCMB[:, 4 * i:4 * (i + 1)], ACT.Exp),
          [(sV, sv)], sS)
    tab_s = sS.n

    for i in range(4):
        O("vector", lambda v, i=i: v.tensor_copy(QHW[:, C * i:C * (i + 1)],
                                                 PREV[:, C * i:C * (i + 1)]),
          [(sS, tab_s)], sV)
    prvc = sV.n
    for i in range(4):
        O("tensor", lambda t, i=i: t.transpose(PT[:, 128 * i:128 * (i + 1)],
                                               QHW[:, C * i:C * (i + 1)], IDT[:, :]),
          [(sV, prvc), (sIO, 16 * 3)], sT)
    trp = sT.n
    O("vector", lambda v: v.tensor_copy(PRVT[:, :].rearrange("p (i c) -> p i c", i=4),
                                        PT[:, 0:512].rearrange("p (i c) -> p i c", i=4)),
      [(sT, trp)], sV)
    prvt_v = sV.n

    win_free_v = 0
    pt_free_v = prvt_v
    xe_free_v = [biascast_v, biascast_v]
    spw_evac = {0: 0, 1: 0}
    py_evac = {0: 0, 1: 0}
    cex_ld = [0, 0]

    for g in range(NG):
        g0 = g * GRP
        par = g % 2
        cex_ld[par] = sdma(CEXW[:, GRP * par:GRP * (par + 1)], CEXT[:, g0:g0 + GRP],
                           [(sV, win_free_v)])
        sdma(SEXW[:, GRP * par:GRP * (par + 1)], SEXT[:, g0:g0 + GRP], [])
        rope_ld = sIO.n
        CEXs = CEXW[:, GRP * par:GRP * (par + 1)]
        SEXs = SEXW[:, GRP * par:GRP * (par + 1)]

        for tt in range(4):
            xb = XT0 if tt % 2 == 0 else XT1
            lv = sdma(xb[:, :], X[g0 + 128 * tt:g0 + 128 * (tt + 1), :],
                      [(sV, xe_free_v[tt % 2])])
            O("vector", lambda v, xb=xb: v.tensor_copy(X2B[:, :], xb[:, :]),
              [(sIO, lv), (sV, pt_free_v)], sV)
            cv = sV.n
            xe_free_v[tt % 2] = cv
            for j in range(8):
                O("tensor", lambda t, j=j: t.transpose(PT[:, 128 * j:128 * (j + 1)],
                                                       X2B[:, 128 * j:128 * (j + 1)],
                                                       IDT[:, :]), [(sV, cv)], sT)
            tv = sT.n
            O("vector", lambda v, tt=tt: v.tensor_copy(
                XFW[:, :].rearrange("p (j c) -> p j c", j=8)[:, :, 128 * tt:128 * (tt + 1)],
                PT[:, 0:1024].rearrange("p (j c) -> p j c", j=8)),
              [(sT, tv), (sV, win_free_v)], sV)
            pt_free_v = sV.n
        xfw_v = sV.n

        rp = [PP0, PP1, PS0, PS1]
        for i in range(4):
            for k in range(8):
                O("tensor", lambda t, i=i, k=k: t.matmul(
                    rp[i][:, 0:GRP], GWTB[:, 512 * k + 128 * i:512 * k + 128 * (i + 1)],
                    XFW[:, GRP * k:GRP * (k + 1)], start=(k == 0), stop=(k == 7)),
                  [(sV, xfw_v), (sW, wsmall_done)], sT)
        gf_t = sT.n
        for i in range(4):
            O("scalar", lambda s, i=i: s.activation(GFW[:, GRP * i:GRP * (i + 1)],
                                                    rp[i][:, 0:GRP], ACT.Sigmoid,
                                                    bias=GBPF[:, i:i + 1]),
              [(sT, gf_t), (sIO, lam_in)], sS)
        gf_s = sS.n
        for tt in range(4):
            pb = rp[tt]
            for k in range(8):
                O("tensor", lambda t, tt=tt, k=k, pb=pb: t.matmul(
                    pb[:, 0:512],
                    XFW[:, GRP * k + 128 * tt:GRP * k + 128 * (tt + 1)],
                    GWTB[:, 512 * k:512 * (k + 1)],
                    start=(k == 0), stop=(k == 7)), [(sS, gf_s)], sT)
            gt_t = sT.n
            O("vector", lambda v, tt=tt, pb=pb: v.tensor_tensor(
                GTW[:, 512 * tt:512 * (tt + 1)], pb[:, 0:512], GBT[:, :], ALU.add),
              [(sT, gt_t)], sV)
            sv = sV.n
            O("scalar", lambda s, tt=tt: s.activation(GTW[:, 512 * tt:512 * (tt + 1)],
                                                      GTW[:, 512 * tt:512 * (tt + 1)],
                                                      ACT.Sigmoid), [(sV, sv)], sS)
        gt_s = sS.n

        qsl = WSL[(3 * g) % 2]
        q_rdy = wrdy[3 * g]
        for j in range(8):
            pj = PY0 if j % 2 == 0 else PY1
            for k in range(8):
                O("tensor", lambda t, j=j, k=k, pj=pj, qsl=qsl: t.matmul(
                    pj[:, 0:GRP], qsl[:, 1024 * k + 128 * j:1024 * k + 128 * (j + 1)],
                    XFW[:, GRP * k:GRP * (k + 1)], start=(k == 0), stop=(k == 7)),
                  [(sS, gt_s), (sW, q_rdy), (sV, py_evac[j % 2])], sT)
            pj_t = sT.n
            O("vector", lambda v, pj=pj: v.tensor_copy(RPW[:, 0:GRP], pj[:, 0:GRP]),
              [(sT, pj_t)], sV)
            raw_v = sV.n
            py_evac[j % 2] = raw_v
            O("tensor", lambda t: t.matmul(PU[:, 0:GRP], SWP[:, :], RPW[:, 0:GRP],
                                           start=True, stop=True), [(sV, raw_v)], sT)
            sw_t = sT.n
            O("vector", lambda v, CEXs=CEXs: v.tensor_tensor(
                RPW[:, GRP:2 * GRP], RPW[:, 0:GRP], CEXs, ALU.mult),
              [(sIO, rope_ld)], sV)
            O("vector", lambda v, SEXs=SEXs: v.tensor_tensor(
                RPW[:, 0:GRP], PU[:, 0:GRP], SEXs, ALU.mult), [(sT, sw_t)], sV)
            O("vector", lambda v: v.tensor_tensor(RPW[:, GRP:2 * GRP], RPW[:, GRP:2 * GRP],
                                                  RPW[:, 0:GRP], ALU.add), [], sV)
            rope_v = sV.n
            pbank = rp[j // 2]
            O("tensor", lambda t, j=j, pbank=pbank: t.matmul(
                pbank[64 * (j % 2):64 * (j % 2) + 64, 0:GRP],
                BDQ[:, 64 * j:64 * (j + 1)], RPW[:, GRP:2 * GRP],
                start=True, stop=True), [(sV, rope_v)], sT)
        qr_t = sT.n
        wconsumed_t[3 * g] = qr_t
        if 3 * g + 2 < 3 * NG:
            wrdy[3 * g + 2] = load_wslot(3 * g + 2, [(sT, wconsumed_t.get(3 * g, 0))])
        for i in range(4):
            for wc in range(4):
                O("vector", lambda v, i=i, wc=wc, g0=g0: v.tensor_tensor(
                    QT[:, PER * i + g0 + C * wc:PER * i + g0 + C * (wc + 1)],
                    rp[i][:, C * wc:C * (wc + 1)], PPOS[:, C * i:C * (i + 1)], ALU.mult),
                  [(sT, qr_t)], sV)
        qt_v = sV.n

        ksl = WSL[(3 * g + 1) % 2]
        k_rdy = wrdy[3 * g + 1]
        for j in range(8):
            pj = PY0 if j % 2 == 0 else PY1
            for k in range(8):
                O("tensor", lambda t, j=j, k=k, pj=pj, ksl=ksl: t.matmul(
                    pj[:, 0:GRP], ksl[:, 1024 * k + 128 * j:1024 * k + 128 * (j + 1)],
                    XFW[:, GRP * k:GRP * (k + 1)], start=(k == 0), stop=(k == 7)),
                  [(sV, qt_v), (sW, k_rdy), (sV, py_evac[j % 2])], sT)
            pj_t = sT.n
            O("vector", lambda v, pj=pj: v.tensor_copy(RPW[:, 0:GRP], pj[:, 0:GRP]),
              [(sT, pj_t)], sV)
            raw_v = sV.n
            py_evac[j % 2] = raw_v
            O("tensor", lambda t: t.matmul(PU[:, 0:GRP], SWP[:, :], RPW[:, 0:GRP],
                                           start=True, stop=True), [(sV, raw_v)], sT)
            sw_t = sT.n
            O("vector", lambda v, j=j, CEXs=CEXs: v.tensor_tensor(
                KRW[:, GRP * j:GRP * (j + 1)], RPW[:, 0:GRP], CEXs, ALU.mult), [], sV)
            O("vector", lambda v, SEXs=SEXs: v.tensor_tensor(
                RPW[:, 0:GRP], PU[:, 0:GRP], SEXs, ALU.mult), [(sT, sw_t)], sV)
            O("vector", lambda v, j=j: v.tensor_tensor(KRW[:, GRP * j:GRP * (j + 1)],
                                                       KRW[:, GRP * j:GRP * (j + 1)],
                                                       RPW[:, 0:GRP], ALU.add), [], sV)
        kro_v = sV.n
        wconsumed_t[3 * g + 1] = sT.n
        if 3 * g + 3 < 3 * NG:
            wrdy[3 * g + 3] = load_wslot(3 * g + 3, [(sT, wconsumed_t.get(3 * g + 1, 0))])
        for j in range(8):
            pbank = rp[j // 2]
            O("tensor", lambda t, j=j, pbank=pbank: t.matmul(
                pbank[64 * (j % 2):64 * (j % 2) + 64, 0:GRP],
                BDK[:, 64 * j:64 * (j + 1)], KRW[:, GRP * j:GRP * (j + 1)],
                start=True, stop=True), [(sV, kro_v)], sT)
        krf_t = sT.n
        for i in range(4):
            for wc in range(4):
                O("vector", lambda v, i=i, wc=wc: v.tensor_tensor(
                    SPW[:, 0:C], rp[i][:, C * wc:C * (wc + 1)],
                    GFW[:, GRP * i + C * wc:GRP * i + C * (wc + 1)], ALU.mult),
                  [(sT, krf_t)], sV)
                O("vector", lambda v, i=i, wc=wc: v.tensor_tensor(
                    ATW[:, GRP * i + C * wc:GRP * i + C * (wc + 1)],
                    SPW[:, 0:C], PNEG[:, C * i:C * (i + 1)], ALU.mult), [], sV)
        atw_v = sV.n
        for tt in range(4):
            pb = PY0 if tt % 2 == 0 else PY1
            for j in range(8):
                O("tensor", lambda t, tt=tt, j=j, pb=pb: t.matmul(
                    pb[:, 64 * j:64 * (j + 1)],
                    KRW[:, GRP * j + 128 * tt:GRP * j + 128 * (tt + 1)],
                    BDK[:, 64 * j:64 * (j + 1)], start=True, stop=True),
                  [(sV, atw_v), (sV, py_evac[tt % 2])], sT)
            kt_t = sT.n
            O("vector", lambda v, tt=tt, pb=pb: v.tensor_tensor(
                SPW[:, 0:512], pb[:, 0:512], GTW[:, 512 * tt:512 * (tt + 1)], ALU.mult),
              [(sT, kt_t), (sV, prvt_v)], sV)
            O("vector", lambda v, tt=tt: v.tensor_tensor(
                AKW[:, 512 * tt:512 * (tt + 1)], SPW[:, 0:512], PRVT[:, :], ALU.mult),
              [], sV)
            py_evac[tt % 2] = sV.n
        akw_v = sV.n

        vsl = WSL[(3 * g + 2) % 2]
        v_rdy = wrdy[3 * g + 2]
        for tt in range(4):
            for hh in range(2):
                pb = PY0 if hh == 0 else PY1
                for k in range(8):
                    O("tensor", lambda t, tt=tt, hh=hh, k=k, pb=pb, vsl=vsl: t.matmul(
                        pb[:, 0:512],
                        XFW[:, GRP * k + 128 * tt:GRP * k + 128 * (tt + 1)],
                        vsl[:, 1024 * k + 512 * hh:1024 * k + 512 * (hh + 1)],
                        start=(k == 0), stop=(k == 7)),
                      [(sV, akw_v), (sW, v_rdy), (sV, py_evac[hh])], sT)
                vt_t = sT.n
                O("vector", lambda v, tt=tt, hh=hh, pb=pb: v.tensor_copy(
                    VW[:, 1024 * tt + 512 * hh:1024 * tt + 512 * (hh + 1)], pb[:, 0:512]),
                  [(sT, vt_t)], sV)
                py_evac[hh] = sV.n
        vw_v = sV.n
        if DEBUG and g == 0:
            sdma(DAT, ATW[:, :], [(sV, atw_v)])
            sdma(DAK, AKW[:, :], [(sV, akw_v)])
            sdma(DVW, VW[:, :], [(sV, vw_v)])
            sdma(DXF, XFW[:, :], [(sV, xfw_v)])
        wconsumed_t[3 * g + 2] = sT.n
        if 3 * g + 4 < 3 * NG:
            wrdy[3 * g + 4] = load_wslot(3 * g + 4, [(sT, wconsumed_t[3 * g + 2])])

        for wc in range(4):
            for i in range(4):
                O("vector", lambda v, i=i, wc=wc, g0=g0: v.tensor_scalar(
                    QHW[:, C * i:C * (i + 1)],
                    QT[:, PER * i + g0 + C * wc:PER * i + g0 + C * (wc + 1)],
                    DECP[:, i:i + 1], None, ALU.mult), [(sV, vw_v)], sV)
            qh_v = sV.n
            for h in range(H):
                i, u = h // 4, h % 4
                pb = PS0 if h % 2 == 0 else PS1
                O("tensor", lambda t, i=i, u=u, wc=wc, g0=g0, pb=pb: t.matmul(
                    pb[:, 0:C],
                    ATW[32 * u:32 * u + 32, GRP * i + C * wc:GRP * i + C * (wc + 1)],
                    QT[32 * u:32 * u + 32, PER * i + g0 + C * wc:PER * i + g0 + C * (wc + 1)],
                    start=True, stop=True, tile_position=(32 * u, 0)),
                  [(sV, qh_v), (sV, spw_evac[h % 2])], sT)
                sc_t = sT.n
                O("vector", lambda v, h=h, pb=pb: v.tensor_tensor(
                    SPW[:, C * h:C * (h + 1)], pb[:, 0:C], MSK[:, :], ALU.mult),
                  [(sT, sc_t)], sV)
                spw_evac[h % 2] = sV.n
            sc_v = sV.n
            for h in range(H):
                pb = PY0 if h % 2 == 0 else PY1
                O("tensor", lambda t, h=h, wc=wc, pb=pb: t.matmul(
                    pb[0:64, 0:C], VW[:, 1024 * wc + 64 * h:1024 * wc + 64 * (h + 1)],
                    SPW[:, C * h:C * (h + 1)], start=True, stop=False),
                  [(sV, sc_v), (sV, py_evac[h % 2])], sT)
                O("tensor", lambda t, h=h, pb=pb: t.matmul(
                    pb[0:64, 0:C], MEXP[:, 64 * h:64 * (h + 1)],
                    QHW[:, C * (h // 4):C * (h // 4 + 1)], start=False, stop=True),
                  [], sT)
                y_t = sT.n
                O("vector", lambda v, h=h, g0=g0, wc=wc, pb=pb: v.tensor_copy(
                    YF[64 * (h % 2):64 * (h % 2) + 64,
                       PER * (h // 2) + g0 + C * wc:PER * (h // 2) + g0 + C * (wc + 1)],
                    pb[0:64, 0:C]), [(sT, y_t)], sV)
                py_evac[h % 2] = sV.n
            y_v = sV.n
            for qg in range(4):
                pb = PS0 if qg < 2 else PS1
                O("tensor", lambda t, qg=qg, wc=wc, pb=pb: t.matmul(
                    pb[:, 256 * (qg % 2):256 * (qg % 2) + 256],
                    AKW[:, 512 * wc + 128 * qg:512 * wc + 128 * (qg + 1)],
                    VW[:, 1024 * wc + 256 * qg:1024 * wc + 256 * (qg + 1)],
                    start=True, stop=True), [(sV, y_v)], sT)
            u_t = sT.n
            for h in range(H):
                qg, u = h // 4, h % 4
                pb = PS0 if qg < 2 else PS1
                O("vector", lambda v, qg=qg, u=u, pb=pb: v.scalar_tensor_tensor(
                    MST[32 * u:32 * u + 32, 64 * qg:64 * (qg + 1)],
                    MST[32 * u:32 * u + 32, 64 * qg:64 * (qg + 1)],
                    DECC[32 * u:32 * u + 32, qg:qg + 1],
                    pb[32 * u:32 * u + 32,
                       256 * (qg % 2) + 64 * u:256 * (qg % 2) + 64 * (u + 1)],
                    ALU.mult, ALU.add), [(sT, u_t)], sV)
            for h in range(H):
                qg, u = h // 4, h % 4
                O("vector", lambda v, h=h, qg=qg, u=u: v.tensor_copy(
                    MEXP[32 * u:32 * u + 32, 64 * h:64 * (h + 1)],
                    MST[32 * u:32 * u + 32, 64 * qg:64 * (qg + 1)]), [], sV)
            mup_v = sV.n
            spw_evac[0] = mup_v
            spw_evac[1] = mup_v
            win_free_v = mup_v
    scan_v = sV.n
    scan_t = sT.n
    if DEBUG:
        sdma(DQT, QT[:, :], [(sV, scan_v)])
        sdma(DMST, MST[:, :], [(sV, scan_v)])
        sdma(DTAB[:, 0:4 * C], PPOS[:, :], [(sS, tab_s)])
        sdma(DTAB[:, 4 * C:8 * C], PNEG[:, :], [])
        sdma(DTAB[:, 8 * C:12 * C], PREV[:, :], [])
        sdma(DTAB[:, 12 * C:12 * C + 512], PRVT[:, :], [(sV, prvt_v)])
        sdma(DTAB[:, 12 * C + 512:12 * C + 516], LAMP[:, :], [(sV, scan_v)])
        sdma(DTAB[:, 12 * C + 516:12 * C + 520], DECP[:, :], [])
        sdma(DTAB[:, 12 * C + 520:12 * C + 524], DECC[:, :], [])

    st_out = sdma(CCIN.ap(), MST[:, :], [(sV, scan_v)])
    O("gpsimd", lambda g_: g_.collective_compute(
        "AllGather", ALU.bypass, replica_groups=[[0, 1, 2, 3], [4, 5, 6, 7]],
        ins=[CCIN.ap().opt()], outs=[CCOUT.ap().opt()]),
      [(sIO, st_out)], sCC)
    cc_n = sCC.n
    for m in range(4):
        sdma(SGT[:, 256 * m:256 * (m + 1)], CCOUT.ap()[128 * m:128 * (m + 1), :],
             [(sCC, cc_n)])
    sg_io = sIO.n
    O("vector", lambda v: v.memset(MST[:, :], 0.0), [(sIO, sg_io)], sV)
    for m in range(4):
        for i in range(4):
            O("vector", lambda v, i=i, m=m: v.scalar_tensor_tensor(
                MST[:, 64 * i:64 * (i + 1)],
                SGT[:, 256 * m + 64 * i:256 * m + 64 * (i + 1)],
                WCMB[:, 4 * i + m:4 * i + m + 1],
                MST[:, 64 * i:64 * (i + 1)], ALU.mult, ALU.add), [], sV, drain=True)
    O("vector", lambda v: v.memset(SPCV[:, :], 0.0), [], sV, drain=True)
    for h in range(H):
        qg, u = h // 4, h % 4
        O("vector", lambda v, h=h, qg=qg, u=u: v.tensor_copy(
            MSTE[32 * u:32 * u + 32, 64 * h:64 * (h + 1)],
            MST[32 * u:32 * u + 32, 64 * qg:64 * (qg + 1)]), [], sV)
    mste_v = sV.n
    if DEBUG:
        sdma(DMSTE, MSTE[:, :], [(sV, mste_v)])
        sdma(DSGT, SGT[:, :], [(sV, mste_v)])

    QGW = KRW
    corr_evac = {0: 0, 1: 0}
    for i in range(4):
        for cc_ in range(NCH):
            O("vector", lambda v, i=i, cc_=cc_: v.tensor_scalar(
                QGW[:, C * cc_:C * (cc_ + 1)],
                QT[:, PER * i + C * cc_:PER * i + C * (cc_ + 1)],
                DPOW[:, NCH * i + cc_:NCH * i + cc_ + 1], None, ALU.mult),
              [(sV, mste_v)], sV)
        qg_v = sV.n
        for u in range(4):
            h = 4 * i + u
            for qq in range(4):
                n = u * 4 + qq
                pb = PY0 if n % 2 == 0 else PY1
                O("tensor", lambda t, h=h, qq=qq, pb=pb: t.matmul(
                    pb[0:64, 0:512], MSTE[:, 64 * h:64 * (h + 1)],
                    QGW[:, 512 * qq:512 * (qq + 1)], start=True, stop=True),
                  [(sV, qg_v), (sT, scan_t), (sV, corr_evac[n % 2])], sT)
                co_t = sT.n
                O("vector", lambda v, h=h, qq=qq, pb=pb: v.tensor_tensor(
                    YF[64 * (h % 2):64 * (h % 2) + 64,
                       PER * (h // 2) + 512 * qq:PER * (h // 2) + 512 * (qq + 1)],
                    YF[64 * (h % 2):64 * (h % 2) + 64,
                       PER * (h // 2) + 512 * qq:PER * (h // 2) + 512 * (qq + 1)],
                    pb[0:64, 0:512], ALU.add), [(sT, co_t)], sV)
                corr_evac[n % 2] = sV.n
    corr_v = sV.n
    corr_t = sT.n
    if DEBUG:
        sdma(DYF, YF[:, :], [(sV, corr_v)])

    esA.close()
    FFW = sbp(es, "ffw", 128, 24576, BF16)
    X2F = sbp(es, "x2f", 128, 8 * PER, BF16)

    for k in range(8):
        gdma(FFW[:, 1024 * k:1024 * (k + 1)], OWT[128 * k:128 * (k + 1), :],
             [(sT, corr_t)])
    owt_w = sW.n

    if DEBUG:
        sdma(DOBR, OBR[:, :], [(sV, corr_v)])
    x2tok_wr = [0] * NCH
    x2f_tile = [0] * NCH
    wfe_tile = [0] * NCH
    h1t_free_io = 0
    du_free_io = 0
    dst_free_io = 0
    op_evac = {0: corr_v, 1: corr_v}
    pt_free2 = pt_free_v
    for tt in range(NCH):
        xb = XT0 if tt % 2 == 0 else XT1
        lx = sdma(xb[:, :], X[128 * tt:128 * (tt + 1), :], [(sV, xe_free_v[tt % 2])])
        for half in range(2):
            pb = PY0 if half == 0 else PY1
            for k in range(8):
                O("tensor", lambda t, tt=tt, half=half, k=k, pb=pb: t.matmul(
                    pb[:, 0:512], YF[:, PER * k + 128 * tt:PER * k + 128 * (tt + 1)],
                    FFW[:, 1024 * k + 512 * half:1024 * k + 512 * (half + 1)],
                    start=(k == 0), stop=(k == 7)),
                  [(sV, corr_v), (sW, owt_w), (sV, op_evac[half])], sT)
        op_t = sT.n
        for half, pb in ((0, PY0), (1, PY1)):
            O("vector", lambda v, half=half, pb=pb: v.tensor_tensor(
                H1T[:, 512 * half:512 * (half + 1)], pb[:, 0:512],
                OBR[:, 512 * half:512 * (half + 1)], ALU.add),
              [(sT, op_t), (sIO, lx), (sIO, h1t_free_io), (sW, biasld_w)], sV)
            O("vector", lambda v, half=half, xb=xb: v.tensor_tensor(
                H1T[:, 512 * half:512 * (half + 1)], H1T[:, 512 * half:512 * (half + 1)],
                xb[:, 512 * half:512 * (half + 1)], ALU.add), [], sV)
            op_evac[half] = sV.n
        h1_v = sV.n
        if DEBUG:
            dh1w = sdma(DH1[128 * tt:128 * (tt + 1), :], H1T[:, :], [(sV, h1_v)])
            h1t_free_io = dh1w
        O("vector", lambda v: v.reduce_sum(STA[:, 0:1], H1T[:, :], axis=AXX),
          [(sV, h1_v), (sIO, du_free_io), (sIO, dst_free_io)], sV)
        O("vector", lambda v: v.scalar_tensor_tensor(OUTT1[:, :], H1T[:, :], 1.0,
                                                     H1T[:, :], ALU.mult, ALU.mult), [], sV)
        O("vector", lambda v: v.reduce_sum(STA[:, 1:2], OUTT1[:, :], axis=AXX), [], sV,
          drain=True)
        st1v = sV.n
        O("scalar", lambda s: s.activation(STA[:, 2:3], STA[:, 0:1], ACT.Identity,
                                           scale=1.0 / HS), [(sV, st1v)], sS)
        O("scalar", lambda s: s.activation(STA[:, 3:4], STA[:, 0:1], ACT.Square,
                                           scale=1.0 / HS), [], sS, drain=True)
        st1s = sS.n
        O("vector", lambda v: v.scalar_tensor_tensor(STA[:, 4:5], STA[:, 1:2], 1.0 / HS,
                                                     STA[:, 3:4], ALU.mult, ALU.subtract),
          [(sS, st1s)], sV, drain=True)
        va1 = sV.n
        O("scalar", lambda s: s.activation(STA[:, 6:7], STA[:, 4:5], ACT.Ln,
                                           bias=EPSC[:, 0:1]), [(sV, va1)], sS, drain=True)
        O("scalar", lambda s: s.activation(STA[:, 7:8], STA[:, 6:7], ACT.Exp, scale=-0.5),
          [], sS, drain=True)
        rs1 = sS.n
        O("vector", lambda v: v.tensor_scalar(OUTT0[:, :], H1T[:, :], STA[:, 2:3],
                                              STA[:, 7:8], ALU.subtract, ALU.mult),
          [(sS, rs1)], sV)
        O("vector", lambda v: v.tensor_tensor(OUTT0[:, :], OUTT0[:, :], LNGR[:, :],
                                              ALU.mult), [], sV)
        O("vector", lambda v: v.tensor_tensor(OUTT0[:, :], OUTT0[:, :], LNBR[:, :],
                                              ALU.add), [], sV)
        O("vector", lambda v, xb=xb: v.tensor_tensor(OUTT0[:, :], OUTT0[:, :], xb[:, :],
                                                     ALU.add), [], sV)
        u_v = sV.n
        if DEBUG:
            du_free_io = sdma(DU[128 * tt:128 * (tt + 1), :], OUTT0[:, :], [(sV, u_v)])
        xe_free_v[tt % 2] = u_v
        O("vector", lambda v: v.reduce_sum(STA[:, 8:9], OUTT0[:, :], axis=AXX),
          [(sV, u_v)], sV)
        O("vector", lambda v: v.scalar_tensor_tensor(OUTT1[:, :], OUTT0[:, :], 1.0,
                                                     OUTT0[:, :], ALU.mult, ALU.mult),
          [], sV)
        O("vector", lambda v: v.reduce_sum(STA[:, 9:10], OUTT1[:, :], axis=AXX), [], sV,
          drain=True)
        st2v = sV.n
        O("scalar", lambda s: s.activation(STA[:, 10:11], STA[:, 8:9], ACT.Identity,
                                           scale=1.0 / HS), [(sV, st2v)], sS)
        O("scalar", lambda s: s.activation(STA[:, 11:12], STA[:, 8:9], ACT.Square,
                                           scale=1.0 / HS), [], sS, drain=True)
        st2s = sS.n
        O("vector", lambda v: v.scalar_tensor_tensor(STA[:, 12:13], STA[:, 9:10], 1.0 / HS,
                                                     STA[:, 11:12], ALU.mult,
                                                     ALU.subtract), [(sS, st2s)], sV,
          drain=True)
        va2 = sV.n
        O("scalar", lambda s: s.activation(STA[:, 14:15], STA[:, 12:13], ACT.Ln,
                                           bias=EPSC[:, 0:1]), [(sV, va2)], sS, drain=True)
        O("scalar", lambda s: s.activation(STA[:, 15:16], STA[:, 14:15], ACT.Exp,
                                           scale=-0.5), [], sS, drain=True)
        rs2 = sS.n
        O("vector", lambda v: v.tensor_scalar(H1T[:, :], OUTT0[:, :], STA[:, 10:11],
                                              STA[:, 15:16], ALU.subtract, ALU.mult),
          [(sS, rs2)], sV)
        x2_v = sV.n
        if DEBUG:
            dst_free_io = sdma(DST2[:, 16 * tt:16 * (tt + 1)], STA[:, :], [(sV, x2_v)])
        x2tok_wr[tt] = sdma(X2TOK[128 * tt:128 * (tt + 1), :], H1T[:, :], [(sV, x2_v)])
        h1t_free_io = x2tok_wr[tt]
        O("vector", lambda v: v.tensor_copy(X2B[:, :], H1T[:, :]), [(sV, pt_free2)], sV)
        x2b_v = sV.n
        for j in range(8):
            O("tensor", lambda t, j=j: t.transpose(PT[:, 128 * j:128 * (j + 1)],
                                                   X2B[:, 128 * j:128 * (j + 1)],
                                                   IDT[:, :]), [(sV, x2b_v)], sT)
        tx_t = sT.n
        O("vector", lambda v, tt=tt: v.tensor_copy(
            X2F[:, :].rearrange("p (j c) -> p j c", j=8)[:, :, 128 * tt:128 * (tt + 1)],
            PT[:, 0:1024].rearrange("p (j c) -> p j c", j=8)), [(sT, tx_t)], sV)
        x2f_v = sV.n
        x2f_tile[tt] = x2f_v
        for k in range(8):
            O("tensor", lambda t, tt=tt, k=k: t.matmul(
                PU[:, 0:E], X2F[:, PER * k + 128 * tt:PER * k + 128 * (tt + 1)],
                WR8B[:, E * k:E * (k + 1)], start=(k == 0), stop=(k == 7)),
              [(sV, x2f_v)], sT)
        lg_t = sT.n
        O("vector", lambda v: v.tensor_tensor(LGT[:, :], PU[:, 0:E], BRR[:, :], ALU.add),
          [(sT, lg_t)], sV, drain=True)
        O("vector", lambda v: v.reduce_max(GTS[:, 0:1], LGT[:, :], axis=AXX), [], sV,
          drain=True)
        mx_v = sV.n
        O("scalar", lambda s: s.activation(GTS[:, 1:2], GTS[:, 0:1], ACT.Identity,
                                           scale=-1.0), [(sV, mx_v)], sS, drain=True)
        O("scalar", lambda s: s.activation(EX8[:, :], LGT[:, :], ACT.Exp,
                                           bias=GTS[:, 1:2]), [], sS, drain=True)
        ex_s = sS.n
        O("vector", lambda v: v.reduce_sum(GTS[:, 3:4], EX8[:, :], axis=AXX),
          [(sS, ex_s)], sV, drain=True)
        O("vector", lambda v, tt=tt: v.tensor_scalar(WTK[:, E * tt:E * (tt + 1)],
                                                     LGT[:, :], GTS[:, 0:1], None,
                                                     ALU.is_ge), [], sV)
        O("vector", lambda v: v.reciprocal(GTS[:, 4:5], GTS[:, 3:4]), [], sV, drain=True)
        O("vector", lambda v, tt=tt: v.tensor_scalar(WTK[:, E * tt:E * (tt + 1)],
                                                     WTK[:, E * tt:E * (tt + 1)],
                                                     GTS[:, 4:5], None, ALU.mult), [], sV,
          drain=True)
        if DEBUG:
            O("vector", lambda v, tt=tt: v.tensor_copy(DBGL[:, E * tt:E * (tt + 1)],
                                                       LGT[:, :]), [], sV)
            O("vector", lambda v, tt=tt: v.tensor_copy(DBGG[:, 8 * tt:8 * (tt + 1)],
                                                       GTS[:, :]), [], sV)
        O("vector", lambda v, tt=tt: v.tensor_copy(X2B[:, 0:E],
                                                   WTK[:, E * tt:E * (tt + 1)]), [], sV,
          drain=True)
        wtk_v = sV.n
        O("tensor", lambda t: t.transpose(PT[0:E, 0:128], X2B[:, 0:E], IDT[:, :]),
          [(sV, wtk_v)], sT)
        wt_t = sT.n
        O("vector", lambda v, tt=tt: v.tensor_copy(WFE[0:E, 128 * tt:128 * (tt + 1)],
                                                   PT[0:E, 0:128]), [(sT, wt_t)], sV)
        pt_free2 = sV.n
        wfe_tile[tt] = sV.n
    epiA_v = sV.n
    epiA_t = sT.n
    if DEBUG:
        sdma(DWTK, WTK[:, :], [(sV, epiA_v)])
        sdma(DWFE, WFE[:, :], [(sV, epiA_v)])
        sdma(DLGT, DBGL[:, :], [(sV, epiA_v)])
        sdma(DGTS, DBGG[:, :], [(sV, epiA_v)])

    esB.close()
    YACC = sbp(es, "yacc", 128, 8192, F32)
    HWB = sbp(es, "hwb", 128, 8192, BF16)
    HT2 = sbp(es, "ht2", 128, 4096, BF16)
    B1FT = sbp(es, "b1ft", 128, 256, F32)
    ldb1 = sdma(B1FT[:, :], B1FD, [])

    slots = [8192, 16384, 0]
    loads = []
    for half in range(2):
        for e in range(E):
            for dc in range(4):
                loads.append(("w1", e, dc))
                loads.append(("w2", e, dc))
    slot_prev_t = {8192: corr_t, 16384: corr_t, 0: epiA_t}
    wload_done = [0] * len(loads)
    load_consumer_t = [0] * len(loads)

    def emit_wload(idx):
        kind, e, dc = loads[idx]
        col0 = slots[idx % 3]
        prev_t = load_consumer_t[idx - 3] if idx >= 3 else slot_prev_t[col0]
        if kind == "w1":
            src = W1D[e, :, 1024 * dc:1024 * (dc + 1)].rearrange(
                "(k p) c -> p k c", p=128)
        else:
            src = W2D[e, 1024 * dc:1024 * (dc + 1), :].rearrange(
                "(m p) c -> p m c", p=128)
        dst = FFW[:, col0:col0 + 8192].rearrange("p (k c) -> p k c", c=1024)
        O("gpsimd", lambda g_, dst=dst, src=src: g_.dma_start(out=dst, in_=src),
          [(sT, prev_t)], sW, 16)
        return sW.n

    wload_done[0] = emit_wload(0)
    wload_done[1] = emit_wload(1)
    nextload = 2

    yps = [PY0, PY1, PS0, PS1]
    yev = {0: 0, 1: 0, 2: 0, 3: 0}
    hev = {0: 0, 1: 0}
    hcv = {0: 0, 1: 0}
    hw_used_t = {0: 0, 1: 0}
    xr_free = [0, 0]
    out_done = 0
    ln2_prev_s = 0
    for half in range(2):
        hoff = 1024 * half
        lxs = [0] * 9
        for tt in range(8):
            gtile = half * 8 + tt
            xb = XR0 if tt % 2 == 0 else XR1
            lxs[tt] = sdma(xb[:, :], X2TOK[128 * gtile:128 * (gtile + 1), :],
                           [(sV, xr_free[tt % 2]), (sIO, x2tok_wr[gtile])])
            if tt == 7:
                lxs[8] = sdma(SGT[:, 0:256], X2TOK[128 * gtile:128 * gtile + 128, 0:256],
                              [])
            for hh in range(2):
                bi = (2 * tt + hh) % 4
                pb = yps[bi]
                O("tensor", lambda t, gtile=gtile, hh=hh, pb=pb: t.matmul(
                    pb[:, 0:512], WFE[0:E, 128 * gtile:128 * (gtile + 1)],
                    B2B[0:E, 512 * hh:512 * (hh + 1)], start=True, stop=True),
                  [(sV, wfe_tile[gtile]), (sW, wsmall_done), (sV, yev[bi])], sT)
                b2_t = sT.n
                O("vector", lambda v, tt=tt, hh=hh, xb=xb, pb=pb: v.tensor_tensor(
                    YACC[:, 1024 * tt + 512 * hh:1024 * tt + 512 * (hh + 1)],
                    pb[:, 0:512], xb[:, 512 * hh:512 * (hh + 1)], ALU.add),
                  [(sT, b2_t), (sIO, lxs[tt] + 16), (sS, ln2_prev_s)], sV)
                yev[bi] = sV.n
            xr_free[tt % 2] = sV.n
        yinit_v = sV.n

        for e in range(E):
            for dc in range(4):
                li = 2 * (32 * half + 4 * e + dc)
                w1_rdy = wload_done[li]
                w2_rdy = wload_done[li + 1]
                s1 = slots[li % 3]
                s2 = slots[(li + 1) % 3]
                for gg in range(2):
                    hbuf = 4096 * (gg % 2)
                    gel = [0] * 8
                    for m in range(8):
                        pb = PP0 if m % 2 == 0 else PP1
                        for k in range(8):
                            O("tensor", lambda t, m=m, k=k, gg=gg, s1=s1, hoff=hoff,
                              pb=pb: t.matmul(
                                  pb[:, 0:512],
                                  FFW[:, s1 + 1024 * k + 128 * m:s1 + 1024 * k + 128 * (m + 1)],
                                  X2F[:, PER * k + hoff + 512 * gg:PER * k + hoff + 512 * (gg + 1)],
                                  start=(k == 0), stop=(k == 7)),
                              [(sW, w1_rdy), (sV, x2f_tile[half * 8 + 4 * gg + 3]),
                               (sS, hev[m % 2])], sT)
                        h_t = sT.n
                        O("scalar", lambda s, m=m, e=e, dc=dc, pb=pb:
                          s.activation(HT2[:, 512 * m:512 * (m + 1)],
                                       pb[:, 0:512], ACT.Gelu,
                                       bias=B1FT[:, 32 * e + 8 * dc + m:32 * e + 8 * dc + m + 1]),
                          [(sT, h_t), (sIO, ldb1), (sV, hcv[m % 2])], sS)
                        hev[m % 2] = sS.n
                        gel[m] = sS.n
                        if m >= 2:
                            O("vector", lambda v, m2=m - 2, hbuf=hbuf: v.tensor_copy(
                                HWB[:, hbuf + 512 * m2:hbuf + 512 * (m2 + 1)],
                                HT2[:, 512 * m2:512 * (m2 + 1)]),
                              [(sS, gel[m]), (sT, hw_used_t[gg % 2])], sV)
                            hcv[m % 2] = sV.n
                    smargin()
                    smargin()
                    tailm = smargin()
                    for m2 in (6, 7):
                        O("vector", lambda v, m2=m2, hbuf=hbuf: v.tensor_copy(
                            HWB[:, hbuf + 512 * m2:hbuf + 512 * (m2 + 1)],
                            HT2[:, 512 * m2:512 * (m2 + 1)]),
                          [(sS, tailm), (sT, hw_used_t[gg % 2])], sV)
                        hcv[m2 % 2] = sV.n
                    hg_v = sV.n
                    for t4 in range(4):
                        tt = 4 * gg + t4
                        for hh in range(2):
                            bi = (2 * tt + hh) % 4
                            pb = yps[bi]
                            for m in range(8):
                                O("tensor", lambda t, m=m, hh=hh, t4=t4, hbuf=hbuf,
                                  s2=s2, pb=pb: t.matmul(
                                      pb[:, 0:512],
                                      HWB[:, hbuf + 512 * m + 128 * t4:hbuf + 512 * m + 128 * (t4 + 1)],
                                      FFW[:, s2 + 1024 * m + 512 * hh:s2 + 1024 * m + 512 * (hh + 1)],
                                      start=(m == 0), stop=(m == 7)),
                                  [(sV, hg_v), (sW, w2_rdy), (sV, yev[bi])], sT)
                            yy_t = sT.n
                            O("vector", lambda v, tt=tt, hh=hh, e=e, half=half, pb=pb:
                              v.scalar_tensor_tensor(
                                  YACC[:, 1024 * tt + 512 * hh:1024 * tt + 512 * (hh + 1)],
                                  pb[:, 0:512],
                                  WTK[:, E * (half * 8 + tt) + e:E * (half * 8 + tt) + e + 1],
                                  YACC[:, 1024 * tt + 512 * hh:1024 * tt + 512 * (hh + 1)],
                                  ALU.mult, ALU.add), [(sT, yy_t)], sV)
                            yev[bi] = sV.n
                    hw_used_t[gg % 2] = sT.n
                load_consumer_t[li] = sT.n
                load_consumer_t[li + 1] = sT.n
                if nextload < len(loads):
                    wload_done[nextload] = emit_wload(nextload)
                    nextload += 1
                if nextload < len(loads):
                    wload_done[nextload] = emit_wload(nextload)
                    nextload += 1
        ffn_v = sV.n
        if DEBUG and half == 0:
            sdma(DYA, YACC[:, :], [(sV, ffn_v)])
        for tt in range(8):
            gtile = half * 8 + tt
            ob = OUTT0 if tt % 2 == 0 else OUTT1
            O("vector", lambda v, tt=tt: v.reduce_sum(
                STA[:, 0:1], YACC[:, 1024 * tt:1024 * (tt + 1)], axis=AXX),
              [(sV, ffn_v), (sIO, out_done)], sV)
            O("vector", lambda v, tt=tt, ob=ob: v.scalar_tensor_tensor(
                ob[:, :], YACC[:, 1024 * tt:1024 * (tt + 1)], 1.0,
                YACC[:, 1024 * tt:1024 * (tt + 1)], ALU.mult, ALU.mult), [], sV)
            O("vector", lambda v, ob=ob: v.reduce_sum(STA[:, 1:2], ob[:, :], axis=AXX),
              [], sV, drain=True)
            stfv = sV.n
            O("scalar", lambda s: s.activation(STA[:, 2:3], STA[:, 0:1], ACT.Identity,
                                               scale=1.0 / HS), [(sV, stfv)], sS)
            O("scalar", lambda s: s.activation(STA[:, 3:4], STA[:, 0:1], ACT.Square,
                                               scale=1.0 / HS), [], sS, drain=True)
            stfs = sS.n
            O("vector", lambda v: v.scalar_tensor_tensor(STA[:, 4:5], STA[:, 1:2],
                                                         1.0 / HS, STA[:, 3:4],
                                                         ALU.mult, ALU.subtract),
              [(sS, stfs)], sV, drain=True)
            va = sV.n
            O("scalar", lambda s: s.activation(STA[:, 6:7], STA[:, 4:5], ACT.Ln,
                                               bias=EPSC[:, 0:1]), [(sV, va)], sS,
              drain=True)
            O("scalar", lambda s: s.activation(STA[:, 7:8], STA[:, 6:7], ACT.Exp,
                                               scale=-0.5), [], sS, drain=True)
            rs = sS.n
            O("vector", lambda v, tt=tt, ob=ob: v.tensor_scalar(
                ob[:, :], YACC[:, 1024 * tt:1024 * (tt + 1)], STA[:, 2:3], STA[:, 7:8],
                ALU.subtract, ALU.mult), [(sS, rs)], sV)
            o_v = sV.n
            out_done = sdma(OUT[128 * gtile:128 * (gtile + 1), :], ob[:, :], [(sV, o_v)])
        ln2_prev_s = sS.n

    final_io = sIO.n
    O("sync", lambda s: s.nop(), [(sIO, final_io)], None)

    global _LAST_SEMS
    _LAST_SEMS = {s.name: s.n for s in (sW, sIO, sT, sV, sS, sCC)}
    _LAST_SEMS["_ids"] = {s.name: s.h for s in (sW, sIO, sT, sV, sS, sCC)}
    with nc.Block() as block:
        bld.emit(block)
    es.close()
    return nc


_NC_CACHE = None


def kernel(**inputs):
    global _NC_CACHE
    cores = host_prepare(inputs)
    if _NC_CACHE is None:
        _NC_CACHE = build_nc()
    nc = _NC_CACHE
    res = run_bass_kernel_spmd(nc, cores, core_ids=list(range(8)))
    out = np.empty((B, T, HS), np.float32)
    for c in range(8):
        b = c // 4
        t0 = (c % 4) * PER
        out[b, t0:t0 + PER] = np.asarray(res.results[c]["out"])
    return out


# version 03
# speedup vs baseline: 1.0531x; 1.0531x over previous
import numpy as np
import ml_dtypes
from contextlib import ExitStack

import concourse.bass as bass
import concourse.mybir as mybir
from concourse.bass_utils import run_bass_kernel_spmd

F32 = mybir.dt.float32
BF16 = mybir.dt.bfloat16
ALU = mybir.AluOpType
ACT = mybir.ActivationFunctionType
AXX = mybir.AxisListType.X
NPBF = ml_dtypes.bfloat16

B, T, HS = 2, 8192, 1024
H, R, D = 16, 16, 64
PER = 2048
C = 128
NCH = PER // C
GRP = 512
NG = PER // GRP
E = 8
DFF = 4096
EPS = 1e-5
LN_SCALE = float(np.log(0.125))


class Ctr:
    __slots__ = ("h", "n", "name")

    def __init__(self, h, name):
        self.h = h
        self.n = 0
        self.name = name


class Builder:
    ENGS = ("tensor", "vector", "scalar", "gpsimd", "sync")

    def __init__(self, nc):
        self.nc = nc
        self.prog = {e: [] for e in self.ENGS}
        self.waited = {}

    def sem(self, es, name):
        return Ctr(es.enter_context(self.nc.semaphore(name)), name)

    def op(self, eng, fn, waits=(), inc=None, inck=1, drain=False):
        wl = []
        for ctr, val in waits:
            key = (eng, ctr.name)
            if val > 0 and self.waited.get(key, 0) < val:
                wl.append((ctr.h, val))
                self.waited[key] = val
        if inc is not None:
            inc.n += inck
        semh = inc.h if inc is not None else None

        if eng in ("vector", "scalar") and semh is not None:
            drain = True

        def closure(e, fn=fn, wl=wl, semh=semh, inck=inck, drain=drain):
            for h, v in wl:
                e.wait_ge(h, v)
            inst = fn(e)
            if drain:
                inst = e.drain()
            if semh is not None:
                if inst is None:
                    inst = e.nop()
                inst.then_inc(semh, inck)
        self.prog[eng].append(closure)

    def emit(self, block):
        for eng in self.ENGS:
            ops = self.prog[eng]
            if not ops:
                continue
            getattr(block, eng)(lambda e, ops=ops: [f(e) for f in ops])


def host_prepare(inputs):
    f = np.float32
    qw = np.asarray(inputs["L0_attn_q_w"], f)
    kw = np.asarray(inputs["L0_attn_k_w"], f)
    vw = np.asarray(inputs["L0_attn_v_w"], f)
    ow = np.asarray(inputs["L0_attn_out_w"], f)
    gw = np.asarray(inputs["L0_attn_g_w"], f)
    Wq = np.asarray(inputs["L0_attn_Wq"], f)
    Wk = np.asarray(inputs["L0_attn_Wk"], f)
    gb = np.asarray(inputs["L0_attn_g_b"], f)
    bdl = np.asarray(inputs["L0_attn_base_decay_logits"], f)
    ob = np.asarray(inputs["L0_attn_out_b"], f)
    lng = np.asarray(inputs["L0_attn_ln_g"], f)
    lnb = np.asarray(inputs["L0_attn_ln_b"], f)
    W1 = np.asarray(inputs["L0_ffn_W1"], f)
    b1 = np.asarray(inputs["L0_ffn_b1"], f)
    W2 = np.asarray(inputs["L0_ffn_W2"], f)
    b2 = np.asarray(inputs["L0_ffn_b2"], f)
    Wr = np.asarray(inputs["L0_ffn_Wr"], f)
    br = np.asarray(inputs["L0_ffn_br"], f)
    x = np.asarray(inputs["L0_attn_x"], f)
    cc = np.ascontiguousarray

    shared = {
        "qwT": cc(qw.T), "kwT": cc(kw.T), "vwT": cc(vw.T), "owT": cc(ow.T),
        "w1": cc(W1), "w2": cc(W2), "b2": cc(b2),
    }
    gwTp = np.zeros((HS, 512), f)
    for h in range(H):
        gwTp[:, 32 * h:32 * h + R] = gw[16 * h:16 * h + R, :].T
    shared["gwTp"] = gwTp
    wqbd = np.zeros((8, 128, 64), f)
    wkbd = np.zeros((8, 128, 64), f)
    for j in range(8):
        for u in range(2):
            h = 2 * j + u
            wqbd[j, 64 * u:64 * u + D, 32 * u:32 * u + R] = Wq[h].T
            wkbd[j, 64 * u:64 * u + D, 32 * u:32 * u + R] = Wk[h].T
    shared["wqbd"] = wqbd
    shared["wkbd"] = wkbd
    wr8 = np.zeros((128, 64), f)
    for k in range(8):
        wr8[:, 8 * k:8 * (k + 1)] = Wr[128 * k:128 * (k + 1), :]
    shared["wr8"] = wr8
    b1f = np.zeros((128, 256), f)
    for e in range(E):
        for dc in range(4):
            for m in range(8):
                b1f[:, e * 32 + dc * 8 + m] = b1[e, dc * 1024 + m * 128:dc * 1024 + (m + 1) * 128]
    shared["b1feat"] = b1f
    shared["brRep"] = np.broadcast_to(br, (128, E)).copy()
    shared["obRep"] = np.broadcast_to(ob, (128, HS)).copy()
    shared["lngRep"] = np.broadcast_to(lng, (128, HS)).copy()
    shared["lnbRep"] = np.broadcast_to(lnb, (128, HS)).copy()
    gbpadv = np.zeros((512,), f)
    for h in range(H):
        gbpadv[32 * h:32 * h + R] = gb[16 * h:16 * h + R]
    shared["gbPadT"] = np.broadcast_to(gbpadv, (128, 512)).copy()
    lgp = np.zeros((128, 4), f)
    gbp = np.zeros((128, 4), f)
    for i in range(4):
        for u in range(4):
            h = 4 * i + u
            lgp[32 * u:32 * u + R, i] = bdl[h]
            gbp[32 * u:32 * u + R, i] = gb[16 * h:16 * h + R]
    shared["lgp"] = lgp
    shared["gbPadF"] = gbp

    halfD = D // 2
    invf = (1.0 / 10000.0 ** (np.arange(halfD, dtype=f) / halfD)).astype(f)
    posg = np.arange(T, dtype=f)
    frq = posg[:, None] * invf[None, :]
    cosT = np.cos(frq).astype(f)
    sinT = np.sin(frq).astype(f)
    cext = np.zeros((T, D), f)
    sext = np.zeros((T, D), f)
    cext[:, 0::2] = cosT
    cext[:, 1::2] = cosT
    sext[:, 0::2] = -sinT
    sext[:, 1::2] = sinT
    swapm = np.zeros((128, 128), f)
    for i in range(0, 128, 2):
        swapm[i, i + 1] = 1.0
        swapm[i + 1, i] = 1.0
    ident = np.eye(128, dtype=f)
    lt = np.arange(C, dtype=f)
    maskst = (lt[None, :] >= lt[:, None]).astype(f)
    idx128 = np.broadcast_to(lt, (128, C)).copy()

    cores = []
    for c in range(8):
        b = c // 4
        t0 = (c % 4) * PER
        ce = np.zeros((128, PER), f)
        se = np.zeros((128, PER), f)
        ce[0:64] = cext[t0:t0 + PER].T
        ce[64:128] = ce[0:64]
        se[0:64] = sext[t0:t0 + PER].T
        se[64:128] = se[0:64]
        offs = np.full((4,), 1e9, f)
        p = c % 4
        for m in range(p):
            offs[m] = float(PER * (p - 1 - m))
        core = dict(shared)
        core.update({
            "x": cc(x[b, t0:t0 + PER]),
            "cext": ce.astype(NPBF), "sext": se.astype(NPBF),
            "swapm": swapm.astype(NPBF), "ident": ident.astype(NPBF),
            "maskst": maskst.astype(NPBF),
            "maskt4": np.tile(maskst, (1, 4)).astype(NPBF),
            "idx128": idx128,
            "offs": np.broadcast_to(offs, (128, 4)).copy(),
        })
        cores.append(core)
    return cores


DEBUG = False
_LAST_SEMS = None


def build_nc():
    nc = bass.Bass("TRN2", target_bir_lowering=False)
    es = ExitStack()
    esA = ExitStack()
    esB = ExitStack()

    def dram(name, shape, dt=F32, kind="ExternalInput"):
        return nc.dram_tensor(name, shape, dt, kind=kind).ap()

    X = dram("x", [PER, HS])
    QWT = dram("qwT", [HS, HS]); KWT = dram("kwT", [HS, HS])
    VWT = dram("vwT", [HS, HS]); OWT = dram("owT", [HS, HS])
    GWTP = dram("gwTp", [HS, 512])
    WQBD = dram("wqbd", [8, 128, 64]); WKBD = dram("wkbd", [8, 128, 64])
    W1D = dram("w1", [E, HS, DFF]); W2D = dram("w2", [E, DFF, HS])
    WR8 = dram("wr8", [128, 64]); B2D = dram("b2", [E, HS])
    B1FD = dram("b1feat", [128, 256])
    BRREP = dram("brRep", [128, E])
    OBREP = dram("obRep", [128, HS]); LNGREP = dram("lngRep", [128, HS])
    LNBREP = dram("lnbRep", [128, HS]); GBPADT = dram("gbPadT", [128, 512])
    LGP = dram("lgp", [128, 4]); GBPADF = dram("gbPadF", [128, 4])
    CEXT = dram("cext", [128, PER], BF16); SEXT = dram("sext", [128, PER], BF16)
    SWAPM = dram("swapm", [128, 128], BF16); IDENT = dram("ident", [128, 128], BF16)
    MASKST = dram("maskst", [128, 128], BF16)
    MASKT4 = dram("maskt4", [128, 512], BF16)
    IDX128 = dram("idx128", [128, C])
    OFFS = dram("offs", [128, 4])
    OUT = dram("out", [PER, HS], kind="ExternalOutput")
    if DEBUG:
        DQT = dram("dqt", [128, 4 * PER], BF16, kind="ExternalOutput")
        DYF = dram("dyf", [128, 8 * PER], BF16, kind="ExternalOutput")
        DMST = dram("dmst", [128, 256], F32, kind="ExternalOutput")
        DTAB = dram("dtab", [128, 4 * C * 3 + 512 + 4 * 3], F32, kind="ExternalOutput")
        DAT = dram("dat", [128, 4 * GRP], BF16, kind="ExternalOutput")
        DAK = dram("dak", [128, 4 * 512], BF16, kind="ExternalOutput")
        DVW = dram("dvw", [128, 4096], BF16, kind="ExternalOutput")
        DXF = dram("dxf", [128, 8 * GRP], BF16, kind="ExternalOutput")
        DH1 = dram("dh1", [PER, HS], F32, kind="ExternalOutput")
        DU = dram("du", [PER, HS], F32, kind="ExternalOutput")
        DST2 = dram("dst2", [128, 16 * NCH], F32, kind="ExternalOutput")
        DWTK = dram("dwtk", [128, NCH * E], F32, kind="ExternalOutput")
        DWFE = dram("dwfe", 8, [PER] if False else [8, PER], BF16, kind="ExternalOutput") if False else dram("dwfe", [8, PER], BF16, kind="ExternalOutput")
        DYA = dram("dya", [128, 8192], F32, kind="ExternalOutput")
        DLGT = dram("dlgt", [128, NCH * E], F32, kind="ExternalOutput")
        DGTS = dram("dgts", [128, NCH * 8], F32, kind="ExternalOutput")
        DMSTE = dram("dmste", [128, 1024], BF16, kind="ExternalOutput")
        DSGT = dram("dsgt", [128, 1024], F32, kind="ExternalOutput")
        DOBR = dram("dobr", [128, HS], BF16, kind="ExternalOutput")
        DSTA = dram("dsta", [128, 16 * NCH], F32, kind="ExternalOutput")
    X2TOK = nc.dram_tensor("x2tok", [PER, HS], F32,
                           kind="ExternalOutput" if DEBUG else "Internal").ap()
    CCIN = nc.dram_tensor("ccin", [128, 256], F32)
    CCOUT = nc.dram_tensor("ccout", [512, 256], F32)

    def sbp(stack, name, p, n, dt, side=None):
        kw = {} if side is None else {"side": side}
        return stack.enter_context(nc.sbuf_tensor(name, [p, n], dt, **kw))

    QT = sbp(es, "qt", 128, 4 * PER, BF16)
    XT0 = sbp(es, "xt0", 128, HS, F32)
    XT1 = sbp(es, "xt1", 128, HS, F32)
    H1T = sbp(es, "h1t", 128, HS, F32)
    OUTT0 = sbp(es, "outt0", 128, HS, F32)
    OUTT1 = sbp(es, "outt1", 128, HS, F32)
    X2B = sbp(es, "x2b", 128, HS, BF16)
    IDT = sbp(es, "idt", 128, 128, BF16)
    OBR = sbp(es, "obr", 128, HS, BF16)
    LNGR = sbp(es, "lngr", 128, HS, BF16)
    LNBR = sbp(es, "lnbr", 128, HS, BF16)
    BRR = sbp(es, "brr", 128, E, F32)
    B2B = sbp(es, "b2b", 8, HS, BF16)
    WR8B = sbp(es, "wr8b", 128, 64, BF16)
    WFE = sbp(es, "wfe", 8, PER, BF16)
    WTK = sbp(es, "wtk", 128, NCH * E, F32)
    STA = sbp(es, "sta", 128, 16, F32)
    LGT = sbp(es, "lgt", 128, E, F32)
    GTS = sbp(es, "gts", 128, 8, F32)
    LAMP = sbp(es, "lamp", 128, 4, F32)
    DECP = sbp(es, "decp", 128, 4, F32)
    DECC = sbp(es, "decc", 128, 4, F32)
    DPOW = sbp(es, "dpow", 128, 4 * NCH, F32)
    WCMB = sbp(es, "wcmb", 128, 16, F32)
    MST = sbp(es, "mst", 128, 256, F32)
    MEXP = sbp(es, "mexp", 128, 16 * 64, BF16)
    MSTE = sbp(es, "mste", 128, 16 * 64, BF16)
    SGT = sbp(es, "sgt", 128, 4 * 256, F32)
    XR0 = sbp(es, "xr0", 128, HS, F32)
    XR1 = sbp(es, "xr1", 128, HS, F32)
    LNSC = sbp(es, "lnsc", 128, 1, F32)
    EX8 = sbp(es, "ex8", 128, 8, F32)
    EPSC = sbp(es, "epsc", 128, 1, F32)
    DBGL = sbp(es, "dbgl", 128, NCH * E, F32)
    DBGG = sbp(es, "dbgg", 128, NCH * 8, F32)
    SPCV = sbp(es, "spcv", 128, 1, F32)

    YF = sbp(esB, "yf", 128, 8 * PER, BF16, side="right")

    WSL = [sbp(esA, "wsl0", 128, 8192, BF16, side="right"),
           sbp(esA, "wsl1", 128, 8192, BF16, side="right")]
    GWTB = sbp(esA, "gwtb", 128, 4096, BF16, side="right")
    BDQ = sbp(esA, "bdq", 128, 512, BF16, side="right")
    BDK = sbp(esA, "bdk", 128, 512, BF16, side="right")
    XFW = sbp(esA, "xfw", 128, 8 * GRP, BF16, side="right")
    VW = sbp(esA, "vw", 128, 4096, BF16, side="right")
    ATW = sbp(esA, "atw", 128, 4 * GRP, BF16, side="right")
    AKW = sbp(esA, "akw", 128, 4 * 512, BF16, side="right")
    KRW = sbp(esA, "krw", 128, 8 * GRP, BF16, side="right")
    GFW = sbp(esA, "gfw", 128, 4 * GRP, BF16, side="right")
    GTW = sbp(esA, "gtw", 128, 4 * 512, BF16, side="right")
    RPW = sbp(esA, "rpw", 128, 2 * GRP, BF16, side="right")
    SPW = sbp(esA, "spw", 128, 16 * C, BF16, side="right")
    QHW = sbp(esA, "qhw", 128, 4 * C, BF16, side="right")
    CEXW = sbp(esA, "cexw", 128, 2 * GRP, BF16, side="right")
    SEXW = sbp(esA, "sexw", 128, 2 * GRP, BF16, side="right")
    SWP = sbp(esA, "swp", 128, 128, BF16, side="right")
    MSK = sbp(esA, "msk", 128, 128, BF16, side="right")
    MSKT4 = sbp(esA, "mskt4", 128, 512, BF16, side="right")
    IDXT = sbp(esA, "idxt", 128, C, F32, side="right")
    GBT = sbp(esA, "gbt", 128, 512, F32, side="right")
    GBPF = sbp(esA, "gbpf", 128, 4, F32, side="right")
    LGPT = sbp(esA, "lgpt", 128, 4, F32, side="right")
    OFFT = sbp(esA, "offt", 128, 4, F32, side="right")
    PPOS = sbp(esA, "ppos", 128, 4 * C, F32, side="right")
    PNEG = sbp(esA, "pneg", 128, 4 * C, F32, side="right")
    PREV = sbp(esA, "prev", 128, 4 * C, F32, side="right")
    PRVT = sbp(esA, "prvt", 128, 512, F32, side="right")

    PP0 = es.enter_context(nc.psum_tensor("pp0", [128, 512], F32))
    PP1 = es.enter_context(nc.psum_tensor("pp1", [128, 512], F32))
    PS0 = es.enter_context(nc.psum_tensor("ps0", [128, 512], F32))
    PS1 = es.enter_context(nc.psum_tensor("ps1", [128, 512], F32))
    PY0 = es.enter_context(nc.psum_tensor("py0", [128, 512], F32))
    PY1 = es.enter_context(nc.psum_tensor("py1", [128, 512], F32))
    PU = es.enter_context(nc.psum_tensor("pu", [128, 512], F32))
    PT = es.enter_context(nc.psum_tensor("pt", [128, 1024], BF16))

    bld = Builder(nc)
    sW = bld.sem(es, "sW")
    sIO = bld.sem(es, "sIO")
    sT = bld.sem(es, "sT")
    sV = bld.sem(es, "sV")
    sS = bld.sem(es, "sS")
    sCC = bld.sem(es, "sCC")
    O = bld.op

    def acc_act(s, *a, **kw):
        s.activation(*a, **kw)
        return s.nop()

    def smargin():
        O("scalar", lambda s: s.activation(SPCV[:, :], LNSC[:, :], ACT.Identity), [], sS)
        return sS.n

    def gdma(dst, src, waits=()):
        O("gpsimd", lambda g, d=dst, s=src: g.dma_start(out=d, in_=s), waits, sW, 16)
        return sW.n

    def sdma(dst, src, waits=()):
        O("sync", lambda s_, d=dst, s=src: s_.dma_start(out=d, in_=s), waits, sIO, 16)
        return sIO.n

    sdma(SWP[:, :], SWAPM); sdma(IDT[:, :], IDENT); sdma(MSK[:, :], MASKST)
    sdma(MSKT4[:, :], MASKT4)
    sdma(IDXT[:, :], IDX128)
    sdma(GBT[:, :], GBPADT); sdma(BRR[:, :], BRREP)
    sdma(OFFT[:, :], OFFS); sdma(LGPT[:, :], LGP); sdma(GBPF[:, :], GBPADF)
    gdma(OBR[:, :], OBREP)
    gdma(LNGR[:, :], LNGREP)
    gdma(LNBR[:, :], LNBREP)
    biascast_v = 0
    biasld_w = sW.n

    WSRC = [QWT, KWT, VWT]

    def load_wslot(seq, waits):
        src = WSRC[seq % 3]
        sl = WSL[seq % 2]
        for k in range(8):
            gdma(sl[:, 1024 * k:1024 * (k + 1)], src[128 * k:128 * (k + 1), :], waits)
        return sW.n

    wrdy = {}
    wconsumed_t = {}
    wrdy[0] = load_wslot(0, [])
    wrdy[1] = load_wslot(1, [])
    for k in range(8):
        gdma(GWTB[:, 512 * k:512 * (k + 1)], GWTP[128 * k:128 * (k + 1), :])
        gdma(BDQ[:, 64 * k:64 * (k + 1)], WQBD[k, :, :])
        gdma(BDK[:, 64 * k:64 * (k + 1)], WKBD[k, :, :])
    gdma(WR8B[:, :], WR8)
    gdma(B2B[:, :], B2D)
    wsmall_done = sW.n

    O("vector", lambda v: v.memset(LNSC[:, :], LN_SCALE), [], sV)
    O("vector", lambda v: v.memset(EPSC[:, :], EPS), [], sV)
    O("vector", lambda v: v.memset(SPCV[:, :], 0.0), [], sV)
    lnsc_v = sV.n
    epsc_v = sV.n
    O("vector", lambda v: v.memset(MEXP[:, :], 0.0), [], sV)
    O("vector", lambda v: v.memset(MST[:, :], 0.0), [], sV)
    O("vector", lambda v: v.memset(MSTE[:, :], 0.0), [], sV)

    lam_in = sIO.n
    O("scalar", lambda s: s.activation(STA[:, 0:4], LGPT[:, :], ACT.Exp, scale=-1.0),
      [(sIO, lam_in)], sS, drain=True)
    O("vector", lambda v: v.tensor_scalar(STA[:, 4:8], STA[:, 0:4], 1.0, None, ALU.add),
      [(sS, sS.n)], sV, drain=True)
    O("scalar", lambda s: s.activation(STA[:, 8:12], STA[:, 4:8], ACT.Ln), [(sV, sV.n)], sS,
      drain=True)
    O("vector", lambda v: v.tensor_scalar(LAMP[:, :], STA[:, 8:12], -1.0, None, ALU.mult),
      [(sS, sS.n)], sV, drain=True)
    O("vector", lambda v: v.memset(SPCV[:, :], 0.0), [], sV)
    O("vector", lambda v: v.memset(SPCV[:, :], 0.0), [], sV)
    lam_v = sV.n
    O("scalar", lambda s: s.activation(DECP[:, :], LAMP[:, :], ACT.Exp), [(sV, lam_v)], sS)
    O("scalar", lambda s: s.activation(DECC[:, :], LAMP[:, :], ACT.Exp, scale=float(C)),
      [], sS)
    for i in range(4):
        O("vector", lambda v, i=i: v.tensor_scalar(PPOS[:, C * i:C * (i + 1)], IDXT[:, :],
                                                   LAMP[:, i:i + 1], None, ALU.mult),
          [(sV, lam_v)], sV, drain=True)
        sv = sV.n
        O("scalar", lambda s, i=i: s.activation(PPOS[:, C * i:C * (i + 1)],
                                                PPOS[:, C * i:C * (i + 1)], ACT.Exp),
          [(sV, sv)], sS)
        O("vector", lambda v, i=i: v.tensor_scalar(PNEG[:, C * i:C * (i + 1)], IDXT[:, :],
                                                   LAMP[:, i:i + 1], None, ALU.mult), [], sV,
          drain=True)
        sv = sV.n
        O("scalar", lambda s, i=i: s.activation(PNEG[:, C * i:C * (i + 1)],
                                                PNEG[:, C * i:C * (i + 1)], ACT.Exp,
                                                scale=-1.0, bias=LNSC[:, 0:1]),
          [(sV, sv), (sV, lnsc_v)], sS)
        O("vector", lambda v, i=i: v.tensor_scalar(STA[:, 12:13], LAMP[:, i:i + 1],
                                                   127.0, LN_SCALE, ALU.mult, ALU.add),
          [(sS, sS.n)], sV, drain=True)
        O("vector", lambda v, i=i: v.tensor_scalar(PREV[:, C * i:C * (i + 1)], IDXT[:, :],
                                                   LAMP[:, i:i + 1], None, ALU.mult), [], sV,
          drain=True)
        sv = sV.n
        O("scalar", lambda s, i=i: s.activation(PREV[:, C * i:C * (i + 1)],
                                                PREV[:, C * i:C * (i + 1)], ACT.Exp,
                                                scale=-1.0, bias=STA[:, 12:13]),
          [(sV, sv)], sS)
        for cc_ in range(NCH):
            O("vector", lambda v, i=i, cc_=cc_: v.tensor_scalar(
                DPOW[:, NCH * i + cc_:NCH * i + cc_ + 1], LAMP[:, i:i + 1],
                float(C * cc_ + 1), None, ALU.mult), [], sV,
              drain=(cc_ == NCH - 1))
        sv = sV.n
        O("scalar", lambda s, i=i: s.activation(DPOW[:, NCH * i:NCH * (i + 1)],
                                                DPOW[:, NCH * i:NCH * (i + 1)], ACT.Exp),
          [(sV, sv)], sS)
        O("vector", lambda v, i=i: v.tensor_scalar(WCMB[:, 4 * i:4 * (i + 1)], OFFT[:, :],
                                                   LAMP[:, i:i + 1], None, ALU.mult), [], sV,
          drain=True)
        sv = sV.n
        O("scalar", lambda s, i=i: s.activation(WCMB[:, 4 * i:4 * (i + 1)],
                                                WCMB[:, 4 * i:4 * (i + 1)], ACT.Exp),
          [(sV, sv)], sS)
    tab_s = sS.n

    for i in range(4):
        O("vector", lambda v, i=i: v.tensor_copy(QHW[:, C * i:C * (i + 1)],
                                                 PREV[:, C * i:C * (i + 1)]),
          [(sS, tab_s)], sV)
    prvc = sV.n
    for i in range(4):
        O("tensor", lambda t, i=i: t.transpose(PT[:, 128 * i:128 * (i + 1)],
                                               QHW[:, C * i:C * (i + 1)], IDT[:, :]),
          [(sV, prvc), (sIO, 16 * 3)], sT)
    trp = sT.n
    O("vector", lambda v: v.tensor_copy(PRVT[:, :].rearrange("p (i c) -> p i c", i=4),
                                        PT[:, 0:512].rearrange("p (i c) -> p i c", i=4)),
      [(sT, trp)], sV)
    prvt_v = sV.n

    win_free_v = 0
    pt_free_v = prvt_v
    xe_free_v = [biascast_v, biascast_v]
    spw_evac = {0: 0, 1: 0}
    py_evac = {0: 0, 1: 0}
    cex_ld = [0, 0]

    for g in range(NG):
        g0 = g * GRP
        par = g % 2
        cex_ld[par] = sdma(CEXW[:, GRP * par:GRP * (par + 1)], CEXT[:, g0:g0 + GRP],
                           [(sV, win_free_v)])
        sdma(SEXW[:, GRP * par:GRP * (par + 1)], SEXT[:, g0:g0 + GRP], [])
        rope_ld = sIO.n
        CEXs = CEXW[:, GRP * par:GRP * (par + 1)]
        SEXs = SEXW[:, GRP * par:GRP * (par + 1)]

        for tt in range(4):
            xb = XT0 if tt % 2 == 0 else XT1
            lv = sdma(xb[:, :], X[g0 + 128 * tt:g0 + 128 * (tt + 1), :],
                      [(sV, xe_free_v[tt % 2])])
            O("vector", lambda v, xb=xb: v.tensor_copy(X2B[:, :], xb[:, :]),
              [(sIO, lv), (sV, pt_free_v)], sV)
            cv = sV.n
            xe_free_v[tt % 2] = cv
            for j in range(8):
                O("tensor", lambda t, j=j: t.transpose(PT[:, 128 * j:128 * (j + 1)],
                                                       X2B[:, 128 * j:128 * (j + 1)],
                                                       IDT[:, :]), [(sV, cv)], sT)
            tv = sT.n
            O("vector", lambda v, tt=tt: v.tensor_copy(
                XFW[:, :].rearrange("p (j c) -> p j c", j=8)[:, :, 128 * tt:128 * (tt + 1)],
                PT[:, 0:1024].rearrange("p (j c) -> p j c", j=8)),
              [(sT, tv), (sV, win_free_v)], sV)
            pt_free_v = sV.n
        xfw_v = sV.n

        rp = [PP0, PP1, PS0, PS1]
        for i in range(4):
            for k in range(8):
                O("tensor", lambda t, i=i, k=k: t.matmul(
                    rp[i][:, 0:GRP], GWTB[:, 512 * k + 128 * i:512 * k + 128 * (i + 1)],
                    XFW[:, GRP * k:GRP * (k + 1)], start=(k == 0), stop=(k == 7)),
                  [(sV, xfw_v), (sW, wsmall_done)], sT)
        gf_t = sT.n
        for i in range(4):
            O("scalar", lambda s, i=i: s.activation(GFW[:, GRP * i:GRP * (i + 1)],
                                                    rp[i][:, 0:GRP], ACT.Sigmoid,
                                                    bias=GBPF[:, i:i + 1]),
              [(sT, gf_t), (sIO, lam_in)], sS)
        gf_s = sS.n
        for tt in range(4):
            pb = rp[tt]
            for k in range(8):
                O("tensor", lambda t, tt=tt, k=k, pb=pb: t.matmul(
                    pb[:, 0:512],
                    XFW[:, GRP * k + 128 * tt:GRP * k + 128 * (tt + 1)],
                    GWTB[:, 512 * k:512 * (k + 1)],
                    start=(k == 0), stop=(k == 7)), [(sS, gf_s)], sT)
            gt_t = sT.n
            O("vector", lambda v, tt=tt, pb=pb: v.tensor_tensor(
                GTW[:, 512 * tt:512 * (tt + 1)], pb[:, 0:512], GBT[:, :], ALU.add),
              [(sT, gt_t)], sV)
            sv = sV.n
            O("scalar", lambda s, tt=tt: s.activation(GTW[:, 512 * tt:512 * (tt + 1)],
                                                      GTW[:, 512 * tt:512 * (tt + 1)],
                                                      ACT.Sigmoid), [(sV, sv)], sS)
        gt_s = sS.n

        qsl = WSL[(3 * g) % 2]
        q_rdy = wrdy[3 * g]
        for j in range(8):
            pj = PY0 if j % 2 == 0 else PY1
            for k in range(8):
                O("tensor", lambda t, j=j, k=k, pj=pj, qsl=qsl: t.matmul(
                    pj[:, 0:GRP], qsl[:, 1024 * k + 128 * j:1024 * k + 128 * (j + 1)],
                    XFW[:, GRP * k:GRP * (k + 1)], start=(k == 0), stop=(k == 7)),
                  [(sS, gt_s), (sW, q_rdy), (sV, py_evac[j % 2])], sT)
            pj_t = sT.n
            O("vector", lambda v, pj=pj: v.tensor_copy(RPW[:, 0:GRP], pj[:, 0:GRP]),
              [(sT, pj_t)], sV)
            raw_v = sV.n
            py_evac[j % 2] = raw_v
            O("tensor", lambda t: t.matmul(PU[:, 0:GRP], SWP[:, :], RPW[:, 0:GRP],
                                           start=True, stop=True), [(sV, raw_v)], sT)
            sw_t = sT.n
            O("vector", lambda v, CEXs=CEXs: v.tensor_tensor(
                RPW[:, GRP:2 * GRP], RPW[:, 0:GRP], CEXs, ALU.mult),
              [(sIO, rope_ld)], sV)
            O("vector", lambda v, SEXs=SEXs: v.tensor_tensor(
                RPW[:, 0:GRP], PU[:, 0:GRP], SEXs, ALU.mult), [(sT, sw_t)], sV)
            O("vector", lambda v: v.tensor_tensor(RPW[:, GRP:2 * GRP], RPW[:, GRP:2 * GRP],
                                                  RPW[:, 0:GRP], ALU.add), [], sV)
            rope_v = sV.n
            pbank = rp[j // 2]
            O("tensor", lambda t, j=j, pbank=pbank: t.matmul(
                pbank[64 * (j % 2):64 * (j % 2) + 64, 0:GRP],
                BDQ[:, 64 * j:64 * (j + 1)], RPW[:, GRP:2 * GRP],
                start=True, stop=True), [(sV, rope_v)], sT)
        qr_t = sT.n
        wconsumed_t[3 * g] = qr_t
        if 3 * g + 2 < 3 * NG:
            wrdy[3 * g + 2] = load_wslot(3 * g + 2, [(sT, wconsumed_t.get(3 * g, 0))])
        for i in range(4):
            for wc in range(4):
                O("vector", lambda v, i=i, wc=wc, g0=g0: v.tensor_tensor(
                    QT[:, PER * i + g0 + C * wc:PER * i + g0 + C * (wc + 1)],
                    rp[i][:, C * wc:C * (wc + 1)], PPOS[:, C * i:C * (i + 1)], ALU.mult),
                  [(sT, qr_t)], sV)
        qt_v = sV.n

        ksl = WSL[(3 * g + 1) % 2]
        k_rdy = wrdy[3 * g + 1]
        for j in range(8):
            pj = PY0 if j % 2 == 0 else PY1
            for k in range(8):
                O("tensor", lambda t, j=j, k=k, pj=pj, ksl=ksl: t.matmul(
                    pj[:, 0:GRP], ksl[:, 1024 * k + 128 * j:1024 * k + 128 * (j + 1)],
                    XFW[:, GRP * k:GRP * (k + 1)], start=(k == 0), stop=(k == 7)),
                  [(sV, qt_v), (sW, k_rdy), (sV, py_evac[j % 2])], sT)
            pj_t = sT.n
            O("vector", lambda v, pj=pj: v.tensor_copy(RPW[:, 0:GRP], pj[:, 0:GRP]),
              [(sT, pj_t)], sV)
            raw_v = sV.n
            py_evac[j % 2] = raw_v
            O("tensor", lambda t: t.matmul(PU[:, 0:GRP], SWP[:, :], RPW[:, 0:GRP],
                                           start=True, stop=True), [(sV, raw_v)], sT)
            sw_t = sT.n
            O("vector", lambda v, j=j, CEXs=CEXs: v.tensor_tensor(
                KRW[:, GRP * j:GRP * (j + 1)], RPW[:, 0:GRP], CEXs, ALU.mult), [], sV)
            O("vector", lambda v, SEXs=SEXs: v.tensor_tensor(
                RPW[:, 0:GRP], PU[:, 0:GRP], SEXs, ALU.mult), [(sT, sw_t)], sV)
            O("vector", lambda v, j=j: v.tensor_tensor(KRW[:, GRP * j:GRP * (j + 1)],
                                                       KRW[:, GRP * j:GRP * (j + 1)],
                                                       RPW[:, 0:GRP], ALU.add), [], sV)
        kro_v = sV.n
        wconsumed_t[3 * g + 1] = sT.n
        if 3 * g + 3 < 3 * NG:
            wrdy[3 * g + 3] = load_wslot(3 * g + 3, [(sT, wconsumed_t.get(3 * g + 1, 0))])
        for j in range(8):
            pbank = rp[j // 2]
            O("tensor", lambda t, j=j, pbank=pbank: t.matmul(
                pbank[64 * (j % 2):64 * (j % 2) + 64, 0:GRP],
                BDK[:, 64 * j:64 * (j + 1)], KRW[:, GRP * j:GRP * (j + 1)],
                start=True, stop=True), [(sV, kro_v)], sT)
        krf_t = sT.n
        for i in range(4):
            for wc in range(4):
                O("vector", lambda v, i=i, wc=wc: v.tensor_tensor(
                    SPW[:, 0:C], rp[i][:, C * wc:C * (wc + 1)],
                    GFW[:, GRP * i + C * wc:GRP * i + C * (wc + 1)], ALU.mult),
                  [(sT, krf_t)], sV)
                O("vector", lambda v, i=i, wc=wc: v.tensor_tensor(
                    ATW[:, GRP * i + C * wc:GRP * i + C * (wc + 1)],
                    SPW[:, 0:C], PNEG[:, C * i:C * (i + 1)], ALU.mult), [], sV)
        atw_v = sV.n
        for tt in range(4):
            pb = PY0 if tt % 2 == 0 else PY1
            for j in range(8):
                O("tensor", lambda t, tt=tt, j=j, pb=pb: t.matmul(
                    pb[:, 64 * j:64 * (j + 1)],
                    KRW[:, GRP * j + 128 * tt:GRP * j + 128 * (tt + 1)],
                    BDK[:, 64 * j:64 * (j + 1)], start=True, stop=True),
                  [(sV, atw_v), (sV, py_evac[tt % 2])], sT)
            kt_t = sT.n
            O("vector", lambda v, tt=tt, pb=pb: v.tensor_tensor(
                SPW[:, 0:512], pb[:, 0:512], GTW[:, 512 * tt:512 * (tt + 1)], ALU.mult),
              [(sT, kt_t), (sV, prvt_v)], sV)
            O("vector", lambda v, tt=tt: v.tensor_tensor(
                AKW[:, 512 * tt:512 * (tt + 1)], SPW[:, 0:512], PRVT[:, :], ALU.mult),
              [], sV)
            py_evac[tt % 2] = sV.n
        akw_v = sV.n

        vsl = WSL[(3 * g + 2) % 2]
        v_rdy = wrdy[3 * g + 2]
        for tt in range(4):
            for hh in range(2):
                pb = PY0 if hh == 0 else PY1
                for k in range(8):
                    O("tensor", lambda t, tt=tt, hh=hh, k=k, pb=pb, vsl=vsl: t.matmul(
                        pb[:, 0:512],
                        XFW[:, GRP * k + 128 * tt:GRP * k + 128 * (tt + 1)],
                        vsl[:, 1024 * k + 512 * hh:1024 * k + 512 * (hh + 1)],
                        start=(k == 0), stop=(k == 7)),
                      [(sV, akw_v), (sW, v_rdy), (sV, py_evac[hh])], sT)
                vt_t = sT.n
                O("vector", lambda v, tt=tt, hh=hh, pb=pb: v.tensor_copy(
                    VW[:, 1024 * tt + 512 * hh:1024 * tt + 512 * (hh + 1)], pb[:, 0:512]),
                  [(sT, vt_t)], sV)
                py_evac[hh] = sV.n
        vw_v = sV.n
        if DEBUG and g == 0:
            sdma(DAT, ATW[:, :], [(sV, atw_v)])
            sdma(DAK, AKW[:, :], [(sV, akw_v)])
            sdma(DVW, VW[:, :], [(sV, vw_v)])
            sdma(DXF, XFW[:, :], [(sV, xfw_v)])
        wconsumed_t[3 * g + 2] = sT.n
        if 3 * g + 4 < 3 * NG:
            wrdy[3 * g + 4] = load_wslot(3 * g + 4, [(sT, wconsumed_t[3 * g + 2])])

        for wc in range(4):
            for i in range(4):
                O("vector", lambda v, i=i, wc=wc, g0=g0: v.tensor_scalar(
                    QHW[:, C * i:C * (i + 1)],
                    QT[:, PER * i + g0 + C * wc:PER * i + g0 + C * (wc + 1)],
                    DECP[:, i:i + 1], None, ALU.mult), [(sV, vw_v)], sV)
            qh_v = sV.n
            for h in range(H):
                i, u = h // 4, h % 4
                pb = PS0 if h % 2 == 0 else PS1
                O("tensor", lambda t, i=i, u=u, wc=wc, g0=g0, pb=pb: t.matmul(
                    pb[:, 0:C],
                    ATW[32 * u:32 * u + 32, GRP * i + C * wc:GRP * i + C * (wc + 1)],
                    QT[32 * u:32 * u + 32, PER * i + g0 + C * wc:PER * i + g0 + C * (wc + 1)],
                    start=True, stop=True, tile_position=(32 * u, 0)),
                  [(sV, qh_v), (sV, spw_evac[h % 2])], sT)
                sc_t = sT.n
                O("vector", lambda v, h=h, pb=pb: v.tensor_tensor(
                    SPW[:, C * h:C * (h + 1)], pb[:, 0:C], MSK[:, :], ALU.mult),
                  [(sT, sc_t)], sV)
                spw_evac[h % 2] = sV.n
            sc_v = sV.n
            for h in range(H):
                pb = PY0 if h % 2 == 0 else PY1
                O("tensor", lambda t, h=h, wc=wc, pb=pb: t.matmul(
                    pb[0:64, 0:C], VW[:, 1024 * wc + 64 * h:1024 * wc + 64 * (h + 1)],
                    SPW[:, C * h:C * (h + 1)], start=True, stop=False),
                  [(sV, sc_v), (sV, py_evac[h % 2])], sT)
                O("tensor", lambda t, h=h, pb=pb: t.matmul(
                    pb[0:64, 0:C], MEXP[:, 64 * h:64 * (h + 1)],
                    QHW[:, C * (h // 4):C * (h // 4 + 1)], start=False, stop=True),
                  [], sT)
                y_t = sT.n
                O("vector", lambda v, h=h, g0=g0, wc=wc, pb=pb: v.tensor_copy(
                    YF[64 * (h % 2):64 * (h % 2) + 64,
                       PER * (h // 2) + g0 + C * wc:PER * (h // 2) + g0 + C * (wc + 1)],
                    pb[0:64, 0:C]), [(sT, y_t)], sV)
                py_evac[h % 2] = sV.n
            y_v = sV.n
            for qg in range(4):
                pb = PS0 if qg < 2 else PS1
                O("tensor", lambda t, qg=qg, wc=wc, pb=pb: t.matmul(
                    pb[:, 256 * (qg % 2):256 * (qg % 2) + 256],
                    AKW[:, 512 * wc + 128 * qg:512 * wc + 128 * (qg + 1)],
                    VW[:, 1024 * wc + 256 * qg:1024 * wc + 256 * (qg + 1)],
                    start=True, stop=True), [(sV, y_v)], sT)
            u_t = sT.n
            for h in range(H):
                qg, u = h // 4, h % 4
                pb = PS0 if qg < 2 else PS1
                O("vector", lambda v, qg=qg, u=u, pb=pb: v.scalar_tensor_tensor(
                    MST[32 * u:32 * u + 32, 64 * qg:64 * (qg + 1)],
                    MST[32 * u:32 * u + 32, 64 * qg:64 * (qg + 1)],
                    DECC[32 * u:32 * u + 32, qg:qg + 1],
                    pb[32 * u:32 * u + 32,
                       256 * (qg % 2) + 64 * u:256 * (qg % 2) + 64 * (u + 1)],
                    ALU.mult, ALU.add), [(sT, u_t)], sV)
            for h in range(H):
                qg, u = h // 4, h % 4
                O("vector", lambda v, h=h, qg=qg, u=u: v.tensor_copy(
                    MEXP[32 * u:32 * u + 32, 64 * h:64 * (h + 1)],
                    MST[32 * u:32 * u + 32, 64 * qg:64 * (qg + 1)]), [], sV)
            mup_v = sV.n
            spw_evac[0] = mup_v
            spw_evac[1] = mup_v
            win_free_v = mup_v
    scan_v = sV.n
    scan_t = sT.n
    if DEBUG:
        sdma(DQT, QT[:, :], [(sV, scan_v)])
        sdma(DMST, MST[:, :], [(sV, scan_v)])
        sdma(DTAB[:, 0:4 * C], PPOS[:, :], [(sS, tab_s)])
        sdma(DTAB[:, 4 * C:8 * C], PNEG[:, :], [])
        sdma(DTAB[:, 8 * C:12 * C], PREV[:, :], [])
        sdma(DTAB[:, 12 * C:12 * C + 512], PRVT[:, :], [(sV, prvt_v)])
        sdma(DTAB[:, 12 * C + 512:12 * C + 516], LAMP[:, :], [(sV, scan_v)])
        sdma(DTAB[:, 12 * C + 516:12 * C + 520], DECP[:, :], [])
        sdma(DTAB[:, 12 * C + 520:12 * C + 524], DECC[:, :], [])

    st_out = sdma(CCIN.ap(), MST[:, :], [(sV, scan_v)])
    O("gpsimd", lambda g_: g_.collective_compute(
        "AllGather", ALU.bypass, replica_groups=[[0, 1, 2, 3], [4, 5, 6, 7]],
        ins=[CCIN.ap().opt()], outs=[CCOUT.ap().opt()]),
      [(sIO, st_out)], sCC)
    cc_n = sCC.n
    for m in range(4):
        sdma(SGT[:, 256 * m:256 * (m + 1)], CCOUT.ap()[128 * m:128 * (m + 1), :],
             [(sCC, cc_n)])
    sg_io = sIO.n
    O("vector", lambda v: v.memset(MST[:, :], 0.0), [(sIO, sg_io)], sV)
    for m in range(4):
        for i in range(4):
            O("vector", lambda v, i=i, m=m: v.scalar_tensor_tensor(
                MST[:, 64 * i:64 * (i + 1)],
                SGT[:, 256 * m + 64 * i:256 * m + 64 * (i + 1)],
                WCMB[:, 4 * i + m:4 * i + m + 1],
                MST[:, 64 * i:64 * (i + 1)], ALU.mult, ALU.add), [], sV, drain=True)
    O("vector", lambda v: v.memset(SPCV[:, :], 0.0), [], sV, drain=True)
    for h in range(H):
        qg, u = h // 4, h % 4
        O("vector", lambda v, h=h, qg=qg, u=u: v.tensor_copy(
            MSTE[32 * u:32 * u + 32, 64 * h:64 * (h + 1)],
            MST[32 * u:32 * u + 32, 64 * qg:64 * (qg + 1)]), [], sV)
    mste_v = sV.n
    if DEBUG:
        sdma(DMSTE, MSTE[:, :], [(sV, mste_v)])
        sdma(DSGT, SGT[:, :], [(sV, mste_v)])

    QGW = KRW
    corr_evac = {0: 0, 1: 0}
    for i in range(4):
        for cc_ in range(NCH):
            O("vector", lambda v, i=i, cc_=cc_: v.tensor_scalar(
                QGW[:, C * cc_:C * (cc_ + 1)],
                QT[:, PER * i + C * cc_:PER * i + C * (cc_ + 1)],
                DPOW[:, NCH * i + cc_:NCH * i + cc_ + 1], None, ALU.mult),
              [(sV, mste_v)], sV)
        qg_v = sV.n
        for u in range(4):
            h = 4 * i + u
            for qq in range(4):
                n = u * 4 + qq
                pb = PY0 if n % 2 == 0 else PY1
                O("tensor", lambda t, h=h, qq=qq, pb=pb: t.matmul(
                    pb[0:64, 0:512], MSTE[:, 64 * h:64 * (h + 1)],
                    QGW[:, 512 * qq:512 * (qq + 1)], start=True, stop=True),
                  [(sV, qg_v), (sT, scan_t), (sV, corr_evac[n % 2])], sT)
                co_t = sT.n
                O("vector", lambda v, h=h, qq=qq, pb=pb: v.tensor_tensor(
                    YF[64 * (h % 2):64 * (h % 2) + 64,
                       PER * (h // 2) + 512 * qq:PER * (h // 2) + 512 * (qq + 1)],
                    YF[64 * (h % 2):64 * (h % 2) + 64,
                       PER * (h // 2) + 512 * qq:PER * (h // 2) + 512 * (qq + 1)],
                    pb[0:64, 0:512], ALU.add), [(sT, co_t)], sV)
                corr_evac[n % 2] = sV.n
    corr_v = sV.n
    corr_t = sT.n
    if DEBUG:
        sdma(DYF, YF[:, :], [(sV, corr_v)])

    esA.close()
    FFW = sbp(es, "ffw", 128, 24576, BF16)
    X2F = sbp(es, "x2f", 128, 8 * PER, BF16)

    for k in range(8):
        gdma(FFW[:, 1024 * k:1024 * (k + 1)], OWT[128 * k:128 * (k + 1), :],
             [(sT, corr_t)])
    owt_w = sW.n

    if DEBUG:
        sdma(DOBR, OBR[:, :], [(sV, corr_v)])
    x2tok_wr = [0] * NCH
    x2f_tile = [0] * NCH
    wfe_tile = [0] * NCH
    h1t_free_io = 0
    du_free_io = 0
    dst_free_io = 0
    op_evac = {0: corr_v, 1: corr_v}
    pt_free2 = pt_free_v
    for tt in range(NCH):
        xb = XT0 if tt % 2 == 0 else XT1
        lx = sdma(xb[:, :], X[128 * tt:128 * (tt + 1), :], [(sV, xe_free_v[tt % 2])])
        for half in range(2):
            pb = PY0 if half == 0 else PY1
            for k in range(8):
                O("tensor", lambda t, tt=tt, half=half, k=k, pb=pb: t.matmul(
                    pb[:, 0:512], YF[:, PER * k + 128 * tt:PER * k + 128 * (tt + 1)],
                    FFW[:, 1024 * k + 512 * half:1024 * k + 512 * (half + 1)],
                    start=(k == 0), stop=(k == 7)),
                  [(sV, corr_v), (sW, owt_w), (sV, op_evac[half])], sT)
        op_t = sT.n
        for half, pb in ((0, PY0), (1, PY1)):
            O("vector", lambda v, half=half, pb=pb: v.tensor_tensor(
                H1T[:, 512 * half:512 * (half + 1)], pb[:, 0:512],
                OBR[:, 512 * half:512 * (half + 1)], ALU.add),
              [(sT, op_t), (sIO, lx), (sIO, h1t_free_io), (sW, biasld_w)], sV)
            O("vector", lambda v, half=half, xb=xb: v.tensor_tensor(
                H1T[:, 512 * half:512 * (half + 1)], H1T[:, 512 * half:512 * (half + 1)],
                xb[:, 512 * half:512 * (half + 1)], ALU.add), [], sV)
            op_evac[half] = sV.n
        h1_v = sV.n
        if DEBUG:
            dh1w = sdma(DH1[128 * tt:128 * (tt + 1), :], H1T[:, :], [(sV, h1_v)])
            h1t_free_io = dh1w
        O("vector", lambda v: v.reduce_sum(STA[:, 0:1], H1T[:, :], axis=AXX),
          [(sV, h1_v), (sIO, du_free_io), (sIO, dst_free_io)], sV)
        O("vector", lambda v: v.scalar_tensor_tensor(OUTT1[:, :], H1T[:, :], 1.0,
                                                     H1T[:, :], ALU.mult, ALU.mult), [], sV)
        O("vector", lambda v: v.reduce_sum(STA[:, 1:2], OUTT1[:, :], axis=AXX), [], sV,
          drain=True)
        st1v = sV.n
        O("scalar", lambda s: s.activation(STA[:, 2:3], STA[:, 0:1], ACT.Identity,
                                           scale=1.0 / HS), [(sV, st1v)], sS)
        O("scalar", lambda s: s.activation(STA[:, 3:4], STA[:, 0:1], ACT.Square,
                                           scale=1.0 / HS), [], sS, drain=True)
        st1s = sS.n
        O("vector", lambda v: v.scalar_tensor_tensor(STA[:, 4:5], STA[:, 1:2], 1.0 / HS,
                                                     STA[:, 3:4], ALU.mult, ALU.subtract),
          [(sS, st1s)], sV, drain=True)
        va1 = sV.n
        O("scalar", lambda s: s.activation(STA[:, 6:7], STA[:, 4:5], ACT.Ln,
                                           bias=EPSC[:, 0:1]), [(sV, va1)], sS, drain=True)
        O("scalar", lambda s: s.activation(STA[:, 7:8], STA[:, 6:7], ACT.Exp, scale=-0.5),
          [], sS, drain=True)
        rs1 = sS.n
        O("vector", lambda v: v.tensor_scalar(OUTT0[:, :], H1T[:, :], STA[:, 2:3],
                                              STA[:, 7:8], ALU.subtract, ALU.mult),
          [(sS, rs1)], sV)
        O("vector", lambda v: v.tensor_tensor(OUTT0[:, :], OUTT0[:, :], LNGR[:, :],
                                              ALU.mult), [], sV)
        O("vector", lambda v: v.tensor_tensor(OUTT0[:, :], OUTT0[:, :], LNBR[:, :],
                                              ALU.add), [], sV)
        O("vector", lambda v, xb=xb: v.tensor_tensor(OUTT0[:, :], OUTT0[:, :], xb[:, :],
                                                     ALU.add), [], sV)
        u_v = sV.n
        if DEBUG:
            du_free_io = sdma(DU[128 * tt:128 * (tt + 1), :], OUTT0[:, :], [(sV, u_v)])
        xe_free_v[tt % 2] = u_v
        O("vector", lambda v: v.reduce_sum(STA[:, 8:9], OUTT0[:, :], axis=AXX),
          [(sV, u_v)], sV)
        O("vector", lambda v: v.scalar_tensor_tensor(OUTT1[:, :], OUTT0[:, :], 1.0,
                                                     OUTT0[:, :], ALU.mult, ALU.mult),
          [], sV)
        O("vector", lambda v: v.reduce_sum(STA[:, 9:10], OUTT1[:, :], axis=AXX), [], sV,
          drain=True)
        st2v = sV.n
        O("scalar", lambda s: s.activation(STA[:, 10:11], STA[:, 8:9], ACT.Identity,
                                           scale=1.0 / HS), [(sV, st2v)], sS)
        O("scalar", lambda s: s.activation(STA[:, 11:12], STA[:, 8:9], ACT.Square,
                                           scale=1.0 / HS), [], sS, drain=True)
        st2s = sS.n
        O("vector", lambda v: v.scalar_tensor_tensor(STA[:, 12:13], STA[:, 9:10], 1.0 / HS,
                                                     STA[:, 11:12], ALU.mult,
                                                     ALU.subtract), [(sS, st2s)], sV,
          drain=True)
        va2 = sV.n
        O("scalar", lambda s: s.activation(STA[:, 14:15], STA[:, 12:13], ACT.Ln,
                                           bias=EPSC[:, 0:1]), [(sV, va2)], sS, drain=True)
        O("scalar", lambda s: s.activation(STA[:, 15:16], STA[:, 14:15], ACT.Exp,
                                           scale=-0.5), [], sS, drain=True)
        rs2 = sS.n
        O("vector", lambda v: v.tensor_scalar(H1T[:, :], OUTT0[:, :], STA[:, 10:11],
                                              STA[:, 15:16], ALU.subtract, ALU.mult),
          [(sS, rs2)], sV)
        x2_v = sV.n
        if DEBUG:
            dst_free_io = sdma(DST2[:, 16 * tt:16 * (tt + 1)], STA[:, :], [(sV, x2_v)])
        x2tok_wr[tt] = sdma(X2TOK[128 * tt:128 * (tt + 1), :], H1T[:, :], [(sV, x2_v)])
        h1t_free_io = x2tok_wr[tt]
        O("vector", lambda v: v.tensor_copy(X2B[:, :], H1T[:, :]), [(sV, pt_free2)], sV)
        x2b_v = sV.n
        for j in range(8):
            O("tensor", lambda t, j=j: t.transpose(PT[:, 128 * j:128 * (j + 1)],
                                                   X2B[:, 128 * j:128 * (j + 1)],
                                                   IDT[:, :]), [(sV, x2b_v)], sT)
        tx_t = sT.n
        O("vector", lambda v, tt=tt: v.tensor_copy(
            X2F[:, :].rearrange("p (j c) -> p j c", j=8)[:, :, 128 * tt:128 * (tt + 1)],
            PT[:, 0:1024].rearrange("p (j c) -> p j c", j=8)), [(sT, tx_t)], sV)
        x2f_v = sV.n
        x2f_tile[tt] = x2f_v
        for k in range(8):
            O("tensor", lambda t, tt=tt, k=k: t.matmul(
                PU[:, 0:E], X2F[:, PER * k + 128 * tt:PER * k + 128 * (tt + 1)],
                WR8B[:, E * k:E * (k + 1)], start=(k == 0), stop=(k == 7)),
              [(sV, x2f_v)], sT)
        lg_t = sT.n
        O("vector", lambda v: v.tensor_tensor(LGT[:, :], PU[:, 0:E], BRR[:, :], ALU.add),
          [(sT, lg_t)], sV, drain=True)
        O("vector", lambda v: v.reduce_max(GTS[:, 0:1], LGT[:, :], axis=AXX), [], sV,
          drain=True)
        mx_v = sV.n
        O("scalar", lambda s: s.activation(GTS[:, 1:2], GTS[:, 0:1], ACT.Identity,
                                           scale=-1.0), [(sV, mx_v)], sS, drain=True)
        O("scalar", lambda s: s.activation(EX8[:, :], LGT[:, :], ACT.Exp,
                                           bias=GTS[:, 1:2]), [], sS, drain=True)
        ex_s = sS.n
        O("vector", lambda v: v.reduce_sum(GTS[:, 3:4], EX8[:, :], axis=AXX),
          [(sS, ex_s)], sV, drain=True)
        O("vector", lambda v, tt=tt: v.tensor_scalar(WTK[:, E * tt:E * (tt + 1)],
                                                     LGT[:, :], GTS[:, 0:1], None,
                                                     ALU.is_ge), [], sV)
        O("vector", lambda v: v.reciprocal(GTS[:, 4:5], GTS[:, 3:4]), [], sV, drain=True)
        O("vector", lambda v, tt=tt: v.tensor_scalar(WTK[:, E * tt:E * (tt + 1)],
                                                     WTK[:, E * tt:E * (tt + 1)],
                                                     GTS[:, 4:5], None, ALU.mult), [], sV,
          drain=True)
        if DEBUG:
            O("vector", lambda v, tt=tt: v.tensor_copy(DBGL[:, E * tt:E * (tt + 1)],
                                                       LGT[:, :]), [], sV)
            O("vector", lambda v, tt=tt: v.tensor_copy(DBGG[:, 8 * tt:8 * (tt + 1)],
                                                       GTS[:, :]), [], sV)
        O("vector", lambda v, tt=tt: v.tensor_copy(X2B[:, 0:E],
                                                   WTK[:, E * tt:E * (tt + 1)]), [], sV,
          drain=True)
        wtk_v = sV.n
        O("tensor", lambda t: t.transpose(PT[0:E, 0:128], X2B[:, 0:E], IDT[:, :]),
          [(sV, wtk_v)], sT)
        wt_t = sT.n
        O("vector", lambda v, tt=tt: v.tensor_copy(WFE[0:E, 128 * tt:128 * (tt + 1)],
                                                   PT[0:E, 0:128]), [(sT, wt_t)], sV)
        pt_free2 = sV.n
        wfe_tile[tt] = sV.n
    epiA_v = sV.n
    epiA_t = sT.n
    if DEBUG:
        sdma(DWTK, WTK[:, :], [(sV, epiA_v)])
        sdma(DWFE, WFE[:, :], [(sV, epiA_v)])
        sdma(DLGT, DBGL[:, :], [(sV, epiA_v)])
        sdma(DGTS, DBGG[:, :], [(sV, epiA_v)])

    esB.close()
    YACC = sbp(es, "yacc", 128, 8192, F32)
    HWB = sbp(es, "hwb", 128, 8192, BF16)
    HT2 = sbp(es, "ht2", 128, 4096, BF16)
    B1FT = sbp(es, "b1ft", 128, 256, F32)
    ldb1 = sdma(B1FT[:, :], B1FD, [])

    slots = [8192, 16384, 0]
    loads = []
    for half in range(2):
        for e in range(E):
            for dc in range(4):
                loads.append(("w1", e, dc))
                loads.append(("w2", e, dc))
    slot_prev_t = {8192: corr_t, 16384: corr_t, 0: epiA_t}
    wload_done = [0] * len(loads)
    load_consumer_t = [0] * len(loads)

    def emit_wload(idx):
        kind, e, dc = loads[idx]
        col0 = slots[idx % 3]
        prev_t = load_consumer_t[idx - 3] if idx >= 3 else slot_prev_t[col0]
        if kind == "w1":
            src = W1D[e, :, 1024 * dc:1024 * (dc + 1)].rearrange(
                "(k p) c -> p k c", p=128)
        else:
            src = W2D[e, 1024 * dc:1024 * (dc + 1), :].rearrange(
                "(m p) c -> p m c", p=128)
        dst = FFW[:, col0:col0 + 8192].rearrange("p (k c) -> p k c", c=1024)
        O("gpsimd", lambda g_, dst=dst, src=src: g_.dma_start(out=dst, in_=src),
          [(sT, prev_t)], sW, 16)
        return sW.n

    wload_done[0] = emit_wload(0)
    wload_done[1] = emit_wload(1)
    nextload = 2

    yps = [PY0, PY1, PS0, PS1]
    yev = {0: 0, 1: 0, 2: 0, 3: 0}
    hev = {0: 0, 1: 0}
    hcv = {0: 0, 1: 0}
    hw_used_t = {0: 0, 1: 0}
    xr_free = [0, 0]
    out_done = 0
    ln2_prev_s = 0
    for half in range(2):
        hoff = 1024 * half
        lxs = [0] * 9
        for tt in range(8):
            gtile = half * 8 + tt
            xb = XR0 if tt % 2 == 0 else XR1
            lxs[tt] = sdma(xb[:, :], X2TOK[128 * gtile:128 * (gtile + 1), :],
                           [(sV, xr_free[tt % 2]), (sIO, x2tok_wr[gtile])])
            if tt == 7:
                lxs[8] = sdma(SGT[:, 0:256], X2TOK[128 * gtile:128 * gtile + 128, 0:256],
                              [])
            for hh in range(2):
                bi = (2 * tt + hh) % 4
                pb = yps[bi]
                O("tensor", lambda t, gtile=gtile, hh=hh, pb=pb: t.matmul(
                    pb[:, 0:512], WFE[0:E, 128 * gtile:128 * (gtile + 1)],
                    B2B[0:E, 512 * hh:512 * (hh + 1)], start=True, stop=True),
                  [(sV, wfe_tile[gtile]), (sW, wsmall_done), (sV, yev[bi])], sT)
                b2_t = sT.n
                O("vector", lambda v, tt=tt, hh=hh, xb=xb, pb=pb: v.tensor_tensor(
                    YACC[:, 1024 * tt + 512 * hh:1024 * tt + 512 * (hh + 1)],
                    pb[:, 0:512], xb[:, 512 * hh:512 * (hh + 1)], ALU.add),
                  [(sT, b2_t), (sIO, lxs[tt] + 16), (sS, ln2_prev_s)], sV)
                yev[bi] = sV.n
            xr_free[tt % 2] = sV.n
        yinit_v = sV.n

        for e in range(E):
            for dc in range(4):
                li = 2 * (32 * half + 4 * e + dc)
                w1_rdy = wload_done[li]
                w2_rdy = wload_done[li + 1]
                s1 = slots[li % 3]
                s2 = slots[(li + 1) % 3]
                for gg in range(2):
                    hbuf = 4096 * (gg % 2)
                    for m in range(8):
                        pb = PP0 if m % 2 == 0 else PP1
                        for k in range(8):
                            O("tensor", lambda t, m=m, k=k, gg=gg, s1=s1, hoff=hoff,
                              pb=pb: t.matmul(
                                  pb[:, 0:512],
                                  FFW[:, s1 + 1024 * k + 128 * m:s1 + 1024 * k + 128 * (m + 1)],
                                  X2F[:, PER * k + hoff + 512 * gg:PER * k + hoff + 512 * (gg + 1)],
                                  start=(k == 0), stop=(k == 7)),
                              [(sW, w1_rdy), (sV, x2f_tile[half * 8 + 4 * gg + 3]),
                               (sS, hev[m % 2])], sT)
                        h_t = sT.n
                        O("scalar", lambda s, m=m, e=e, dc=dc, hbuf=hbuf, pb=pb:
                          s.activation(HWB[:, hbuf + 512 * m:hbuf + 512 * (m + 1)],
                                       pb[:, 0:512], ACT.Gelu,
                                       bias=B1FT[:, 32 * e + 8 * dc + m:32 * e + 8 * dc + m + 1]),
                          [(sT, h_t), (sIO, ldb1), (sT, hw_used_t[gg % 2])], sS)
                        hev[m % 2] = sS.n
                    hg_s = smargin()
                    for t4 in range(4):
                        tt = 4 * gg + t4
                        for hh in range(2):
                            bi = (2 * tt + hh) % 4
                            pb = yps[bi]
                            for m in range(8):
                                O("tensor", lambda t, m=m, hh=hh, t4=t4, hbuf=hbuf,
                                  s2=s2, pb=pb: t.matmul(
                                      pb[:, 0:512],
                                      HWB[:, hbuf + 512 * m + 128 * t4:hbuf + 512 * m + 128 * (t4 + 1)],
                                      FFW[:, s2 + 1024 * m + 512 * hh:s2 + 1024 * m + 512 * (hh + 1)],
                                      start=(m == 0), stop=(m == 7)),
                                  [(sS, hg_s), (sW, w2_rdy), (sV, yev[bi])], sT)
                            yy_t = sT.n
                            O("vector", lambda v, tt=tt, hh=hh, e=e, half=half, pb=pb:
                              v.scalar_tensor_tensor(
                                  YACC[:, 1024 * tt + 512 * hh:1024 * tt + 512 * (hh + 1)],
                                  pb[:, 0:512],
                                  WTK[:, E * (half * 8 + tt) + e:E * (half * 8 + tt) + e + 1],
                                  YACC[:, 1024 * tt + 512 * hh:1024 * tt + 512 * (hh + 1)],
                                  ALU.mult, ALU.add), [(sT, yy_t)], sV)
                            yev[bi] = sV.n
                    hw_used_t[gg % 2] = sT.n
                load_consumer_t[li] = sT.n
                load_consumer_t[li + 1] = sT.n
                if nextload < len(loads):
                    wload_done[nextload] = emit_wload(nextload)
                    nextload += 1
                if nextload < len(loads):
                    wload_done[nextload] = emit_wload(nextload)
                    nextload += 1
        ffn_v = sV.n
        if DEBUG and half == 0:
            sdma(DYA, YACC[:, :], [(sV, ffn_v)])
        for tt in range(8):
            gtile = half * 8 + tt
            ob = OUTT0 if tt % 2 == 0 else OUTT1
            O("vector", lambda v, tt=tt: v.reduce_sum(
                STA[:, 0:1], YACC[:, 1024 * tt:1024 * (tt + 1)], axis=AXX),
              [(sV, ffn_v), (sIO, out_done)], sV)
            O("vector", lambda v, tt=tt, ob=ob: v.scalar_tensor_tensor(
                ob[:, :], YACC[:, 1024 * tt:1024 * (tt + 1)], 1.0,
                YACC[:, 1024 * tt:1024 * (tt + 1)], ALU.mult, ALU.mult), [], sV)
            O("vector", lambda v, ob=ob: v.reduce_sum(STA[:, 1:2], ob[:, :], axis=AXX),
              [], sV, drain=True)
            stfv = sV.n
            O("scalar", lambda s: s.activation(STA[:, 2:3], STA[:, 0:1], ACT.Identity,
                                               scale=1.0 / HS), [(sV, stfv)], sS)
            O("scalar", lambda s: s.activation(STA[:, 3:4], STA[:, 0:1], ACT.Square,
                                               scale=1.0 / HS), [], sS, drain=True)
            stfs = sS.n
            O("vector", lambda v: v.scalar_tensor_tensor(STA[:, 4:5], STA[:, 1:2],
                                                         1.0 / HS, STA[:, 3:4],
                                                         ALU.mult, ALU.subtract),
              [(sS, stfs)], sV, drain=True)
            va = sV.n
            O("scalar", lambda s: s.activation(STA[:, 6:7], STA[:, 4:5], ACT.Ln,
                                               bias=EPSC[:, 0:1]), [(sV, va)], sS,
              drain=True)
            O("scalar", lambda s: s.activation(STA[:, 7:8], STA[:, 6:7], ACT.Exp,
                                               scale=-0.5), [], sS, drain=True)
            rs = sS.n
            O("vector", lambda v, tt=tt, ob=ob: v.tensor_scalar(
                ob[:, :], YACC[:, 1024 * tt:1024 * (tt + 1)], STA[:, 2:3], STA[:, 7:8],
                ALU.subtract, ALU.mult), [(sS, rs)], sV)
            o_v = sV.n
            out_done = sdma(OUT[128 * gtile:128 * (gtile + 1), :], ob[:, :], [(sV, o_v)])
        ln2_prev_s = sS.n

    final_io = sIO.n
    O("sync", lambda s: s.nop(), [(sIO, final_io)], None)

    global _LAST_SEMS
    _LAST_SEMS = {s.name: s.n for s in (sW, sIO, sT, sV, sS, sCC)}
    _LAST_SEMS["_ids"] = {s.name: s.h for s in (sW, sIO, sT, sV, sS, sCC)}
    with nc.Block() as block:
        bld.emit(block)
    es.close()
    return nc


_NC_CACHE = None


def kernel(**inputs):
    global _NC_CACHE
    cores = host_prepare(inputs)
    if _NC_CACHE is None:
        _NC_CACHE = build_nc()
    nc = _NC_CACHE
    res = run_bass_kernel_spmd(nc, cores, core_ids=list(range(8)))
    out = np.empty((B, T, HS), np.float32)
    for c in range(8):
        b = c // 4
        t0 = (c % 4) * PER
        out[b, t0:t0 + PER] = np.asarray(res.results[c]["out"])
    return out


# version 04
# speedup vs baseline: 1.0690x; 1.0152x over previous
import numpy as np
import ml_dtypes
from contextlib import ExitStack

import concourse.bass as bass
import concourse.mybir as mybir
from concourse.bass_utils import run_bass_kernel_spmd

F32 = mybir.dt.float32
BF16 = mybir.dt.bfloat16
ALU = mybir.AluOpType
ACT = mybir.ActivationFunctionType
AXX = mybir.AxisListType.X
NPBF = ml_dtypes.bfloat16

B, T, HS = 2, 8192, 1024
H, R, D = 16, 16, 64
PER = 2048
C = 128
NCH = PER // C
GRP = 512
NG = PER // GRP
E = 8
DFF = 4096
EPS = 1e-5
LN_SCALE = float(np.log(0.125))


class Ctr:
    __slots__ = ("h", "n", "name")

    def __init__(self, h, name):
        self.h = h
        self.n = 0
        self.name = name


class Builder:
    ENGS = ("tensor", "vector", "scalar", "gpsimd", "sync")

    def __init__(self, nc):
        self.nc = nc
        self.prog = {e: [] for e in self.ENGS}
        self.waited = {}

    def sem(self, es, name):
        return Ctr(es.enter_context(self.nc.semaphore(name)), name)

    def op(self, eng, fn, waits=(), inc=None, inck=1, drain=False):
        wl = []
        for ctr, val in waits:
            key = (eng, ctr.name)
            if val > 0 and self.waited.get(key, 0) < val:
                wl.append((ctr.h, val))
                self.waited[key] = val
        if inc is not None:
            inc.n += inck
        semh = inc.h if inc is not None else None

        if eng in ("vector", "scalar") and semh is not None:
            drain = True

        def closure(e, fn=fn, wl=wl, semh=semh, inck=inck, drain=drain):
            for h, v in wl:
                e.wait_ge(h, v)
            inst = fn(e)
            if drain:
                inst = e.drain()
            if semh is not None:
                if inst is None:
                    inst = e.nop()
                inst.then_inc(semh, inck)
        self.prog[eng].append(closure)

    def emit(self, block):
        for eng in self.ENGS:
            ops = self.prog[eng]
            if not ops:
                continue
            getattr(block, eng)(lambda e, ops=ops: [f(e) for f in ops])


def host_prepare(inputs):
    f = np.float32
    qw = np.asarray(inputs["L0_attn_q_w"], f)
    kw = np.asarray(inputs["L0_attn_k_w"], f)
    vw = np.asarray(inputs["L0_attn_v_w"], f)
    ow = np.asarray(inputs["L0_attn_out_w"], f)
    gw = np.asarray(inputs["L0_attn_g_w"], f)
    Wq = np.asarray(inputs["L0_attn_Wq"], f)
    Wk = np.asarray(inputs["L0_attn_Wk"], f)
    gb = np.asarray(inputs["L0_attn_g_b"], f)
    bdl = np.asarray(inputs["L0_attn_base_decay_logits"], f)
    ob = np.asarray(inputs["L0_attn_out_b"], f)
    lng = np.asarray(inputs["L0_attn_ln_g"], f)
    lnb = np.asarray(inputs["L0_attn_ln_b"], f)
    W1 = np.asarray(inputs["L0_ffn_W1"], f)
    b1 = np.asarray(inputs["L0_ffn_b1"], f)
    W2 = np.asarray(inputs["L0_ffn_W2"], f)
    b2 = np.asarray(inputs["L0_ffn_b2"], f)
    Wr = np.asarray(inputs["L0_ffn_Wr"], f)
    br = np.asarray(inputs["L0_ffn_br"], f)
    x = np.asarray(inputs["L0_attn_x"], f)
    cc = np.ascontiguousarray

    shared = {
        "qwT": cc(qw.T), "kwT": cc(kw.T), "vwT": cc(vw.T), "owT": cc(ow.T),
        "w1": cc(W1), "w2": cc(W2), "b2": cc(b2),
    }
    gwTp = np.zeros((HS, 512), f)
    for h in range(H):
        gwTp[:, 32 * h:32 * h + R] = gw[16 * h:16 * h + R, :].T
    shared["gwTp"] = gwTp
    wqbd = np.zeros((8, 128, 64), f)
    wkbd = np.zeros((8, 128, 64), f)
    for j in range(8):
        for u in range(2):
            h = 2 * j + u
            wqbd[j, 64 * u:64 * u + D, 32 * u:32 * u + R] = Wq[h].T
            wkbd[j, 64 * u:64 * u + D, 32 * u:32 * u + R] = Wk[h].T
    shared["wqbd"] = wqbd
    shared["wkbd"] = wkbd
    wr8 = np.zeros((128, 64), f)
    for k in range(8):
        wr8[:, 8 * k:8 * (k + 1)] = Wr[128 * k:128 * (k + 1), :]
    shared["wr8"] = wr8
    b1f = np.zeros((128, 256), f)
    for e in range(E):
        for dc in range(4):
            for m in range(8):
                b1f[:, e * 32 + dc * 8 + m] = b1[e, dc * 1024 + m * 128:dc * 1024 + (m + 1) * 128]
    shared["b1feat"] = b1f
    shared["brRep"] = np.broadcast_to(br, (128, E)).copy()
    shared["obRep"] = np.broadcast_to(ob, (128, HS)).copy()
    shared["lngRep"] = np.broadcast_to(lng, (128, HS)).copy()
    shared["lnbRep"] = np.broadcast_to(lnb, (128, HS)).copy()
    gbpadv = np.zeros((512,), f)
    for h in range(H):
        gbpadv[32 * h:32 * h + R] = gb[16 * h:16 * h + R]
    shared["gbPadT"] = np.broadcast_to(gbpadv, (128, 512)).copy()
    lgp = np.zeros((128, 4), f)
    gbp = np.zeros((128, 4), f)
    for i in range(4):
        for u in range(4):
            h = 4 * i + u
            lgp[32 * u:32 * u + R, i] = bdl[h]
            gbp[32 * u:32 * u + R, i] = gb[16 * h:16 * h + R]
    shared["lgp"] = lgp
    shared["gbPadF"] = gbp

    halfD = D // 2
    invf = (1.0 / 10000.0 ** (np.arange(halfD, dtype=f) / halfD)).astype(f)
    posg = np.arange(T, dtype=f)
    frq = posg[:, None] * invf[None, :]
    cosT = np.cos(frq).astype(f)
    sinT = np.sin(frq).astype(f)
    cext = np.zeros((T, D), f)
    sext = np.zeros((T, D), f)
    cext[:, 0::2] = cosT
    cext[:, 1::2] = cosT
    sext[:, 0::2] = -sinT
    sext[:, 1::2] = sinT
    swapm = np.zeros((128, 128), f)
    for i in range(0, 128, 2):
        swapm[i, i + 1] = 1.0
        swapm[i + 1, i] = 1.0
    ident = np.eye(128, dtype=f)
    lt = np.arange(C, dtype=f)
    maskst = (lt[None, :] >= lt[:, None]).astype(f)
    idx128 = np.broadcast_to(lt, (128, C)).copy()

    cores = []
    for c in range(8):
        b = c // 4
        t0 = (c % 4) * PER
        ce = np.zeros((128, PER), f)
        se = np.zeros((128, PER), f)
        ce[0:64] = cext[t0:t0 + PER].T
        ce[64:128] = ce[0:64]
        se[0:64] = sext[t0:t0 + PER].T
        se[64:128] = se[0:64]
        offs = np.full((4,), 1e9, f)
        p = c % 4
        for m in range(p):
            offs[m] = float(PER * (p - 1 - m))
        core = dict(shared)
        core.update({
            "x": cc(x[b, t0:t0 + PER]),
            "cext": ce.astype(NPBF), "sext": se.astype(NPBF),
            "swapm": swapm.astype(NPBF), "ident": ident.astype(NPBF),
            "maskst": maskst.astype(NPBF),
            "maskt4": np.tile(maskst, (1, 4)).astype(NPBF),
            "idx128": idx128,
            "offs": np.broadcast_to(offs, (128, 4)).copy(),
        })
        cores.append(core)
    return cores


DEBUG = False
_LAST_SEMS = None


def build_nc():
    nc = bass.Bass("TRN2", target_bir_lowering=False)
    es = ExitStack()
    esA = ExitStack()
    esB = ExitStack()

    def dram(name, shape, dt=F32, kind="ExternalInput"):
        return nc.dram_tensor(name, shape, dt, kind=kind).ap()

    X = dram("x", [PER, HS])
    QWT = dram("qwT", [HS, HS]); KWT = dram("kwT", [HS, HS])
    VWT = dram("vwT", [HS, HS]); OWT = dram("owT", [HS, HS])
    GWTP = dram("gwTp", [HS, 512])
    WQBD = dram("wqbd", [8, 128, 64]); WKBD = dram("wkbd", [8, 128, 64])
    W1D = dram("w1", [E, HS, DFF]); W2D = dram("w2", [E, DFF, HS])
    WR8 = dram("wr8", [128, 64]); B2D = dram("b2", [E, HS])
    B1FD = dram("b1feat", [128, 256])
    BRREP = dram("brRep", [128, E])
    OBREP = dram("obRep", [128, HS]); LNGREP = dram("lngRep", [128, HS])
    LNBREP = dram("lnbRep", [128, HS]); GBPADT = dram("gbPadT", [128, 512])
    LGP = dram("lgp", [128, 4]); GBPADF = dram("gbPadF", [128, 4])
    CEXT = dram("cext", [128, PER], BF16); SEXT = dram("sext", [128, PER], BF16)
    SWAPM = dram("swapm", [128, 128], BF16); IDENT = dram("ident", [128, 128], BF16)
    MASKST = dram("maskst", [128, 128], BF16)
    MASKT4 = dram("maskt4", [128, 512], BF16)
    IDX128 = dram("idx128", [128, C])
    OFFS = dram("offs", [128, 4])
    OUT = dram("out", [PER, HS], kind="ExternalOutput")
    if DEBUG:
        DQT = dram("dqt", [128, 4 * PER], BF16, kind="ExternalOutput")
        DYF = dram("dyf", [128, 8 * PER], BF16, kind="ExternalOutput")
        DMST = dram("dmst", [128, 256], F32, kind="ExternalOutput")
        DTAB = dram("dtab", [128, 4 * C * 3 + 512 + 4 * 3], F32, kind="ExternalOutput")
        DAT = dram("dat", [128, 4 * GRP], BF16, kind="ExternalOutput")
        DAK = dram("dak", [128, 4 * 512], BF16, kind="ExternalOutput")
        DVW = dram("dvw", [128, 4096], BF16, kind="ExternalOutput")
        DXF = dram("dxf", [128, 8 * GRP], BF16, kind="ExternalOutput")
        DH1 = dram("dh1", [PER, HS], F32, kind="ExternalOutput")
        DU = dram("du", [PER, HS], F32, kind="ExternalOutput")
        DST2 = dram("dst2", [128, 16 * NCH], F32, kind="ExternalOutput")
        DWTK = dram("dwtk", [128, NCH * E], F32, kind="ExternalOutput")
        DWFE = dram("dwfe", 8, [PER] if False else [8, PER], BF16, kind="ExternalOutput") if False else dram("dwfe", [8, PER], BF16, kind="ExternalOutput")
        DYA = dram("dya", [128, 8192], F32, kind="ExternalOutput")
        DLGT = dram("dlgt", [128, NCH * E], F32, kind="ExternalOutput")
        DGTS = dram("dgts", [128, NCH * 8], F32, kind="ExternalOutput")
        DMSTE = dram("dmste", [128, 1024], BF16, kind="ExternalOutput")
        DSGT = dram("dsgt", [128, 1024], F32, kind="ExternalOutput")
        DOBR = dram("dobr", [128, HS], BF16, kind="ExternalOutput")
        DSTA = dram("dsta", [128, 16 * NCH], F32, kind="ExternalOutput")
    X2TOK = nc.dram_tensor("x2tok", [PER, HS], F32,
                           kind="ExternalOutput" if DEBUG else "Internal").ap()
    CCIN = nc.dram_tensor("ccin", [128, 256], F32)
    CCOUT = nc.dram_tensor("ccout", [512, 256], F32)

    def sbp(stack, name, p, n, dt, side=None):
        kw = {} if side is None else {"side": side}
        return stack.enter_context(nc.sbuf_tensor(name, [p, n], dt, **kw))

    QT = sbp(es, "qt", 128, 4 * PER, BF16)
    XT0 = sbp(es, "xt0", 128, HS, F32)
    XT1 = sbp(es, "xt1", 128, HS, F32)
    H1T = sbp(es, "h1t", 128, HS, F32)
    OUTT0 = sbp(es, "outt0", 128, HS, F32)
    OUTT1 = sbp(es, "outt1", 128, HS, F32)
    X2B = sbp(es, "x2b", 128, HS, BF16)
    IDT = sbp(es, "idt", 128, 128, BF16)
    OBR = sbp(es, "obr", 128, HS, BF16)
    LNGR = sbp(es, "lngr", 128, HS, BF16)
    LNBR = sbp(es, "lnbr", 128, HS, BF16)
    BRR = sbp(es, "brr", 128, E, F32)
    B2B = sbp(es, "b2b", 8, HS, BF16)
    WR8B = sbp(es, "wr8b", 128, 64, BF16)
    WFE = sbp(es, "wfe", 8, PER, BF16)
    WTK = sbp(es, "wtk", 128, NCH * E, F32)
    STA = sbp(es, "sta", 128, 16, F32)
    LGT = sbp(es, "lgt", 128, E, F32)
    GTS = sbp(es, "gts", 128, 8, F32)
    LAMP = sbp(es, "lamp", 128, 4, F32)
    DECP = sbp(es, "decp", 128, 4, F32)
    DECC = sbp(es, "decc", 128, 4, F32)
    DPOW = sbp(es, "dpow", 128, 4 * NCH, F32)
    WCMB = sbp(es, "wcmb", 128, 16, F32)
    MST = sbp(es, "mst", 128, 256, F32)
    MEXP = sbp(es, "mexp", 128, 16 * 64, BF16)
    MSTE = sbp(es, "mste", 128, 16 * 64, BF16)
    SGT = sbp(es, "sgt", 128, 4 * 256, F32)
    XR0 = sbp(es, "xr0", 128, HS, F32)
    XR1 = sbp(es, "xr1", 128, HS, F32)
    LNSC = sbp(es, "lnsc", 128, 1, F32)
    EX8 = sbp(es, "ex8", 128, 8, F32)
    EPSC = sbp(es, "epsc", 128, 1, F32)
    DBGL = sbp(es, "dbgl", 128, NCH * E, F32)
    DBGG = sbp(es, "dbgg", 128, NCH * 8, F32)
    SPCV = sbp(es, "spcv", 128, 1, F32)

    YF = sbp(esB, "yf", 128, 8 * PER, BF16, side="right")

    WSL = [sbp(esA, "wsl0", 128, 8192, BF16, side="right"),
           sbp(esA, "wsl1", 128, 8192, BF16, side="right")]
    GWTB = sbp(esA, "gwtb", 128, 4096, BF16, side="right")
    BDQ = sbp(esA, "bdq", 128, 512, BF16, side="right")
    BDK = sbp(esA, "bdk", 128, 512, BF16, side="right")
    XFW = sbp(esA, "xfw", 128, 8 * GRP, BF16, side="right")
    VW = sbp(esA, "vw", 128, 4096, BF16, side="right")
    ATW = sbp(esA, "atw", 128, 4 * GRP, BF16, side="right")
    AKW = sbp(esA, "akw", 128, 4 * 512, BF16, side="right")
    KRW = sbp(esA, "krw", 128, 8 * GRP, BF16, side="right")
    GFW = sbp(esA, "gfw", 128, 4 * GRP, BF16, side="right")
    GTW = sbp(esA, "gtw", 128, 4 * 512, BF16, side="right")
    RPW = sbp(esA, "rpw", 128, 2 * GRP, BF16, side="right")
    SPW = sbp(esA, "spw", 128, 16 * C, BF16, side="right")
    QHW = sbp(esA, "qhw", 128, 4 * C, BF16, side="right")
    CEXW = sbp(esA, "cexw", 128, 2 * GRP, BF16, side="right")
    SEXW = sbp(esA, "sexw", 128, 2 * GRP, BF16, side="right")
    SWP = sbp(esA, "swp", 128, 128, BF16, side="right")
    MSK = sbp(esA, "msk", 128, 128, BF16, side="right")
    MSKT4 = sbp(esA, "mskt4", 128, 512, BF16, side="right")
    IDXT = sbp(esA, "idxt", 128, C, F32, side="right")
    GBT = sbp(esA, "gbt", 128, 512, F32, side="right")
    GBPF = sbp(esA, "gbpf", 128, 4, F32, side="right")
    LGPT = sbp(esA, "lgpt", 128, 4, F32, side="right")
    OFFT = sbp(esA, "offt", 128, 4, F32, side="right")
    PPOS = sbp(esA, "ppos", 128, 4 * C, F32, side="right")
    PNEG = sbp(esA, "pneg", 128, 4 * C, F32, side="right")
    PREV = sbp(esA, "prev", 128, 4 * C, F32, side="right")
    PRVT = sbp(esA, "prvt", 128, 512, F32, side="right")

    PP0 = es.enter_context(nc.psum_tensor("pp0", [128, 512], F32))
    PP1 = es.enter_context(nc.psum_tensor("pp1", [128, 512], F32))
    PS0 = es.enter_context(nc.psum_tensor("ps0", [128, 512], F32))
    PS1 = es.enter_context(nc.psum_tensor("ps1", [128, 512], F32))
    PY0 = es.enter_context(nc.psum_tensor("py0", [128, 512], F32))
    PY1 = es.enter_context(nc.psum_tensor("py1", [128, 512], F32))
    PU = es.enter_context(nc.psum_tensor("pu", [128, 512], F32))
    PT = es.enter_context(nc.psum_tensor("pt", [128, 1024], BF16))

    bld = Builder(nc)
    sW = bld.sem(es, "sW")
    sIO = bld.sem(es, "sIO")
    sT = bld.sem(es, "sT")
    sV = bld.sem(es, "sV")
    sS = bld.sem(es, "sS")
    sCC = bld.sem(es, "sCC")
    sG = bld.sem(es, "sG")
    O = bld.op

    def acc_act(s, *a, **kw):
        s.activation(*a, **kw)
        return s.nop()

    def smargin():
        O("scalar", lambda s: s.activation(SPCV[:, :], LNSC[:, :], ACT.Identity), [], sS)
        return sS.n

    def gdma(dst, src, waits=()):
        O("gpsimd", lambda g, d=dst, s=src: g.dma_start(out=d, in_=s), waits, sW, 16)
        return sW.n

    def sdma(dst, src, waits=()):
        O("sync", lambda s_, d=dst, s=src: s_.dma_start(out=d, in_=s), waits, sIO, 16)
        return sIO.n

    sdma(SWP[:, :], SWAPM); sdma(IDT[:, :], IDENT); sdma(MSK[:, :], MASKST)
    sdma(MSKT4[:, :], MASKT4)
    sdma(IDXT[:, :], IDX128)
    sdma(GBT[:, :], GBPADT); sdma(BRR[:, :], BRREP)
    sdma(OFFT[:, :], OFFS); sdma(LGPT[:, :], LGP); sdma(GBPF[:, :], GBPADF)
    gdma(OBR[:, :], OBREP)
    gdma(LNGR[:, :], LNGREP)
    gdma(LNBR[:, :], LNBREP)
    biascast_v = 0
    biasld_w = sW.n

    WSRC = [QWT, KWT, VWT]

    def load_wslot(seq, waits):
        src = WSRC[seq % 3]
        sl = WSL[seq % 2]
        for k in range(8):
            gdma(sl[:, 1024 * k:1024 * (k + 1)], src[128 * k:128 * (k + 1), :], waits)
        return sW.n

    wrdy = {}
    wconsumed_t = {}
    wrdy[0] = load_wslot(0, [])
    wrdy[1] = load_wslot(1, [])
    for k in range(8):
        gdma(GWTB[:, 512 * k:512 * (k + 1)], GWTP[128 * k:128 * (k + 1), :])
        gdma(BDQ[:, 64 * k:64 * (k + 1)], WQBD[k, :, :])
        gdma(BDK[:, 64 * k:64 * (k + 1)], WKBD[k, :, :])
    gdma(WR8B[:, :], WR8)
    gdma(B2B[:, :], B2D)
    wsmall_done = sW.n

    O("vector", lambda v: v.memset(LNSC[:, :], LN_SCALE), [], sV)
    O("vector", lambda v: v.memset(EPSC[:, :], EPS), [], sV)
    O("vector", lambda v: v.memset(SPCV[:, :], 0.0), [], sV)
    lnsc_v = sV.n
    epsc_v = sV.n
    O("vector", lambda v: v.memset(MEXP[:, :], 0.0), [], sV)
    O("vector", lambda v: v.memset(MST[:, :], 0.0), [], sV)
    O("vector", lambda v: v.memset(MSTE[:, :], 0.0), [], sV)

    lam_in = sIO.n
    O("scalar", lambda s: s.activation(STA[:, 0:4], LGPT[:, :], ACT.Exp, scale=-1.0),
      [(sIO, lam_in)], sS, drain=True)
    O("vector", lambda v: v.tensor_scalar(STA[:, 4:8], STA[:, 0:4], 1.0, None, ALU.add),
      [(sS, sS.n)], sV, drain=True)
    O("scalar", lambda s: s.activation(STA[:, 8:12], STA[:, 4:8], ACT.Ln), [(sV, sV.n)], sS,
      drain=True)
    O("vector", lambda v: v.tensor_scalar(LAMP[:, :], STA[:, 8:12], -1.0, None, ALU.mult),
      [(sS, sS.n)], sV, drain=True)
    O("vector", lambda v: v.memset(SPCV[:, :], 0.0), [], sV)
    O("vector", lambda v: v.memset(SPCV[:, :], 0.0), [], sV)
    lam_v = sV.n
    O("scalar", lambda s: s.activation(DECP[:, :], LAMP[:, :], ACT.Exp), [(sV, lam_v)], sS)
    O("scalar", lambda s: s.activation(DECC[:, :], LAMP[:, :], ACT.Exp, scale=float(C)),
      [], sS)
    for i in range(4):
        O("vector", lambda v, i=i: v.tensor_scalar(PPOS[:, C * i:C * (i + 1)], IDXT[:, :],
                                                   LAMP[:, i:i + 1], None, ALU.mult),
          [(sV, lam_v)], sV, drain=True)
        sv = sV.n
        O("scalar", lambda s, i=i: s.activation(PPOS[:, C * i:C * (i + 1)],
                                                PPOS[:, C * i:C * (i + 1)], ACT.Exp),
          [(sV, sv)], sS)
        O("vector", lambda v, i=i: v.tensor_scalar(PNEG[:, C * i:C * (i + 1)], IDXT[:, :],
                                                   LAMP[:, i:i + 1], None, ALU.mult), [], sV,
          drain=True)
        sv = sV.n
        O("scalar", lambda s, i=i: s.activation(PNEG[:, C * i:C * (i + 1)],
                                                PNEG[:, C * i:C * (i + 1)], ACT.Exp,
                                                scale=-1.0, bias=LNSC[:, 0:1]),
          [(sV, sv), (sV, lnsc_v)], sS)
        O("vector", lambda v, i=i: v.tensor_scalar(STA[:, 12:13], LAMP[:, i:i + 1],
                                                   127.0, LN_SCALE, ALU.mult, ALU.add),
          [(sS, sS.n)], sV, drain=True)
        O("vector", lambda v, i=i: v.tensor_scalar(PREV[:, C * i:C * (i + 1)], IDXT[:, :],
                                                   LAMP[:, i:i + 1], None, ALU.mult), [], sV,
          drain=True)
        sv = sV.n
        O("scalar", lambda s, i=i: s.activation(PREV[:, C * i:C * (i + 1)],
                                                PREV[:, C * i:C * (i + 1)], ACT.Exp,
                                                scale=-1.0, bias=STA[:, 12:13]),
          [(sV, sv)], sS)
        for cc_ in range(NCH):
            O("vector", lambda v, i=i, cc_=cc_: v.tensor_scalar(
                DPOW[:, NCH * i + cc_:NCH * i + cc_ + 1], LAMP[:, i:i + 1],
                float(C * cc_ + 1), None, ALU.mult), [], sV,
              drain=(cc_ == NCH - 1))
        sv = sV.n
        O("scalar", lambda s, i=i: s.activation(DPOW[:, NCH * i:NCH * (i + 1)],
                                                DPOW[:, NCH * i:NCH * (i + 1)], ACT.Exp),
          [(sV, sv)], sS)
        O("vector", lambda v, i=i: v.tensor_scalar(WCMB[:, 4 * i:4 * (i + 1)], OFFT[:, :],
                                                   LAMP[:, i:i + 1], None, ALU.mult), [], sV,
          drain=True)
        sv = sV.n
        O("scalar", lambda s, i=i: s.activation(WCMB[:, 4 * i:4 * (i + 1)],
                                                WCMB[:, 4 * i:4 * (i + 1)], ACT.Exp),
          [(sV, sv)], sS)
    tab_s = sS.n

    for i in range(4):
        O("vector", lambda v, i=i: v.tensor_copy(QHW[:, C * i:C * (i + 1)],
                                                 PREV[:, C * i:C * (i + 1)]),
          [(sS, tab_s)], sV)
    prvc = sV.n
    for i in range(4):
        O("tensor", lambda t, i=i: t.transpose(PT[:, 128 * i:128 * (i + 1)],
                                               QHW[:, C * i:C * (i + 1)], IDT[:, :]),
          [(sV, prvc), (sIO, 16 * 3)], sT)
    trp = sT.n
    O("vector", lambda v: v.tensor_copy(PRVT[:, :].rearrange("p (i c) -> p i c", i=4),
                                        PT[:, 0:512].rearrange("p (i c) -> p i c", i=4)),
      [(sT, trp)], sV)
    prvt_v = sV.n

    win_free_v = 0
    pt_free_v = prvt_v
    xe_free_v = [biascast_v, biascast_v]
    spw_evac = {0: 0, 1: 0}
    py_evac = {0: 0, 1: 0}
    cex_ld = [0, 0]
    qhw_free_t = trp
    mexp_g = 0

    for g in range(NG):
        g0 = g * GRP
        par = g % 2
        cex_ld[par] = sdma(CEXW[:, GRP * par:GRP * (par + 1)], CEXT[:, g0:g0 + GRP],
                           [(sV, win_free_v)])
        sdma(SEXW[:, GRP * par:GRP * (par + 1)], SEXT[:, g0:g0 + GRP], [])
        rope_ld = sIO.n
        CEXs = CEXW[:, GRP * par:GRP * (par + 1)]
        SEXs = SEXW[:, GRP * par:GRP * (par + 1)]

        for tt in range(4):
            xb = XT0 if tt % 2 == 0 else XT1
            lv = sdma(xb[:, :], X[g0 + 128 * tt:g0 + 128 * (tt + 1), :],
                      [(sV, xe_free_v[tt % 2])])
            O("vector", lambda v, xb=xb: v.tensor_copy(X2B[:, :], xb[:, :]),
              [(sIO, lv), (sV, pt_free_v)], sV)
            cv = sV.n
            xe_free_v[tt % 2] = cv
            for j in range(8):
                O("tensor", lambda t, j=j: t.transpose(PT[:, 128 * j:128 * (j + 1)],
                                                       X2B[:, 128 * j:128 * (j + 1)],
                                                       IDT[:, :]), [(sV, cv)], sT)
            tv = sT.n
            O("vector", lambda v, tt=tt: v.tensor_copy(
                XFW[:, :].rearrange("p (j c) -> p j c", j=8)[:, :, 128 * tt:128 * (tt + 1)],
                PT[:, 0:1024].rearrange("p (j c) -> p j c", j=8)),
              [(sT, tv), (sV, win_free_v)], sV)
            pt_free_v = sV.n
        xfw_v = sV.n

        rp = [PP0, PP1, PS0, PS1]
        for i in range(4):
            for k in range(8):
                O("tensor", lambda t, i=i, k=k: t.matmul(
                    rp[i][:, 0:GRP], GWTB[:, 512 * k + 128 * i:512 * k + 128 * (i + 1)],
                    XFW[:, GRP * k:GRP * (k + 1)], start=(k == 0), stop=(k == 7)),
                  [(sV, xfw_v), (sW, wsmall_done)], sT)
        gf_t = sT.n
        for i in range(4):
            O("scalar", lambda s, i=i: s.activation(GFW[:, GRP * i:GRP * (i + 1)],
                                                    rp[i][:, 0:GRP], ACT.Sigmoid,
                                                    bias=GBPF[:, i:i + 1]),
              [(sT, gf_t), (sIO, lam_in)], sS)
        gf_s = sS.n
        for tt in range(4):
            pb = rp[tt]
            for k in range(8):
                O("tensor", lambda t, tt=tt, k=k, pb=pb: t.matmul(
                    pb[:, 0:512],
                    XFW[:, GRP * k + 128 * tt:GRP * k + 128 * (tt + 1)],
                    GWTB[:, 512 * k:512 * (k + 1)],
                    start=(k == 0), stop=(k == 7)), [(sS, gf_s)], sT)
            gt_t = sT.n
            O("vector", lambda v, tt=tt, pb=pb: v.tensor_tensor(
                GTW[:, 512 * tt:512 * (tt + 1)], pb[:, 0:512], GBT[:, :], ALU.add),
              [(sT, gt_t)], sV)
            sv = sV.n
            O("scalar", lambda s, tt=tt: s.activation(GTW[:, 512 * tt:512 * (tt + 1)],
                                                      GTW[:, 512 * tt:512 * (tt + 1)],
                                                      ACT.Sigmoid), [(sV, sv)], sS)
        gt_s = sS.n

        qsl = WSL[(3 * g) % 2]
        q_rdy = wrdy[3 * g]
        for j in range(8):
            pj = PY0 if j % 2 == 0 else PY1
            for k in range(8):
                O("tensor", lambda t, j=j, k=k, pj=pj, qsl=qsl: t.matmul(
                    pj[:, 0:GRP], qsl[:, 1024 * k + 128 * j:1024 * k + 128 * (j + 1)],
                    XFW[:, GRP * k:GRP * (k + 1)], start=(k == 0), stop=(k == 7)),
                  [(sS, gt_s), (sW, q_rdy), (sV, py_evac[j % 2])], sT)
            pj_t = sT.n
            O("vector", lambda v, pj=pj: v.tensor_copy(RPW[:, 0:GRP], pj[:, 0:GRP]),
              [(sT, pj_t)], sV)
            raw_v = sV.n
            py_evac[j % 2] = raw_v
            O("tensor", lambda t: t.matmul(PU[:, 0:GRP], SWP[:, :], RPW[:, 0:GRP],
                                           start=True, stop=True), [(sV, raw_v)], sT)
            sw_t = sT.n
            O("vector", lambda v, CEXs=CEXs: v.tensor_tensor(
                RPW[:, GRP:2 * GRP], RPW[:, 0:GRP], CEXs, ALU.mult),
              [(sIO, rope_ld)], sV)
            O("vector", lambda v, SEXs=SEXs: v.tensor_tensor(
                RPW[:, 0:GRP], PU[:, 0:GRP], SEXs, ALU.mult), [(sT, sw_t)], sV)
            O("vector", lambda v: v.tensor_tensor(RPW[:, GRP:2 * GRP], RPW[:, GRP:2 * GRP],
                                                  RPW[:, 0:GRP], ALU.add), [], sV)
            rope_v = sV.n
            pbank = rp[j // 2]
            O("tensor", lambda t, j=j, pbank=pbank: t.matmul(
                pbank[64 * (j % 2):64 * (j % 2) + 64, 0:GRP],
                BDQ[:, 64 * j:64 * (j + 1)], RPW[:, GRP:2 * GRP],
                start=True, stop=True), [(sV, rope_v)], sT)
        qr_t = sT.n
        wconsumed_t[3 * g] = qr_t
        if 3 * g + 2 < 3 * NG:
            wrdy[3 * g + 2] = load_wslot(3 * g + 2, [(sT, wconsumed_t.get(3 * g, 0))])
        for i in range(4):
            for wc in range(4):
                O("vector", lambda v, i=i, wc=wc, g0=g0: v.tensor_tensor(
                    QT[:, PER * i + g0 + C * wc:PER * i + g0 + C * (wc + 1)],
                    rp[i][:, C * wc:C * (wc + 1)], PPOS[:, C * i:C * (i + 1)], ALU.mult),
                  [(sT, qr_t)], sV)
        qt_v = sV.n

        ksl = WSL[(3 * g + 1) % 2]
        k_rdy = wrdy[3 * g + 1]
        for j in range(8):
            pj = PY0 if j % 2 == 0 else PY1
            for k in range(8):
                O("tensor", lambda t, j=j, k=k, pj=pj, ksl=ksl: t.matmul(
                    pj[:, 0:GRP], ksl[:, 1024 * k + 128 * j:1024 * k + 128 * (j + 1)],
                    XFW[:, GRP * k:GRP * (k + 1)], start=(k == 0), stop=(k == 7)),
                  [(sV, qt_v), (sW, k_rdy), (sV, py_evac[j % 2])], sT)
            pj_t = sT.n
            O("vector", lambda v, pj=pj: v.tensor_copy(RPW[:, 0:GRP], pj[:, 0:GRP]),
              [(sT, pj_t)], sV)
            raw_v = sV.n
            py_evac[j % 2] = raw_v
            O("tensor", lambda t: t.matmul(PU[:, 0:GRP], SWP[:, :], RPW[:, 0:GRP],
                                           start=True, stop=True), [(sV, raw_v)], sT)
            sw_t = sT.n
            O("vector", lambda v, j=j, CEXs=CEXs: v.tensor_tensor(
                KRW[:, GRP * j:GRP * (j + 1)], RPW[:, 0:GRP], CEXs, ALU.mult), [], sV)
            O("vector", lambda v, SEXs=SEXs: v.tensor_tensor(
                RPW[:, 0:GRP], PU[:, 0:GRP], SEXs, ALU.mult), [(sT, sw_t)], sV)
            O("vector", lambda v, j=j: v.tensor_tensor(KRW[:, GRP * j:GRP * (j + 1)],
                                                       KRW[:, GRP * j:GRP * (j + 1)],
                                                       RPW[:, 0:GRP], ALU.add), [], sV)
        kro_v = sV.n
        wconsumed_t[3 * g + 1] = sT.n
        if 3 * g + 3 < 3 * NG:
            wrdy[3 * g + 3] = load_wslot(3 * g + 3, [(sT, wconsumed_t.get(3 * g + 1, 0))])
        for j in range(8):
            pbank = rp[j // 2]
            O("tensor", lambda t, j=j, pbank=pbank: t.matmul(
                pbank[64 * (j % 2):64 * (j % 2) + 64, 0:GRP],
                BDK[:, 64 * j:64 * (j + 1)], KRW[:, GRP * j:GRP * (j + 1)],
                start=True, stop=True), [(sV, kro_v)], sT)
        krf_t = sT.n
        for i in range(4):
            for wc in range(4):
                O("vector", lambda v, i=i, wc=wc: v.tensor_tensor(
                    SPW[:, 0:C], rp[i][:, C * wc:C * (wc + 1)],
                    GFW[:, GRP * i + C * wc:GRP * i + C * (wc + 1)], ALU.mult),
                  [(sT, krf_t)], sV)
                O("vector", lambda v, i=i, wc=wc: v.tensor_tensor(
                    ATW[:, GRP * i + C * wc:GRP * i + C * (wc + 1)],
                    SPW[:, 0:C], PNEG[:, C * i:C * (i + 1)], ALU.mult), [], sV)
        atw_v = sV.n
        for tt in range(4):
            pb = PY0 if tt % 2 == 0 else PY1
            for j in range(8):
                O("tensor", lambda t, tt=tt, j=j, pb=pb: t.matmul(
                    pb[:, 64 * j:64 * (j + 1)],
                    KRW[:, GRP * j + 128 * tt:GRP * j + 128 * (tt + 1)],
                    BDK[:, 64 * j:64 * (j + 1)], start=True, stop=True),
                  [(sV, atw_v), (sV, py_evac[tt % 2])], sT)
            kt_t = sT.n
            O("vector", lambda v, tt=tt, pb=pb: v.tensor_tensor(
                SPW[:, 0:512], pb[:, 0:512], GTW[:, 512 * tt:512 * (tt + 1)], ALU.mult),
              [(sT, kt_t), (sV, prvt_v)], sV)
            O("vector", lambda v, tt=tt: v.tensor_tensor(
                AKW[:, 512 * tt:512 * (tt + 1)], SPW[:, 0:512], PRVT[:, :], ALU.mult),
              [], sV)
            py_evac[tt % 2] = sV.n
        akw_v = sV.n

        vsl = WSL[(3 * g + 2) % 2]
        v_rdy = wrdy[3 * g + 2]
        for tt in range(4):
            for hh in range(2):
                pb = PY0 if hh == 0 else PY1
                for k in range(8):
                    O("tensor", lambda t, tt=tt, hh=hh, k=k, pb=pb, vsl=vsl: t.matmul(
                        pb[:, 0:512],
                        XFW[:, GRP * k + 128 * tt:GRP * k + 128 * (tt + 1)],
                        vsl[:, 1024 * k + 512 * hh:1024 * k + 512 * (hh + 1)],
                        start=(k == 0), stop=(k == 7)),
                      [(sV, akw_v), (sW, v_rdy), (sV, py_evac[hh])], sT)
                vt_t = sT.n
                O("vector", lambda v, tt=tt, hh=hh, pb=pb: v.tensor_copy(
                    VW[:, 1024 * tt + 512 * hh:1024 * tt + 512 * (hh + 1)], pb[:, 0:512]),
                  [(sT, vt_t)], sV)
                py_evac[hh] = sV.n
        vw_v = sV.n
        if DEBUG and g == 0:
            sdma(DAT, ATW[:, :], [(sV, atw_v)])
            sdma(DAK, AKW[:, :], [(sV, akw_v)])
            sdma(DVW, VW[:, :], [(sV, vw_v)])
            sdma(DXF, XFW[:, :], [(sV, xfw_v)])
        wconsumed_t[3 * g + 2] = sT.n

        for wc in range(4):
            for i in range(4):
                O("gpsimd", lambda g_, i=i, wc=wc, g0=g0: g_.tensor_scalar(
                    QHW[:, C * i:C * (i + 1)],
                    QT[:, PER * i + g0 + C * wc:PER * i + g0 + C * (wc + 1)],
                    DECP[:, i:i + 1], 1.0, ALU.mult, ALU.mult),
                  [(sV, vw_v), (sT, qhw_free_t)], sG)
            qh_g = sG.n
            for u in range(4):
                pb = [PS0, PS1, PY0, PY1][u]
                bfree = spw_evac[u] if u < 2 else py_evac[u - 2]
                for kk in range(4):
                    h = u + 4 * kk
                    i = h // 4
                    O("tensor", lambda t, i=i, u=u, kk=kk, wc=wc, g0=g0, pb=pb: t.matmul(
                        pb[:, C * kk:C * (kk + 1)],
                        ATW[32 * u:32 * u + 32, GRP * i + C * wc:GRP * i + C * (wc + 1)],
                        QT[32 * u:32 * u + 32,
                           PER * i + g0 + C * wc:PER * i + g0 + C * (wc + 1)],
                        start=True, stop=True, tile_position=(32 * u, 0)),
                      [(sG, qh_g), (sV, vw_v), (sV, bfree)], sT)
                sc_t = sT.n
                O("vector", lambda v, u=u, pb=pb: v.tensor_tensor(
                    SPW[:, 4 * C * u:4 * C * (u + 1)], pb[:, 0:4 * C], MSKT4[:, :],
                    ALU.mult), [(sT, sc_t)], sV)
            sc_v = sV.n
            for j in range(8):
                pb = PY0 if j % 2 == 0 else PY1
                for hh in range(2):
                    h = 2 * j + hh
                    slot = 4 * (h % 4) + h // 4
                    O("tensor", lambda t, h=h, hh=hh, wc=wc, pb=pb, slot=slot: t.matmul(
                        pb[64 * hh:64 * hh + 64, 0:C],
                        VW[:, 1024 * wc + 64 * h:1024 * wc + 64 * (h + 1)],
                        SPW[:, C * slot:C * (slot + 1)], start=True, stop=False),
                      [(sV, sc_v), (sV, py_evac[j % 2]), (sG, mexp_g)], sT)
                    O("tensor", lambda t, h=h, hh=hh, pb=pb: t.matmul(
                        pb[64 * hh:64 * hh + 64, 0:C], MEXP[:, 64 * h:64 * (h + 1)],
                        QHW[:, C * (h // 4):C * (h // 4 + 1)], start=False, stop=True),
                      [], sT)
                y_t = sT.n
                O("vector", lambda v, j=j, g0=g0, wc=wc, pb=pb: v.tensor_copy(
                    YF[:, PER * j + g0 + C * wc:PER * j + g0 + C * (wc + 1)],
                    pb[:, 0:C]), [(sT, y_t)], sV)
                py_evac[j % 2] = sV.n
            y_v = sV.n
            qhw_free_t = sT.n
            for qg in range(4):
                pb = PS0 if qg < 2 else PS1
                O("tensor", lambda t, qg=qg, wc=wc, pb=pb: t.matmul(
                    pb[:, 256 * (qg % 2):256 * (qg % 2) + 256],
                    AKW[:, 512 * wc + 128 * qg:512 * wc + 128 * (qg + 1)],
                    VW[:, 1024 * wc + 256 * qg:1024 * wc + 256 * (qg + 1)],
                    start=True, stop=True), [(sV, y_v)], sT)
            u_t = sT.n
            for h in range(H):
                qg, u = h // 4, h % 4
                pb = PS0 if qg < 2 else PS1
                O("vector", lambda v, qg=qg, u=u, pb=pb: v.scalar_tensor_tensor(
                    MST[32 * u:32 * u + 32, 64 * qg:64 * (qg + 1)],
                    MST[32 * u:32 * u + 32, 64 * qg:64 * (qg + 1)],
                    DECC[32 * u:32 * u + 32, qg:qg + 1],
                    pb[32 * u:32 * u + 32,
                       256 * (qg % 2) + 64 * u:256 * (qg % 2) + 64 * (u + 1)],
                    ALU.mult, ALU.add), [(sT, u_t), (sG, mexp_g)], sV)
            mst_v = sV.n
            for h in range(H):
                qg, u = h // 4, h % 4
                O("gpsimd", lambda g_, h=h, qg=qg, u=u: g_.tensor_copy(
                    MEXP[32 * u:32 * u + 32, 64 * h:64 * (h + 1)],
                    MST[32 * u:32 * u + 32, 64 * qg:64 * (qg + 1)]),
                  [(sV, mst_v), (sT, qhw_free_t)], sG)
            mexp_g = sG.n
            spw_evac[0] = mst_v
            spw_evac[1] = mst_v
            win_free_v = mst_v
        if 3 * g + 4 < 3 * NG:
            wrdy[3 * g + 4] = load_wslot(3 * g + 4, [(sT, wconsumed_t[3 * g + 2])])
    scan_v = sV.n
    scan_t = sT.n
    if DEBUG:
        sdma(DQT, QT[:, :], [(sV, scan_v)])
        sdma(DMST, MST[:, :], [(sV, scan_v)])
        sdma(DTAB[:, 0:4 * C], PPOS[:, :], [(sS, tab_s)])
        sdma(DTAB[:, 4 * C:8 * C], PNEG[:, :], [])
        sdma(DTAB[:, 8 * C:12 * C], PREV[:, :], [])
        sdma(DTAB[:, 12 * C:12 * C + 512], PRVT[:, :], [(sV, prvt_v)])
        sdma(DTAB[:, 12 * C + 512:12 * C + 516], LAMP[:, :], [(sV, scan_v)])
        sdma(DTAB[:, 12 * C + 516:12 * C + 520], DECP[:, :], [])
        sdma(DTAB[:, 12 * C + 520:12 * C + 524], DECC[:, :], [])

    st_out = sdma(CCIN.ap(), MST[:, :], [(sV, scan_v)])
    O("gpsimd", lambda g_: g_.collective_compute(
        "AllGather", ALU.bypass, replica_groups=[[0, 1, 2, 3], [4, 5, 6, 7]],
        ins=[CCIN.ap().opt()], outs=[CCOUT.ap().opt()]),
      [(sIO, st_out)], sCC)
    cc_n = sCC.n
    for m in range(4):
        sdma(SGT[:, 256 * m:256 * (m + 1)], CCOUT.ap()[128 * m:128 * (m + 1), :],
             [(sCC, cc_n)])
    sg_io = sIO.n
    O("vector", lambda v: v.memset(MST[:, :], 0.0), [(sIO, sg_io)], sV)
    for m in range(4):
        for i in range(4):
            O("vector", lambda v, i=i, m=m: v.scalar_tensor_tensor(
                MST[:, 64 * i:64 * (i + 1)],
                SGT[:, 256 * m + 64 * i:256 * m + 64 * (i + 1)],
                WCMB[:, 4 * i + m:4 * i + m + 1],
                MST[:, 64 * i:64 * (i + 1)], ALU.mult, ALU.add), [], sV, drain=True)
    O("vector", lambda v: v.memset(SPCV[:, :], 0.0), [], sV, drain=True)
    for h in range(H):
        qg, u = h // 4, h % 4
        O("vector", lambda v, h=h, qg=qg, u=u: v.tensor_copy(
            MSTE[32 * u:32 * u + 32, 64 * h:64 * (h + 1)],
            MST[32 * u:32 * u + 32, 64 * qg:64 * (qg + 1)]), [], sV)
    mste_v = sV.n
    if DEBUG:
        sdma(DMSTE, MSTE[:, :], [(sV, mste_v)])
        sdma(DSGT, SGT[:, :], [(sV, mste_v)])

    QGW = KRW
    corr_evac = {0: 0, 1: 0}
    for i in range(4):
        for cc_ in range(NCH):
            O("vector", lambda v, i=i, cc_=cc_: v.tensor_scalar(
                QGW[:, C * cc_:C * (cc_ + 1)],
                QT[:, PER * i + C * cc_:PER * i + C * (cc_ + 1)],
                DPOW[:, NCH * i + cc_:NCH * i + cc_ + 1], None, ALU.mult),
              [(sV, mste_v)], sV)
        qg_v = sV.n
        for u in range(4):
            h = 4 * i + u
            for qq in range(4):
                n = u * 4 + qq
                pb = PY0 if n % 2 == 0 else PY1
                O("tensor", lambda t, h=h, qq=qq, pb=pb: t.matmul(
                    pb[0:64, 0:512], MSTE[:, 64 * h:64 * (h + 1)],
                    QGW[:, 512 * qq:512 * (qq + 1)], start=True, stop=True),
                  [(sV, qg_v), (sT, scan_t), (sV, corr_evac[n % 2])], sT)
                co_t = sT.n
                O("vector", lambda v, h=h, qq=qq, pb=pb: v.tensor_tensor(
                    YF[64 * (h % 2):64 * (h % 2) + 64,
                       PER * (h // 2) + 512 * qq:PER * (h // 2) + 512 * (qq + 1)],
                    YF[64 * (h % 2):64 * (h % 2) + 64,
                       PER * (h // 2) + 512 * qq:PER * (h // 2) + 512 * (qq + 1)],
                    pb[0:64, 0:512], ALU.add), [(sT, co_t)], sV)
                corr_evac[n % 2] = sV.n
    corr_v = sV.n
    corr_t = sT.n
    if DEBUG:
        sdma(DYF, YF[:, :], [(sV, corr_v)])

    esA.close()
    FFW = sbp(es, "ffw", 128, 24576, BF16)
    X2F = sbp(es, "x2f", 128, 8 * PER, BF16)

    for k in range(8):
        gdma(FFW[:, 1024 * k:1024 * (k + 1)], OWT[128 * k:128 * (k + 1), :],
             [(sT, corr_t)])
    owt_w = sW.n

    if DEBUG:
        sdma(DOBR, OBR[:, :], [(sV, corr_v)])
    x2tok_wr = [0] * NCH
    x2f_tile = [0] * NCH
    wfe_tile = [0] * NCH
    h1t_free_io = 0
    du_free_io = 0
    dst_free_io = 0
    op_evac = {0: corr_v, 1: corr_v}
    pt_free2 = pt_free_v
    for tt in range(NCH):
        xb = XT0 if tt % 2 == 0 else XT1
        lx = sdma(xb[:, :], X[128 * tt:128 * (tt + 1), :], [(sV, xe_free_v[tt % 2])])
        for half in range(2):
            pb = PY0 if half == 0 else PY1
            for k in range(8):
                O("tensor", lambda t, tt=tt, half=half, k=k, pb=pb: t.matmul(
                    pb[:, 0:512], YF[:, PER * k + 128 * tt:PER * k + 128 * (tt + 1)],
                    FFW[:, 1024 * k + 512 * half:1024 * k + 512 * (half + 1)],
                    start=(k == 0), stop=(k == 7)),
                  [(sV, corr_v), (sW, owt_w), (sV, op_evac[half])], sT)
        op_t = sT.n
        for half, pb in ((0, PY0), (1, PY1)):
            O("vector", lambda v, half=half, pb=pb: v.tensor_tensor(
                H1T[:, 512 * half:512 * (half + 1)], pb[:, 0:512],
                OBR[:, 512 * half:512 * (half + 1)], ALU.add),
              [(sT, op_t), (sIO, lx), (sIO, h1t_free_io), (sW, biasld_w)], sV)
            O("vector", lambda v, half=half, xb=xb: v.tensor_tensor(
                H1T[:, 512 * half:512 * (half + 1)], H1T[:, 512 * half:512 * (half + 1)],
                xb[:, 512 * half:512 * (half + 1)], ALU.add), [], sV)
            op_evac[half] = sV.n
        h1_v = sV.n
        if DEBUG:
            dh1w = sdma(DH1[128 * tt:128 * (tt + 1), :], H1T[:, :], [(sV, h1_v)])
            h1t_free_io = dh1w
        O("vector", lambda v: v.reduce_sum(STA[:, 0:1], H1T[:, :], axis=AXX),
          [(sV, h1_v), (sIO, du_free_io), (sIO, dst_free_io)], sV)
        O("vector", lambda v: v.scalar_tensor_tensor(OUTT1[:, :], H1T[:, :], 1.0,
                                                     H1T[:, :], ALU.mult, ALU.mult), [], sV)
        O("vector", lambda v: v.reduce_sum(STA[:, 1:2], OUTT1[:, :], axis=AXX), [], sV,
          drain=True)
        st1v = sV.n
        O("scalar", lambda s: s.activation(STA[:, 2:3], STA[:, 0:1], ACT.Identity,
                                           scale=1.0 / HS), [(sV, st1v)], sS)
        O("scalar", lambda s: s.activation(STA[:, 3:4], STA[:, 0:1], ACT.Square,
                                           scale=1.0 / HS), [], sS, drain=True)
        st1s = sS.n
        O("vector", lambda v: v.scalar_tensor_tensor(STA[:, 4:5], STA[:, 1:2], 1.0 / HS,
                                                     STA[:, 3:4], ALU.mult, ALU.subtract),
          [(sS, st1s)], sV, drain=True)
        va1 = sV.n
        O("scalar", lambda s: s.activation(STA[:, 6:7], STA[:, 4:5], ACT.Ln,
                                           bias=EPSC[:, 0:1]), [(sV, va1)], sS, drain=True)
        O("scalar", lambda s: s.activation(STA[:, 7:8], STA[:, 6:7], ACT.Exp, scale=-0.5),
          [], sS, drain=True)
        rs1 = sS.n
        O("vector", lambda v: v.tensor_scalar(OUTT0[:, :], H1T[:, :], STA[:, 2:3],
                                              STA[:, 7:8], ALU.subtract, ALU.mult),
          [(sS, rs1)], sV)
        O("vector", lambda v: v.tensor_tensor(OUTT0[:, :], OUTT0[:, :], LNGR[:, :],
                                              ALU.mult), [], sV)
        O("vector", lambda v: v.tensor_tensor(OUTT0[:, :], OUTT0[:, :], LNBR[:, :],
                                              ALU.add), [], sV)
        O("vector", lambda v, xb=xb: v.tensor_tensor(OUTT0[:, :], OUTT0[:, :], xb[:, :],
                                                     ALU.add), [], sV)
        u_v = sV.n
        if DEBUG:
            du_free_io = sdma(DU[128 * tt:128 * (tt + 1), :], OUTT0[:, :], [(sV, u_v)])
        xe_free_v[tt % 2] = u_v
        O("vector", lambda v: v.reduce_sum(STA[:, 8:9], OUTT0[:, :], axis=AXX),
          [(sV, u_v)], sV)
        O("vector", lambda v: v.scalar_tensor_tensor(OUTT1[:, :], OUTT0[:, :], 1.0,
                                                     OUTT0[:, :], ALU.mult, ALU.mult),
          [], sV)
        O("vector", lambda v: v.reduce_sum(STA[:, 9:10], OUTT1[:, :], axis=AXX), [], sV,
          drain=True)
        st2v = sV.n
        O("scalar", lambda s: s.activation(STA[:, 10:11], STA[:, 8:9], ACT.Identity,
                                           scale=1.0 / HS), [(sV, st2v)], sS)
        O("scalar", lambda s: s.activation(STA[:, 11:12], STA[:, 8:9], ACT.Square,
                                           scale=1.0 / HS), [], sS, drain=True)
        st2s = sS.n
        O("vector", lambda v: v.scalar_tensor_tensor(STA[:, 12:13], STA[:, 9:10], 1.0 / HS,
                                                     STA[:, 11:12], ALU.mult,
                                                     ALU.subtract), [(sS, st2s)], sV,
          drain=True)
        va2 = sV.n
        O("scalar", lambda s: s.activation(STA[:, 14:15], STA[:, 12:13], ACT.Ln,
                                           bias=EPSC[:, 0:1]), [(sV, va2)], sS, drain=True)
        O("scalar", lambda s: s.activation(STA[:, 15:16], STA[:, 14:15], ACT.Exp,
                                           scale=-0.5), [], sS, drain=True)
        rs2 = sS.n
        O("vector", lambda v: v.tensor_scalar(H1T[:, :], OUTT0[:, :], STA[:, 10:11],
                                              STA[:, 15:16], ALU.subtract, ALU.mult),
          [(sS, rs2)], sV)
        x2_v = sV.n
        if DEBUG:
            dst_free_io = sdma(DST2[:, 16 * tt:16 * (tt + 1)], STA[:, :], [(sV, x2_v)])
        x2tok_wr[tt] = sdma(X2TOK[128 * tt:128 * (tt + 1), :], H1T[:, :], [(sV, x2_v)])
        h1t_free_io = x2tok_wr[tt]
        O("vector", lambda v: v.tensor_copy(X2B[:, :], H1T[:, :]), [(sV, pt_free2)], sV)
        x2b_v = sV.n
        for j in range(8):
            O("tensor", lambda t, j=j: t.transpose(PT[:, 128 * j:128 * (j + 1)],
                                                   X2B[:, 128 * j:128 * (j + 1)],
                                                   IDT[:, :]), [(sV, x2b_v)], sT)
        tx_t = sT.n
        O("vector", lambda v, tt=tt: v.tensor_copy(
            X2F[:, :].rearrange("p (j c) -> p j c", j=8)[:, :, 128 * tt:128 * (tt + 1)],
            PT[:, 0:1024].rearrange("p (j c) -> p j c", j=8)), [(sT, tx_t)], sV)
        x2f_v = sV.n
        x2f_tile[tt] = x2f_v
        for k in range(8):
            O("tensor", lambda t, tt=tt, k=k: t.matmul(
                PU[:, 0:E], X2F[:, PER * k + 128 * tt:PER * k + 128 * (tt + 1)],
                WR8B[:, E * k:E * (k + 1)], start=(k == 0), stop=(k == 7)),
              [(sV, x2f_v)], sT)
        lg_t = sT.n
        O("vector", lambda v: v.tensor_tensor(LGT[:, :], PU[:, 0:E], BRR[:, :], ALU.add),
          [(sT, lg_t)], sV, drain=True)
        O("vector", lambda v: v.reduce_max(GTS[:, 0:1], LGT[:, :], axis=AXX), [], sV,
          drain=True)
        mx_v = sV.n
        O("scalar", lambda s: s.activation(GTS[:, 1:2], GTS[:, 0:1], ACT.Identity,
                                           scale=-1.0), [(sV, mx_v)], sS, drain=True)
        O("scalar", lambda s: s.activation(EX8[:, :], LGT[:, :], ACT.Exp,
                                           bias=GTS[:, 1:2]), [], sS, drain=True)
        ex_s = sS.n
        O("vector", lambda v: v.reduce_sum(GTS[:, 3:4], EX8[:, :], axis=AXX),
          [(sS, ex_s)], sV, drain=True)
        O("vector", lambda v, tt=tt: v.tensor_scalar(WTK[:, E * tt:E * (tt + 1)],
                                                     LGT[:, :], GTS[:, 0:1], None,
                                                     ALU.is_ge), [], sV)
        O("vector", lambda v: v.reciprocal(GTS[:, 4:5], GTS[:, 3:4]), [], sV, drain=True)
        O("vector", lambda v, tt=tt: v.tensor_scalar(WTK[:, E * tt:E * (tt + 1)],
                                                     WTK[:, E * tt:E * (tt + 1)],
                                                     GTS[:, 4:5], None, ALU.mult), [], sV,
          drain=True)
        if DEBUG:
            O("vector", lambda v, tt=tt: v.tensor_copy(DBGL[:, E * tt:E * (tt + 1)],
                                                       LGT[:, :]), [], sV)
            O("vector", lambda v, tt=tt: v.tensor_copy(DBGG[:, 8 * tt:8 * (tt + 1)],
                                                       GTS[:, :]), [], sV)
        O("vector", lambda v, tt=tt: v.tensor_copy(X2B[:, 0:E],
                                                   WTK[:, E * tt:E * (tt + 1)]), [], sV,
          drain=True)
        wtk_v = sV.n
        O("tensor", lambda t: t.transpose(PT[0:E, 0:128], X2B[:, 0:E], IDT[:, :]),
          [(sV, wtk_v)], sT)
        wt_t = sT.n
        O("vector", lambda v, tt=tt: v.tensor_copy(WFE[0:E, 128 * tt:128 * (tt + 1)],
                                                   PT[0:E, 0:128]), [(sT, wt_t)], sV)
        pt_free2 = sV.n
        wfe_tile[tt] = sV.n
    epiA_v = sV.n
    epiA_t = sT.n
    if DEBUG:
        sdma(DWTK, WTK[:, :], [(sV, epiA_v)])
        sdma(DWFE, WFE[:, :], [(sV, epiA_v)])
        sdma(DLGT, DBGL[:, :], [(sV, epiA_v)])
        sdma(DGTS, DBGG[:, :], [(sV, epiA_v)])

    esB.close()
    YACC = sbp(es, "yacc", 128, 8192, F32)
    HWB = sbp(es, "hwb", 128, 8192, BF16)
    HT2 = sbp(es, "ht2", 128, 4096, BF16)
    B1FT = sbp(es, "b1ft", 128, 256, F32)
    ldb1 = sdma(B1FT[:, :], B1FD, [])

    slots = [8192, 16384, 0]
    loads = []
    for half in range(2):
        for e in range(E):
            for dc in range(4):
                loads.append(("w1", e, dc))
                loads.append(("w2", e, dc))
    slot_prev_t = {8192: corr_t, 16384: corr_t, 0: epiA_t}
    wload_done = [0] * len(loads)
    load_consumer_t = [0] * len(loads)

    def emit_wload(idx):
        kind, e, dc = loads[idx]
        col0 = slots[idx % 3]
        prev_t = load_consumer_t[idx - 3] if idx >= 3 else slot_prev_t[col0]
        if kind == "w1":
            src = W1D[e, :, 1024 * dc:1024 * (dc + 1)].rearrange(
                "(k p) c -> p k c", p=128)
        else:
            src = W2D[e, 1024 * dc:1024 * (dc + 1), :].rearrange(
                "(m p) c -> p m c", p=128)
        dst = FFW[:, col0:col0 + 8192].rearrange("p (k c) -> p k c", c=1024)
        O("gpsimd", lambda g_, dst=dst, src=src: g_.dma_start(out=dst, in_=src),
          [(sT, prev_t)], sW, 16)
        return sW.n

    wload_done[0] = emit_wload(0)
    wload_done[1] = emit_wload(1)
    nextload = 2

    yps = [PY0, PY1, PS0, PS1]
    yev = {0: 0, 1: 0, 2: 0, 3: 0}
    hev = {0: 0, 1: 0}
    hcv = {0: 0, 1: 0}
    hw_used_t = {0: 0, 1: 0}
    xr_free = [0, 0]
    out_done = 0
    ln2_prev_s = 0
    for half in range(2):
        hoff = 1024 * half
        lxs = [0] * 9
        for tt in range(8):
            gtile = half * 8 + tt
            xb = XR0 if tt % 2 == 0 else XR1
            lxs[tt] = sdma(xb[:, :], X2TOK[128 * gtile:128 * (gtile + 1), :],
                           [(sV, xr_free[tt % 2]), (sIO, x2tok_wr[gtile])])
            if tt == 7:
                lxs[8] = sdma(SGT[:, 0:256], X2TOK[128 * gtile:128 * gtile + 128, 0:256],
                              [])
            for hh in range(2):
                bi = (2 * tt + hh) % 4
                pb = yps[bi]
                O("tensor", lambda t, gtile=gtile, hh=hh, pb=pb: t.matmul(
                    pb[:, 0:512], WFE[0:E, 128 * gtile:128 * (gtile + 1)],
                    B2B[0:E, 512 * hh:512 * (hh + 1)], start=True, stop=True),
                  [(sV, wfe_tile[gtile]), (sW, wsmall_done), (sV, yev[bi])], sT)
                b2_t = sT.n
                O("vector", lambda v, tt=tt, hh=hh, xb=xb, pb=pb: v.tensor_tensor(
                    YACC[:, 1024 * tt + 512 * hh:1024 * tt + 512 * (hh + 1)],
                    pb[:, 0:512], xb[:, 512 * hh:512 * (hh + 1)], ALU.add),
                  [(sT, b2_t), (sIO, lxs[tt] + 16), (sS, ln2_prev_s)], sV)
                yev[bi] = sV.n
            xr_free[tt % 2] = sV.n
        yinit_v = sV.n

        for e in range(E):
            for dc in range(4):
                li = 2 * (32 * half + 4 * e + dc)
                w1_rdy = wload_done[li]
                w2_rdy = wload_done[li + 1]
                s1 = slots[li % 3]
                s2 = slots[(li + 1) % 3]
                for gg in range(2):
                    hbuf = 4096 * (gg % 2)
                    for m in range(8):
                        pb = PP0 if m % 2 == 0 else PP1
                        for k in range(8):
                            O("tensor", lambda t, m=m, k=k, gg=gg, s1=s1, hoff=hoff,
                              pb=pb: t.matmul(
                                  pb[:, 0:512],
                                  FFW[:, s1 + 1024 * k + 128 * m:s1 + 1024 * k + 128 * (m + 1)],
                                  X2F[:, PER * k + hoff + 512 * gg:PER * k + hoff + 512 * (gg + 1)],
                                  start=(k == 0), stop=(k == 7)),
                              [(sW, w1_rdy), (sV, x2f_tile[half * 8 + 4 * gg + 3]),
                               (sS, hev[m % 2])], sT)
                        h_t = sT.n
                        O("scalar", lambda s, m=m, e=e, dc=dc, hbuf=hbuf, pb=pb:
                          s.activation(HWB[:, hbuf + 512 * m:hbuf + 512 * (m + 1)],
                                       pb[:, 0:512], ACT.Gelu,
                                       bias=B1FT[:, 32 * e + 8 * dc + m:32 * e + 8 * dc + m + 1]),
                          [(sT, h_t), (sIO, ldb1), (sT, hw_used_t[gg % 2])], sS)
                        hev[m % 2] = sS.n
                    hg_s = smargin()
                    for t4 in range(4):
                        tt = 4 * gg + t4
                        for hh in range(2):
                            bi = (2 * tt + hh) % 4
                            pb = yps[bi]
                            for m in range(8):
                                O("tensor", lambda t, m=m, hh=hh, t4=t4, hbuf=hbuf,
                                  s2=s2, pb=pb: t.matmul(
                                      pb[:, 0:512],
                                      HWB[:, hbuf + 512 * m + 128 * t4:hbuf + 512 * m + 128 * (t4 + 1)],
                                      FFW[:, s2 + 1024 * m + 512 * hh:s2 + 1024 * m + 512 * (hh + 1)],
                                      start=(m == 0), stop=(m == 7)),
                                  [(sS, hg_s), (sW, w2_rdy), (sV, yev[bi])], sT)
                            yy_t = sT.n
                            O("vector", lambda v, tt=tt, hh=hh, e=e, half=half, pb=pb:
                              v.scalar_tensor_tensor(
                                  YACC[:, 1024 * tt + 512 * hh:1024 * tt + 512 * (hh + 1)],
                                  pb[:, 0:512],
                                  WTK[:, E * (half * 8 + tt) + e:E * (half * 8 + tt) + e + 1],
                                  YACC[:, 1024 * tt + 512 * hh:1024 * tt + 512 * (hh + 1)],
                                  ALU.mult, ALU.add), [(sT, yy_t)], sV)
                            yev[bi] = sV.n
                    hw_used_t[gg % 2] = sT.n
                load_consumer_t[li] = sT.n
                load_consumer_t[li + 1] = sT.n
                if nextload < len(loads):
                    wload_done[nextload] = emit_wload(nextload)
                    nextload += 1
                if nextload < len(loads):
                    wload_done[nextload] = emit_wload(nextload)
                    nextload += 1
        ffn_v = sV.n
        if DEBUG and half == 0:
            sdma(DYA, YACC[:, :], [(sV, ffn_v)])
        for tt in range(8):
            gtile = half * 8 + tt
            ob = OUTT0 if tt % 2 == 0 else OUTT1
            O("vector", lambda v, tt=tt: v.reduce_sum(
                STA[:, 0:1], YACC[:, 1024 * tt:1024 * (tt + 1)], axis=AXX),
              [(sV, ffn_v), (sIO, out_done)], sV)
            O("vector", lambda v, tt=tt, ob=ob: v.scalar_tensor_tensor(
                ob[:, :], YACC[:, 1024 * tt:1024 * (tt + 1)], 1.0,
                YACC[:, 1024 * tt:1024 * (tt + 1)], ALU.mult, ALU.mult), [], sV)
            O("vector", lambda v, ob=ob: v.reduce_sum(STA[:, 1:2], ob[:, :], axis=AXX),
              [], sV, drain=True)
            stfv = sV.n
            O("scalar", lambda s: s.activation(STA[:, 2:3], STA[:, 0:1], ACT.Identity,
                                               scale=1.0 / HS), [(sV, stfv)], sS)
            O("scalar", lambda s: s.activation(STA[:, 3:4], STA[:, 0:1], ACT.Square,
                                               scale=1.0 / HS), [], sS, drain=True)
            stfs = sS.n
            O("vector", lambda v: v.scalar_tensor_tensor(STA[:, 4:5], STA[:, 1:2],
                                                         1.0 / HS, STA[:, 3:4],
                                                         ALU.mult, ALU.subtract),
              [(sS, stfs)], sV, drain=True)
            va = sV.n
            O("scalar", lambda s: s.activation(STA[:, 6:7], STA[:, 4:5], ACT.Ln,
                                               bias=EPSC[:, 0:1]), [(sV, va)], sS,
              drain=True)
            O("scalar", lambda s: s.activation(STA[:, 7:8], STA[:, 6:7], ACT.Exp,
                                               scale=-0.5), [], sS, drain=True)
            rs = sS.n
            O("vector", lambda v, tt=tt, ob=ob: v.tensor_scalar(
                ob[:, :], YACC[:, 1024 * tt:1024 * (tt + 1)], STA[:, 2:3], STA[:, 7:8],
                ALU.subtract, ALU.mult), [(sS, rs)], sV)
            o_v = sV.n
            out_done = sdma(OUT[128 * gtile:128 * (gtile + 1), :], ob[:, :], [(sV, o_v)])
        ln2_prev_s = sS.n

    final_io = sIO.n
    O("sync", lambda s: s.nop(), [(sIO, final_io)], None)

    global _LAST_SEMS
    _LAST_SEMS = {s.name: s.n for s in (sW, sIO, sT, sV, sS, sCC)}
    _LAST_SEMS["_ids"] = {s.name: s.h for s in (sW, sIO, sT, sV, sS, sCC)}
    with nc.Block() as block:
        bld.emit(block)
    es.close()
    return nc


_NC_CACHE = None


def kernel(**inputs):
    global _NC_CACHE
    cores = host_prepare(inputs)
    if _NC_CACHE is None:
        _NC_CACHE = build_nc()
    nc = _NC_CACHE
    res = run_bass_kernel_spmd(nc, cores, core_ids=list(range(8)))
    out = np.empty((B, T, HS), np.float32)
    for c in range(8):
        b = c // 4
        t0 = (c % 4) * PER
        out[b, t0:t0 + PER] = np.asarray(res.results[c]["out"])
    return out


# version 06
# speedup vs baseline: 1.0693x; 1.0003x over previous
import numpy as np
import ml_dtypes
from contextlib import ExitStack

import concourse.bass as bass
import concourse.mybir as mybir
from concourse.bass_utils import run_bass_kernel_spmd

F32 = mybir.dt.float32
BF16 = mybir.dt.bfloat16
ALU = mybir.AluOpType
ACT = mybir.ActivationFunctionType
AXX = mybir.AxisListType.X
NPBF = ml_dtypes.bfloat16

B, T, HS = 2, 8192, 1024
H, R, D = 16, 16, 64
PER = 2048
C = 128
NCH = PER // C
GRP = 512
NG = PER // GRP
E = 8
DFF = 4096
EPS = 1e-5
LN_SCALE = float(np.log(0.125))


class Ctr:
    __slots__ = ("h", "n", "name")

    def __init__(self, h, name):
        self.h = h
        self.n = 0
        self.name = name


class Builder:
    ENGS = ("tensor", "vector", "scalar", "gpsimd", "sync")

    def __init__(self, nc):
        self.nc = nc
        self.prog = {e: [] for e in self.ENGS}
        self.waited = {}

    def sem(self, es, name):
        return Ctr(es.enter_context(self.nc.semaphore(name)), name)

    def op(self, eng, fn, waits=(), inc=None, inck=1, drain=False):
        wl = []
        for ctr, val in waits:
            key = (eng, ctr.name)
            if val > 0 and self.waited.get(key, 0) < val:
                wl.append((ctr.h, val))
                self.waited[key] = val
        if inc is not None:
            inc.n += inck
        semh = inc.h if inc is not None else None

        if eng in ("vector", "scalar") and semh is not None:
            drain = True

        def closure(e, fn=fn, wl=wl, semh=semh, inck=inck, drain=drain):
            for h, v in wl:
                e.wait_ge(h, v)
            inst = fn(e)
            if drain:
                inst = e.drain()
            if semh is not None:
                if inst is None:
                    inst = e.nop()
                inst.then_inc(semh, inck)
        self.prog[eng].append(closure)

    def emit(self, block):
        for eng in self.ENGS:
            ops = self.prog[eng]
            if not ops:
                continue
            getattr(block, eng)(lambda e, ops=ops: [f(e) for f in ops])


def host_prepare(inputs):
    f = np.float32
    qw = np.asarray(inputs["L0_attn_q_w"], f)
    kw = np.asarray(inputs["L0_attn_k_w"], f)
    vw = np.asarray(inputs["L0_attn_v_w"], f)
    ow = np.asarray(inputs["L0_attn_out_w"], f)
    gw = np.asarray(inputs["L0_attn_g_w"], f)
    Wq = np.asarray(inputs["L0_attn_Wq"], f)
    Wk = np.asarray(inputs["L0_attn_Wk"], f)
    gb = np.asarray(inputs["L0_attn_g_b"], f)
    bdl = np.asarray(inputs["L0_attn_base_decay_logits"], f)
    ob = np.asarray(inputs["L0_attn_out_b"], f)
    lng = np.asarray(inputs["L0_attn_ln_g"], f)
    lnb = np.asarray(inputs["L0_attn_ln_b"], f)
    W1 = np.asarray(inputs["L0_ffn_W1"], f)
    b1 = np.asarray(inputs["L0_ffn_b1"], f)
    W2 = np.asarray(inputs["L0_ffn_W2"], f)
    b2 = np.asarray(inputs["L0_ffn_b2"], f)
    Wr = np.asarray(inputs["L0_ffn_Wr"], f)
    br = np.asarray(inputs["L0_ffn_br"], f)
    x = np.asarray(inputs["L0_attn_x"], f)
    cc = np.ascontiguousarray

    shared = {
        "qwT": cc(qw.T), "kwT": cc(kw.T), "vwT": cc(vw.T), "owT": cc(ow.T),
        "w1": cc(W1), "w2": cc(W2), "b2": cc(b2),
    }
    gwTp = np.zeros((HS, 512), f)
    for h in range(H):
        gwTp[:, 32 * h:32 * h + R] = gw[16 * h:16 * h + R, :].T
    shared["gwTp"] = gwTp
    wqbd = np.zeros((8, 128, 64), f)
    wkbd = np.zeros((8, 128, 64), f)
    for j in range(8):
        for u in range(2):
            h = 2 * j + u
            wqbd[j, 64 * u:64 * u + D, 32 * u:32 * u + R] = Wq[h].T
            wkbd[j, 64 * u:64 * u + D, 32 * u:32 * u + R] = Wk[h].T
    shared["wqbd"] = wqbd
    shared["wkbd"] = wkbd
    wr8 = np.zeros((128, 64), f)
    for k in range(8):
        wr8[:, 8 * k:8 * (k + 1)] = Wr[128 * k:128 * (k + 1), :]
    shared["wr8"] = wr8
    b1f = np.zeros((128, 256), f)
    for e in range(E):
        for dc in range(4):
            for m in range(8):
                b1f[:, e * 32 + dc * 8 + m] = b1[e, dc * 1024 + m * 128:dc * 1024 + (m + 1) * 128]
    shared["b1feat"] = b1f
    shared["brRep"] = np.broadcast_to(br, (128, E)).copy()
    shared["obRep"] = np.broadcast_to(ob, (128, HS)).copy()
    shared["lngRep"] = np.broadcast_to(lng, (128, HS)).copy()
    shared["lnbRep"] = np.broadcast_to(lnb, (128, HS)).copy()
    gbpadv = np.zeros((512,), f)
    for h in range(H):
        gbpadv[32 * h:32 * h + R] = gb[16 * h:16 * h + R]
    shared["gbPadT"] = np.broadcast_to(gbpadv, (128, 512)).copy()
    lgp = np.zeros((128, 4), f)
    gbp = np.zeros((128, 4), f)
    for i in range(4):
        for u in range(4):
            h = 4 * i + u
            lgp[32 * u:32 * u + R, i] = bdl[h]
            gbp[32 * u:32 * u + R, i] = gb[16 * h:16 * h + R]
    shared["lgp"] = lgp
    shared["gbPadF"] = gbp

    halfD = D // 2
    invf = (1.0 / 10000.0 ** (np.arange(halfD, dtype=f) / halfD)).astype(f)
    posg = np.arange(T, dtype=f)
    frq = posg[:, None] * invf[None, :]
    cosT = np.cos(frq).astype(f)
    sinT = np.sin(frq).astype(f)
    cext = np.zeros((T, D), f)
    sext = np.zeros((T, D), f)
    cext[:, 0::2] = cosT
    cext[:, 1::2] = cosT
    sext[:, 0::2] = -sinT
    sext[:, 1::2] = sinT
    swapm = np.zeros((128, 128), f)
    for i in range(0, 128, 2):
        swapm[i, i + 1] = 1.0
        swapm[i + 1, i] = 1.0
    ident = np.eye(128, dtype=f)
    lt = np.arange(C, dtype=f)
    maskst = (lt[None, :] >= lt[:, None]).astype(f)
    idx128 = np.broadcast_to(lt, (128, C)).copy()

    cores = []
    for c in range(8):
        b = c // 4
        t0 = (c % 4) * PER
        ce = np.zeros((128, PER), f)
        se = np.zeros((128, PER), f)
        ce[0:64] = cext[t0:t0 + PER].T
        ce[64:128] = ce[0:64]
        se[0:64] = sext[t0:t0 + PER].T
        se[64:128] = se[0:64]
        offs = np.full((4,), 1e9, f)
        p = c % 4
        for m in range(p):
            offs[m] = float(PER * (p - 1 - m))
        core = dict(shared)
        core.update({
            "x": cc(x[b, t0:t0 + PER]),
            "cext": ce.astype(NPBF), "sext": se.astype(NPBF),
            "swapm": swapm.astype(NPBF), "ident": ident.astype(NPBF),
            "maskst": maskst.astype(NPBF),
            "maskt4": np.tile(maskst, (1, 4)).astype(NPBF),
            "idx128": idx128,
            "offs": np.broadcast_to(offs, (128, 4)).copy(),
        })
        cores.append(core)
    return cores


DEBUG = False
_LAST_SEMS = None


def build_nc():
    nc = bass.Bass("TRN2", target_bir_lowering=False)
    es = ExitStack()
    esA = ExitStack()
    esB = ExitStack()

    def dram(name, shape, dt=F32, kind="ExternalInput"):
        return nc.dram_tensor(name, shape, dt, kind=kind).ap()

    X = dram("x", [PER, HS])
    QWT = dram("qwT", [HS, HS]); KWT = dram("kwT", [HS, HS])
    VWT = dram("vwT", [HS, HS]); OWT = dram("owT", [HS, HS])
    GWTP = dram("gwTp", [HS, 512])
    WQBD = dram("wqbd", [8, 128, 64]); WKBD = dram("wkbd", [8, 128, 64])
    W1D = dram("w1", [E, HS, DFF]); W2D = dram("w2", [E, DFF, HS])
    WR8 = dram("wr8", [128, 64]); B2D = dram("b2", [E, HS])
    B1FD = dram("b1feat", [128, 256])
    BRREP = dram("brRep", [128, E])
    OBREP = dram("obRep", [128, HS]); LNGREP = dram("lngRep", [128, HS])
    LNBREP = dram("lnbRep", [128, HS]); GBPADT = dram("gbPadT", [128, 512])
    LGP = dram("lgp", [128, 4]); GBPADF = dram("gbPadF", [128, 4])
    CEXT = dram("cext", [128, PER], BF16); SEXT = dram("sext", [128, PER], BF16)
    SWAPM = dram("swapm", [128, 128], BF16); IDENT = dram("ident", [128, 128], BF16)
    MASKST = dram("maskst", [128, 128], BF16)
    MASKT4 = dram("maskt4", [128, 512], BF16)
    IDX128 = dram("idx128", [128, C])
    OFFS = dram("offs", [128, 4])
    OUT = dram("out", [PER, HS], kind="ExternalOutput")
    if DEBUG:
        DQT = dram("dqt", [128, 4 * PER], BF16, kind="ExternalOutput")
        DYF = dram("dyf", [128, 8 * PER], BF16, kind="ExternalOutput")
        DMST = dram("dmst", [128, 256], F32, kind="ExternalOutput")
        DTAB = dram("dtab", [128, 4 * C * 3 + 512 + 4 * 3], F32, kind="ExternalOutput")
        DAT = dram("dat", [128, 4 * GRP], BF16, kind="ExternalOutput")
        DAK = dram("dak", [128, 4 * 512], BF16, kind="ExternalOutput")
        DVW = dram("dvw", [128, 4096], BF16, kind="ExternalOutput")
        DXF = dram("dxf", [128, 8 * GRP], BF16, kind="ExternalOutput")
        DH1 = dram("dh1", [PER, HS], F32, kind="ExternalOutput")
        DU = dram("du", [PER, HS], F32, kind="ExternalOutput")
        DST2 = dram("dst2", [128, 16 * NCH], F32, kind="ExternalOutput")
        DWTK = dram("dwtk", [128, NCH * E], F32, kind="ExternalOutput")
        DWFE = dram("dwfe", 8, [PER] if False else [8, PER], BF16, kind="ExternalOutput") if False else dram("dwfe", [8, PER], BF16, kind="ExternalOutput")
        DYA = dram("dya", [128, 8192], F32, kind="ExternalOutput")
        DLGT = dram("dlgt", [128, NCH * E], F32, kind="ExternalOutput")
        DGTS = dram("dgts", [128, NCH * 8], F32, kind="ExternalOutput")
        DMSTE = dram("dmste", [128, 1024], BF16, kind="ExternalOutput")
        DSGT = dram("dsgt", [128, 1024], F32, kind="ExternalOutput")
        DOBR = dram("dobr", [128, HS], BF16, kind="ExternalOutput")
        DSTA = dram("dsta", [128, 16 * NCH], F32, kind="ExternalOutput")
    X2TOK = nc.dram_tensor("x2tok", [PER, HS], F32,
                           kind="ExternalOutput" if DEBUG else "Internal").ap()
    CCIN = nc.dram_tensor("ccin", [128, 256], F32)
    CCOUT = nc.dram_tensor("ccout", [512, 256], F32)

    def sbp(stack, name, p, n, dt, side=None):
        kw = {} if side is None else {"side": side}
        return stack.enter_context(nc.sbuf_tensor(name, [p, n], dt, **kw))

    QT = sbp(es, "qt", 128, 4 * PER, BF16)
    XT0 = sbp(es, "xt0", 128, HS, F32)
    XT1 = sbp(es, "xt1", 128, HS, F32)
    H1T = sbp(es, "h1t", 128, HS, F32)
    OUTT0 = sbp(es, "outt0", 128, HS, F32)
    OUTT1 = sbp(es, "outt1", 128, HS, F32)
    X2B = sbp(es, "x2b", 128, HS, BF16)
    IDT = sbp(es, "idt", 128, 128, BF16)
    OBR = sbp(es, "obr", 128, HS, BF16)
    LNGR = sbp(es, "lngr", 128, HS, BF16)
    LNBR = sbp(es, "lnbr", 128, HS, BF16)
    BRR = sbp(es, "brr", 128, E, F32)
    B2B = sbp(es, "b2b", 8, HS, BF16)
    WR8B = sbp(es, "wr8b", 128, 64, BF16)
    WFE = sbp(es, "wfe", 8, PER, BF16)
    WTK = sbp(es, "wtk", 128, NCH * E, F32)
    STA = sbp(es, "sta", 128, 16, F32)
    LGT = sbp(es, "lgt", 128, E, F32)
    GTS = sbp(es, "gts", 128, 8, F32)
    LAMP = sbp(es, "lamp", 128, 4, F32)
    DECP = sbp(es, "decp", 128, 4, F32)
    DECC = sbp(es, "decc", 128, 4, F32)
    DPOW = sbp(es, "dpow", 128, 4 * NCH, F32)
    WCMB = sbp(es, "wcmb", 128, 16, F32)
    MST = sbp(es, "mst", 128, 256, F32)
    MEXP = sbp(es, "mexp", 128, 16 * 64, BF16)
    MSTE = sbp(es, "mste", 128, 16 * 64, BF16)
    SGT = sbp(es, "sgt", 128, 4 * 256, F32)
    XR0 = sbp(es, "xr0", 128, HS, F32)
    XR1 = sbp(es, "xr1", 128, HS, F32)
    LNSC = sbp(es, "lnsc", 128, 1, F32)
    EX8 = sbp(es, "ex8", 128, 8, F32)
    EPSC = sbp(es, "epsc", 128, 1, F32)
    DBGL = sbp(es, "dbgl", 128, NCH * E, F32)
    DBGG = sbp(es, "dbgg", 128, NCH * 8, F32)
    SPCV = sbp(es, "spcv", 128, 1, F32)

    YF = sbp(esB, "yf", 128, 8 * PER, BF16, side="right")

    WSL = [sbp(esA, "wsl0", 128, 8192, BF16, side="right"),
           sbp(esA, "wsl1", 128, 8192, BF16, side="right")]
    GWTB = sbp(esA, "gwtb", 128, 4096, BF16, side="right")
    BDQ = sbp(esA, "bdq", 128, 512, BF16, side="right")
    BDK = sbp(esA, "bdk", 128, 512, BF16, side="right")
    XFW = sbp(esA, "xfw", 128, 8 * GRP, BF16, side="right")
    VW = sbp(esA, "vw", 128, 4096, BF16, side="right")
    ATW = sbp(esA, "atw", 128, 4 * GRP, BF16, side="right")
    AKW = sbp(esA, "akw", 128, 4 * 512, BF16, side="right")
    KRW = sbp(esA, "krw", 128, 8 * GRP, BF16, side="right")
    GFW = sbp(esA, "gfw", 128, 4 * GRP, BF16, side="right")
    GTW = sbp(esA, "gtw", 128, 4 * 512, BF16, side="right")
    RPW = sbp(esA, "rpw", 128, 2 * GRP, BF16, side="right")
    SPW = sbp(esA, "spw", 128, 16 * C, BF16, side="right")
    QHW = sbp(esA, "qhw", 128, 4 * C, BF16, side="right")
    CEXW = sbp(esA, "cexw", 128, 2 * GRP, BF16, side="right")
    SEXW = sbp(esA, "sexw", 128, 2 * GRP, BF16, side="right")
    SWP = sbp(esA, "swp", 128, 128, BF16, side="right")
    MSK = sbp(esA, "msk", 128, 128, BF16, side="right")
    MSKT4 = sbp(esA, "mskt4", 128, 512, BF16, side="right")
    IDXT = sbp(esA, "idxt", 128, C, F32, side="right")
    GBT = sbp(esA, "gbt", 128, 512, F32, side="right")
    GBPF = sbp(esA, "gbpf", 128, 4, F32, side="right")
    LGPT = sbp(esA, "lgpt", 128, 4, F32, side="right")
    OFFT = sbp(esA, "offt", 128, 4, F32, side="right")
    PPOS = sbp(esA, "ppos", 128, 4 * C, F32, side="right")
    PNEG = sbp(esA, "pneg", 128, 4 * C, F32, side="right")
    PREV = sbp(esA, "prev", 128, 4 * C, F32, side="right")
    PRVT = sbp(esA, "prvt", 128, 512, F32, side="right")

    PP0 = es.enter_context(nc.psum_tensor("pp0", [128, 512], F32))
    PP1 = es.enter_context(nc.psum_tensor("pp1", [128, 512], F32))
    PS0 = es.enter_context(nc.psum_tensor("ps0", [128, 512], F32))
    PS1 = es.enter_context(nc.psum_tensor("ps1", [128, 512], F32))
    PY0 = es.enter_context(nc.psum_tensor("py0", [128, 512], F32))
    PY1 = es.enter_context(nc.psum_tensor("py1", [128, 512], F32))
    PU = es.enter_context(nc.psum_tensor("pu", [128, 512], F32))
    PT = es.enter_context(nc.psum_tensor("pt", [128, 1024], BF16))

    bld = Builder(nc)
    sW = bld.sem(es, "sW")
    sIO = bld.sem(es, "sIO")
    sT = bld.sem(es, "sT")
    sV = bld.sem(es, "sV")
    sS = bld.sem(es, "sS")
    sCC = bld.sem(es, "sCC")
    sG = bld.sem(es, "sG")
    sCN = bld.sem(es, "sCN")
    O = bld.op

    def acc_act(s, *a, **kw):
        s.activation(*a, **kw)
        return s.nop()

    def smargin():
        O("scalar", lambda s: s.activation(SPCV[:, :], LNSC[:, :], ACT.Identity), [], sS)
        return sS.n

    def gdma(dst, src, waits=()):
        O("gpsimd", lambda g, d=dst, s=src: g.dma_start(out=d, in_=s), waits, sW, 16)
        return sW.n

    def sdma(dst, src, waits=()):
        O("sync", lambda s_, d=dst, s=src: s_.dma_start(out=d, in_=s), waits, sIO, 16)
        return sIO.n

    def cdma(dst, src):
        O("sync", lambda s_, d=dst, s=src: s_.dma_start(out=d, in_=s), (), sCN, 16)
        return sCN.n

    cdma(SWP[:, :], SWAPM); cdma(IDT[:, :], IDENT); cdma(MSK[:, :], MASKST)
    cdma(MSKT4[:, :], MASKT4)
    cdma(IDXT[:, :], IDX128)
    cdma(GBT[:, :], GBPADT); cdma(BRR[:, :], BRREP)
    cdma(OFFT[:, :], OFFS); cdma(LGPT[:, :], LGP); cdma(GBPF[:, :], GBPADF)
    gdma(OBR[:, :], OBREP)
    gdma(LNGR[:, :], LNGREP)
    gdma(LNBR[:, :], LNBREP)
    biascast_v = 0
    biasld_w = sW.n

    WSRC = [QWT, KWT, VWT]

    def load_wslot(seq, waits):
        src = WSRC[seq % 3]
        sl = WSL[seq % 2]
        for k in range(8):
            gdma(sl[:, 1024 * k:1024 * (k + 1)], src[128 * k:128 * (k + 1), :], waits)
        return sW.n

    wrdy = {}
    wconsumed_t = {}
    for k in range(8):
        gdma(GWTB[:, 512 * k:512 * (k + 1)], GWTP[128 * k:128 * (k + 1), :])
        gdma(BDQ[:, 64 * k:64 * (k + 1)], WQBD[k, :, :])
        gdma(BDK[:, 64 * k:64 * (k + 1)], WKBD[k, :, :])
    gdma(WR8B[:, :], WR8)
    gdma(B2B[:, :], B2D)
    wsmall_done = sW.n
    wrdy[0] = load_wslot(0, [(sW, wsmall_done)])
    wrdy[1] = load_wslot(1, [(sW, wrdy[0])])

    O("vector", lambda v: v.memset(LNSC[:, :], LN_SCALE), [], sV)
    O("vector", lambda v: v.memset(EPSC[:, :], EPS), [], sV)
    O("vector", lambda v: v.memset(SPCV[:, :], 0.0), [], sV)
    lnsc_v = sV.n
    epsc_v = sV.n
    O("vector", lambda v: v.memset(MEXP[:, :], 0.0), [], sV)
    O("vector", lambda v: v.memset(MST[:, :], 0.0), [], sV)
    O("vector", lambda v: v.memset(MSTE[:, :], 0.0), [], sV)

    lam_in = sCN.n
    O("scalar", lambda s: s.activation(STA[:, 0:4], LGPT[:, :], ACT.Exp, scale=-1.0),
      [(sCN, lam_in)], sS, drain=True)
    O("vector", lambda v: v.tensor_scalar(STA[:, 4:8], STA[:, 0:4], 1.0, None, ALU.add),
      [(sS, sS.n)], sV, drain=True)
    O("scalar", lambda s: s.activation(STA[:, 8:12], STA[:, 4:8], ACT.Ln), [(sV, sV.n)], sS,
      drain=True)
    O("vector", lambda v: v.tensor_scalar(LAMP[:, :], STA[:, 8:12], -1.0, None, ALU.mult),
      [(sS, sS.n)], sV, drain=True)
    O("vector", lambda v: v.memset(SPCV[:, :], 0.0), [], sV)
    O("vector", lambda v: v.memset(SPCV[:, :], 0.0), [], sV)
    lam_v = sV.n
    O("scalar", lambda s: s.activation(DECP[:, :], LAMP[:, :], ACT.Exp), [(sV, lam_v)], sS)
    O("scalar", lambda s: s.activation(DECC[:, :], LAMP[:, :], ACT.Exp, scale=float(C)),
      [], sS)
    for i in range(4):
        O("vector", lambda v, i=i: v.tensor_scalar(PPOS[:, C * i:C * (i + 1)], IDXT[:, :],
                                                   LAMP[:, i:i + 1], None, ALU.mult),
          [(sV, lam_v)], sV, drain=True)
        sv = sV.n
        O("scalar", lambda s, i=i: s.activation(PPOS[:, C * i:C * (i + 1)],
                                                PPOS[:, C * i:C * (i + 1)], ACT.Exp),
          [(sV, sv)], sS)
        O("vector", lambda v, i=i: v.tensor_scalar(PNEG[:, C * i:C * (i + 1)], IDXT[:, :],
                                                   LAMP[:, i:i + 1], None, ALU.mult), [], sV,
          drain=True)
        sv = sV.n
        O("scalar", lambda s, i=i: s.activation(PNEG[:, C * i:C * (i + 1)],
                                                PNEG[:, C * i:C * (i + 1)], ACT.Exp,
                                                scale=-1.0, bias=LNSC[:, 0:1]),
          [(sV, sv), (sV, lnsc_v)], sS)
        O("vector", lambda v, i=i: v.tensor_scalar(STA[:, 12:13], LAMP[:, i:i + 1],
                                                   127.0, LN_SCALE, ALU.mult, ALU.add),
          [(sS, sS.n)], sV, drain=True)
        O("vector", lambda v, i=i: v.tensor_scalar(PREV[:, C * i:C * (i + 1)], IDXT[:, :],
                                                   LAMP[:, i:i + 1], None, ALU.mult), [], sV,
          drain=True)
        sv = sV.n
        O("scalar", lambda s, i=i: s.activation(PREV[:, C * i:C * (i + 1)],
                                                PREV[:, C * i:C * (i + 1)], ACT.Exp,
                                                scale=-1.0, bias=STA[:, 12:13]),
          [(sV, sv)], sS)
        for cc_ in range(NCH):
            O("vector", lambda v, i=i, cc_=cc_: v.tensor_scalar(
                DPOW[:, NCH * i + cc_:NCH * i + cc_ + 1], LAMP[:, i:i + 1],
                float(C * cc_ + 1), None, ALU.mult), [], sV,
              drain=(cc_ == NCH - 1))
        sv = sV.n
        O("scalar", lambda s, i=i: s.activation(DPOW[:, NCH * i:NCH * (i + 1)],
                                                DPOW[:, NCH * i:NCH * (i + 1)], ACT.Exp),
          [(sV, sv)], sS)
        O("vector", lambda v, i=i: v.tensor_scalar(WCMB[:, 4 * i:4 * (i + 1)], OFFT[:, :],
                                                   LAMP[:, i:i + 1], None, ALU.mult), [], sV,
          drain=True)
        sv = sV.n
        O("scalar", lambda s, i=i: s.activation(WCMB[:, 4 * i:4 * (i + 1)],
                                                WCMB[:, 4 * i:4 * (i + 1)], ACT.Exp),
          [(sV, sv)], sS)
    tab_s = sS.n

    for i in range(4):
        O("vector", lambda v, i=i: v.tensor_copy(QHW[:, C * i:C * (i + 1)],
                                                 PREV[:, C * i:C * (i + 1)]),
          [(sS, tab_s)], sV)
    prvc = sV.n
    for i in range(4):
        O("tensor", lambda t, i=i: t.transpose(PT[:, 128 * i:128 * (i + 1)],
                                               QHW[:, C * i:C * (i + 1)], IDT[:, :]),
          [(sV, prvc), (sCN, 16 * 3)], sT)
    trp = sT.n
    O("vector", lambda v: v.tensor_copy(PRVT[:, :].rearrange("p (i c) -> p i c", i=4),
                                        PT[:, 0:512].rearrange("p (i c) -> p i c", i=4)),
      [(sT, trp)], sV)
    prvt_v = sV.n

    win_free_v = 0
    pt_free_v = prvt_v
    xe_free_v = [biascast_v, biascast_v]
    spw_evac = {0: 0, 1: 0}
    py_evac = {0: 0, 1: 0}
    cex_ld = [0, 0]
    qhw_free_t = trp
    mexp_g = 0
    x_chain_io = 0

    for g in range(NG):
        g0 = g * GRP
        par = g % 2
        cex_ld[par] = sdma(CEXW[:, GRP * par:GRP * (par + 1)], CEXT[:, g0:g0 + GRP],
                           [(sV, win_free_v)])
        sdma(SEXW[:, GRP * par:GRP * (par + 1)], SEXT[:, g0:g0 + GRP], [])
        rope_ld = sIO.n
        CEXs = CEXW[:, GRP * par:GRP * (par + 1)]
        SEXs = SEXW[:, GRP * par:GRP * (par + 1)]

        for tt in range(4):
            xb = XT0 if tt % 2 == 0 else XT1
            lv = sdma(xb[:, :], X[g0 + 128 * tt:g0 + 128 * (tt + 1), :],
                      [(sV, xe_free_v[tt % 2]), (sIO, x_chain_io)])
            x_chain_io = lv
            O("vector", lambda v, xb=xb: v.tensor_copy(X2B[:, :], xb[:, :]),
              [(sIO, lv), (sV, pt_free_v)], sV)
            cv = sV.n
            xe_free_v[tt % 2] = cv
            for j in range(8):
                O("tensor", lambda t, j=j: t.transpose(PT[:, 128 * j:128 * (j + 1)],
                                                       X2B[:, 128 * j:128 * (j + 1)],
                                                       IDT[:, :]), [(sV, cv)], sT)
            tv = sT.n
            O("vector", lambda v, tt=tt: v.tensor_copy(
                XFW[:, :].rearrange("p (j c) -> p j c", j=8)[:, :, 128 * tt:128 * (tt + 1)],
                PT[:, 0:1024].rearrange("p (j c) -> p j c", j=8)),
              [(sT, tv), (sV, win_free_v)], sV)
            pt_free_v = sV.n
        xfw_v = sV.n

        rp = [PP0, PP1, PS0, PS1]
        for i in range(4):
            for k in range(8):
                O("tensor", lambda t, i=i, k=k: t.matmul(
                    rp[i][:, 0:GRP], GWTB[:, 512 * k + 128 * i:512 * k + 128 * (i + 1)],
                    XFW[:, GRP * k:GRP * (k + 1)], start=(k == 0), stop=(k == 7)),
                  [(sV, xfw_v), (sW, wsmall_done)], sT)
        gf_t = sT.n
        for i in range(4):
            O("scalar", lambda s, i=i: s.activation(GFW[:, GRP * i:GRP * (i + 1)],
                                                    rp[i][:, 0:GRP], ACT.Sigmoid,
                                                    bias=GBPF[:, i:i + 1]),
              [(sT, gf_t), (sCN, lam_in)], sS)
        gf_s = sS.n
        for tt in range(4):
            pb = rp[tt]
            for k in range(8):
                O("tensor", lambda t, tt=tt, k=k, pb=pb: t.matmul(
                    pb[:, 0:512],
                    XFW[:, GRP * k + 128 * tt:GRP * k + 128 * (tt + 1)],
                    GWTB[:, 512 * k:512 * (k + 1)],
                    start=(k == 0), stop=(k == 7)), [(sS, gf_s)], sT)
            gt_t = sT.n
            O("vector", lambda v, tt=tt, pb=pb: v.tensor_tensor(
                GTW[:, 512 * tt:512 * (tt + 1)], pb[:, 0:512], GBT[:, :], ALU.add),
              [(sT, gt_t)], sV)
            sv = sV.n
            O("scalar", lambda s, tt=tt: s.activation(GTW[:, 512 * tt:512 * (tt + 1)],
                                                      GTW[:, 512 * tt:512 * (tt + 1)],
                                                      ACT.Sigmoid), [(sV, sv)], sS)
        gt_s = sS.n

        qsl = WSL[(3 * g) % 2]
        q_rdy = wrdy[3 * g]
        for j in range(8):
            pj = PY0 if j % 2 == 0 else PY1
            for k in range(8):
                O("tensor", lambda t, j=j, k=k, pj=pj, qsl=qsl: t.matmul(
                    pj[:, 0:GRP], qsl[:, 1024 * k + 128 * j:1024 * k + 128 * (j + 1)],
                    XFW[:, GRP * k:GRP * (k + 1)], start=(k == 0), stop=(k == 7)),
                  [(sS, gt_s), (sW, q_rdy), (sV, py_evac[j % 2])], sT)
            pj_t = sT.n
            O("vector", lambda v, pj=pj: v.tensor_copy(RPW[:, 0:GRP], pj[:, 0:GRP]),
              [(sT, pj_t)], sV)
            raw_v = sV.n
            py_evac[j % 2] = raw_v
            O("tensor", lambda t: t.matmul(PU[:, 0:GRP], SWP[:, :], RPW[:, 0:GRP],
                                           start=True, stop=True), [(sV, raw_v)], sT)
            sw_t = sT.n
            O("vector", lambda v, CEXs=CEXs: v.tensor_tensor(
                RPW[:, GRP:2 * GRP], RPW[:, 0:GRP], CEXs, ALU.mult),
              [(sIO, rope_ld)], sV)
            O("vector", lambda v, SEXs=SEXs: v.tensor_tensor(
                RPW[:, 0:GRP], PU[:, 0:GRP], SEXs, ALU.mult), [(sT, sw_t)], sV)
            O("vector", lambda v: v.tensor_tensor(RPW[:, GRP:2 * GRP], RPW[:, GRP:2 * GRP],
                                                  RPW[:, 0:GRP], ALU.add), [], sV)
            rope_v = sV.n
            pbank = rp[j // 2]
            O("tensor", lambda t, j=j, pbank=pbank: t.matmul(
                pbank[64 * (j % 2):64 * (j % 2) + 64, 0:GRP],
                BDQ[:, 64 * j:64 * (j + 1)], RPW[:, GRP:2 * GRP],
                start=True, stop=True), [(sV, rope_v)], sT)
        qr_t = sT.n
        wconsumed_t[3 * g] = qr_t
        if 3 * g + 2 < 3 * NG:
            wrdy[3 * g + 2] = load_wslot(3 * g + 2, [(sT, wconsumed_t.get(3 * g, 0))])
        for i in range(4):
            for wc in range(4):
                O("vector", lambda v, i=i, wc=wc, g0=g0: v.tensor_tensor(
                    QT[:, PER * i + g0 + C * wc:PER * i + g0 + C * (wc + 1)],
                    rp[i][:, C * wc:C * (wc + 1)], PPOS[:, C * i:C * (i + 1)], ALU.mult),
                  [(sT, qr_t)], sV)
        qt_v = sV.n

        ksl = WSL[(3 * g + 1) % 2]
        k_rdy = wrdy[3 * g + 1]
        for j in range(8):
            pj = PY0 if j % 2 == 0 else PY1
            for k in range(8):
                O("tensor", lambda t, j=j, k=k, pj=pj, ksl=ksl: t.matmul(
                    pj[:, 0:GRP], ksl[:, 1024 * k + 128 * j:1024 * k + 128 * (j + 1)],
                    XFW[:, GRP * k:GRP * (k + 1)], start=(k == 0), stop=(k == 7)),
                  [(sV, qt_v), (sW, k_rdy), (sV, py_evac[j % 2])], sT)
            pj_t = sT.n
            O("vector", lambda v, pj=pj: v.tensor_copy(RPW[:, 0:GRP], pj[:, 0:GRP]),
              [(sT, pj_t)], sV)
            raw_v = sV.n
            py_evac[j % 2] = raw_v
            O("tensor", lambda t: t.matmul(PU[:, 0:GRP], SWP[:, :], RPW[:, 0:GRP],
                                           start=True, stop=True), [(sV, raw_v)], sT)
            sw_t = sT.n
            O("vector", lambda v, j=j, CEXs=CEXs: v.tensor_tensor(
                KRW[:, GRP * j:GRP * (j + 1)], RPW[:, 0:GRP], CEXs, ALU.mult), [], sV)
            O("vector", lambda v, SEXs=SEXs: v.tensor_tensor(
                RPW[:, 0:GRP], PU[:, 0:GRP], SEXs, ALU.mult), [(sT, sw_t)], sV)
            O("vector", lambda v, j=j: v.tensor_tensor(KRW[:, GRP * j:GRP * (j + 1)],
                                                       KRW[:, GRP * j:GRP * (j + 1)],
                                                       RPW[:, 0:GRP], ALU.add), [], sV)
        kro_v = sV.n
        wconsumed_t[3 * g + 1] = sT.n
        if 3 * g + 3 < 3 * NG:
            wrdy[3 * g + 3] = load_wslot(3 * g + 3, [(sT, wconsumed_t.get(3 * g + 1, 0))])
        for j in range(8):
            pbank = rp[j // 2]
            O("tensor", lambda t, j=j, pbank=pbank: t.matmul(
                pbank[64 * (j % 2):64 * (j % 2) + 64, 0:GRP],
                BDK[:, 64 * j:64 * (j + 1)], KRW[:, GRP * j:GRP * (j + 1)],
                start=True, stop=True), [(sV, kro_v)], sT)
        krf_t = sT.n
        for i in range(4):
            for wc in range(4):
                O("vector", lambda v, i=i, wc=wc: v.tensor_tensor(
                    SPW[:, 0:C], rp[i][:, C * wc:C * (wc + 1)],
                    GFW[:, GRP * i + C * wc:GRP * i + C * (wc + 1)], ALU.mult),
                  [(sT, krf_t)], sV)
                O("vector", lambda v, i=i, wc=wc: v.tensor_tensor(
                    ATW[:, GRP * i + C * wc:GRP * i + C * (wc + 1)],
                    SPW[:, 0:C], PNEG[:, C * i:C * (i + 1)], ALU.mult), [], sV)
        atw_v = sV.n
        for tt in range(4):
            pb = PY0 if tt % 2 == 0 else PY1
            for j in range(8):
                O("tensor", lambda t, tt=tt, j=j, pb=pb: t.matmul(
                    pb[:, 64 * j:64 * (j + 1)],
                    KRW[:, GRP * j + 128 * tt:GRP * j + 128 * (tt + 1)],
                    BDK[:, 64 * j:64 * (j + 1)], start=True, stop=True),
                  [(sV, atw_v), (sV, py_evac[tt % 2])], sT)
            kt_t = sT.n
            O("vector", lambda v, tt=tt, pb=pb: v.tensor_tensor(
                SPW[:, 0:512], pb[:, 0:512], GTW[:, 512 * tt:512 * (tt + 1)], ALU.mult),
              [(sT, kt_t), (sV, prvt_v)], sV)
            O("vector", lambda v, tt=tt: v.tensor_tensor(
                AKW[:, 512 * tt:512 * (tt + 1)], SPW[:, 0:512], PRVT[:, :], ALU.mult),
              [], sV)
            py_evac[tt % 2] = sV.n
        akw_v = sV.n

        vsl = WSL[(3 * g + 2) % 2]
        v_rdy = wrdy[3 * g + 2]
        for tt in range(4):
            for hh in range(2):
                pb = PY0 if hh == 0 else PY1
                for k in range(8):
                    O("tensor", lambda t, tt=tt, hh=hh, k=k, pb=pb, vsl=vsl: t.matmul(
                        pb[:, 0:512],
                        XFW[:, GRP * k + 128 * tt:GRP * k + 128 * (tt + 1)],
                        vsl[:, 1024 * k + 512 * hh:1024 * k + 512 * (hh + 1)],
                        start=(k == 0), stop=(k == 7)),
                      [(sV, akw_v), (sW, v_rdy), (sV, py_evac[hh])], sT)
                vt_t = sT.n
                O("vector", lambda v, tt=tt, hh=hh, pb=pb: v.tensor_copy(
                    VW[:, 1024 * tt + 512 * hh:1024 * tt + 512 * (hh + 1)], pb[:, 0:512]),
                  [(sT, vt_t)], sV)
                py_evac[hh] = sV.n
        vw_v = sV.n
        if DEBUG and g == 0:
            sdma(DAT, ATW[:, :], [(sV, atw_v)])
            sdma(DAK, AKW[:, :], [(sV, akw_v)])
            sdma(DVW, VW[:, :], [(sV, vw_v)])
            sdma(DXF, XFW[:, :], [(sV, xfw_v)])
        wconsumed_t[3 * g + 2] = sT.n

        for wc in range(4):
            for i in range(4):
                O("gpsimd", lambda g_, i=i, wc=wc, g0=g0: g_.tensor_scalar(
                    QHW[:, C * i:C * (i + 1)],
                    QT[:, PER * i + g0 + C * wc:PER * i + g0 + C * (wc + 1)],
                    DECP[:, i:i + 1], 1.0, ALU.mult, ALU.mult),
                  [(sV, vw_v), (sT, qhw_free_t)], sG)
            qh_g = sG.n
            for u in range(4):
                pb = [PS0, PS1, PY0, PY1][u]
                bfree = spw_evac[u] if u < 2 else py_evac[u - 2]
                for kk in range(4):
                    h = u + 4 * kk
                    i = h // 4
                    O("tensor", lambda t, i=i, u=u, kk=kk, wc=wc, g0=g0, pb=pb: t.matmul(
                        pb[:, C * kk:C * (kk + 1)],
                        ATW[32 * u:32 * u + 32, GRP * i + C * wc:GRP * i + C * (wc + 1)],
                        QT[32 * u:32 * u + 32,
                           PER * i + g0 + C * wc:PER * i + g0 + C * (wc + 1)],
                        start=True, stop=True, tile_position=(32 * u, 0)),
                      [(sG, qh_g), (sV, vw_v), (sV, bfree)], sT)
                sc_t = sT.n
                O("vector", lambda v, u=u, pb=pb: v.tensor_tensor(
                    SPW[:, 4 * C * u:4 * C * (u + 1)], pb[:, 0:4 * C], MSKT4[:, :],
                    ALU.mult), [(sT, sc_t)], sV)
            sc_v = sV.n
            for j in range(8):
                pb = PY0 if j % 2 == 0 else PY1
                for hh in range(2):
                    h = 2 * j + hh
                    slot = 4 * (h % 4) + h // 4
                    O("tensor", lambda t, h=h, hh=hh, wc=wc, pb=pb, slot=slot: t.matmul(
                        pb[64 * hh:64 * hh + 64, 0:C],
                        VW[:, 1024 * wc + 64 * h:1024 * wc + 64 * (h + 1)],
                        SPW[:, C * slot:C * (slot + 1)], start=True, stop=False),
                      [(sV, sc_v), (sV, py_evac[j % 2]), (sG, mexp_g)], sT)
                    O("tensor", lambda t, h=h, hh=hh, pb=pb: t.matmul(
                        pb[64 * hh:64 * hh + 64, 0:C], MEXP[:, 64 * h:64 * (h + 1)],
                        QHW[:, C * (h // 4):C * (h // 4 + 1)], start=False, stop=True),
                      [], sT)
                y_t = sT.n
                O("vector", lambda v, j=j, g0=g0, wc=wc, pb=pb: v.tensor_copy(
                    YF[:, PER * j + g0 + C * wc:PER * j + g0 + C * (wc + 1)],
                    pb[:, 0:C]), [(sT, y_t)], sV)
                py_evac[j % 2] = sV.n
            y_v = sV.n
            qhw_free_t = sT.n
            for qg in range(4):
                pb = PS0 if qg < 2 else PS1
                O("tensor", lambda t, qg=qg, wc=wc, pb=pb: t.matmul(
                    pb[:, 256 * (qg % 2):256 * (qg % 2) + 256],
                    AKW[:, 512 * wc + 128 * qg:512 * wc + 128 * (qg + 1)],
                    VW[:, 1024 * wc + 256 * qg:1024 * wc + 256 * (qg + 1)],
                    start=True, stop=True), [(sV, y_v)], sT)
            u_t = sT.n
            for h in range(H):
                qg, u = h // 4, h % 4
                pb = PS0 if qg < 2 else PS1
                O("vector", lambda v, qg=qg, u=u, pb=pb: v.scalar_tensor_tensor(
                    MST[32 * u:32 * u + 32, 64 * qg:64 * (qg + 1)],
                    MST[32 * u:32 * u + 32, 64 * qg:64 * (qg + 1)],
                    DECC[32 * u:32 * u + 32, qg:qg + 1],
                    pb[32 * u:32 * u + 32,
                       256 * (qg % 2) + 64 * u:256 * (qg % 2) + 64 * (u + 1)],
                    ALU.mult, ALU.add), [(sT, u_t), (sG, mexp_g)], sV)
            mst_v = sV.n
            for h in range(H):
                qg, u = h // 4, h % 4
                O("gpsimd", lambda g_, h=h, qg=qg, u=u: g_.tensor_copy(
                    MEXP[32 * u:32 * u + 32, 64 * h:64 * (h + 1)],
                    MST[32 * u:32 * u + 32, 64 * qg:64 * (qg + 1)]),
                  [(sV, mst_v), (sT, qhw_free_t)], sG)
            mexp_g = sG.n
            spw_evac[0] = mst_v
            spw_evac[1] = mst_v
            win_free_v = mst_v
        if 3 * g + 4 < 3 * NG:
            wrdy[3 * g + 4] = load_wslot(3 * g + 4, [(sT, wconsumed_t[3 * g + 2])])
    scan_v = sV.n
    scan_t = sT.n
    if DEBUG:
        sdma(DQT, QT[:, :], [(sV, scan_v)])
        sdma(DMST, MST[:, :], [(sV, scan_v)])
        sdma(DTAB[:, 0:4 * C], PPOS[:, :], [(sS, tab_s)])
        sdma(DTAB[:, 4 * C:8 * C], PNEG[:, :], [])
        sdma(DTAB[:, 8 * C:12 * C], PREV[:, :], [])
        sdma(DTAB[:, 12 * C:12 * C + 512], PRVT[:, :], [(sV, prvt_v)])
        sdma(DTAB[:, 12 * C + 512:12 * C + 516], LAMP[:, :], [(sV, scan_v)])
        sdma(DTAB[:, 12 * C + 516:12 * C + 520], DECP[:, :], [])
        sdma(DTAB[:, 12 * C + 520:12 * C + 524], DECC[:, :], [])

    st_out = sdma(CCIN.ap(), MST[:, :], [(sV, scan_v)])
    O("gpsimd", lambda g_: g_.collective_compute(
        "AllGather", ALU.bypass, replica_groups=[[0, 1, 2, 3], [4, 5, 6, 7]],
        ins=[CCIN.ap().opt()], outs=[CCOUT.ap().opt()]),
      [(sIO, st_out)], sCC)
    cc_n = sCC.n
    for m in range(4):
        sdma(SGT[:, 256 * m:256 * (m + 1)], CCOUT.ap()[128 * m:128 * (m + 1), :],
             [(sCC, cc_n)])
    sg_io = sIO.n
    O("vector", lambda v: v.memset(MST[:, :], 0.0), [(sIO, sg_io)], sV)
    for m in range(4):
        for i in range(4):
            O("vector", lambda v, i=i, m=m: v.scalar_tensor_tensor(
                MST[:, 64 * i:64 * (i + 1)],
                SGT[:, 256 * m + 64 * i:256 * m + 64 * (i + 1)],
                WCMB[:, 4 * i + m:4 * i + m + 1],
                MST[:, 64 * i:64 * (i + 1)], ALU.mult, ALU.add), [], sV, drain=True)
    O("vector", lambda v: v.memset(SPCV[:, :], 0.0), [], sV, drain=True)
    for h in range(H):
        qg, u = h // 4, h % 4
        O("vector", lambda v, h=h, qg=qg, u=u: v.tensor_copy(
            MSTE[32 * u:32 * u + 32, 64 * h:64 * (h + 1)],
            MST[32 * u:32 * u + 32, 64 * qg:64 * (qg + 1)]), [], sV)
    mste_v = sV.n
    if DEBUG:
        sdma(DMSTE, MSTE[:, :], [(sV, mste_v)])
        sdma(DSGT, SGT[:, :], [(sV, mste_v)])

    QGW = KRW
    corr_evac = {0: 0, 1: 0}
    for i in range(4):
        for cc_ in range(NCH):
            O("vector", lambda v, i=i, cc_=cc_: v.tensor_scalar(
                QGW[:, C * cc_:C * (cc_ + 1)],
                QT[:, PER * i + C * cc_:PER * i + C * (cc_ + 1)],
                DPOW[:, NCH * i + cc_:NCH * i + cc_ + 1], None, ALU.mult),
              [(sV, mste_v)], sV)
        qg_v = sV.n
        for u in range(4):
            h = 4 * i + u
            for qq in range(4):
                n = u * 4 + qq
                pb = PY0 if n % 2 == 0 else PY1
                O("tensor", lambda t, h=h, qq=qq, pb=pb: t.matmul(
                    pb[0:64, 0:512], MSTE[:, 64 * h:64 * (h + 1)],
                    QGW[:, 512 * qq:512 * (qq + 1)], start=True, stop=True),
                  [(sV, qg_v), (sT, scan_t), (sV, corr_evac[n % 2])], sT)
                co_t = sT.n
                O("vector", lambda v, h=h, qq=qq, pb=pb: v.tensor_tensor(
                    YF[64 * (h % 2):64 * (h % 2) + 64,
                       PER * (h // 2) + 512 * qq:PER * (h // 2) + 512 * (qq + 1)],
                    YF[64 * (h % 2):64 * (h % 2) + 64,
                       PER * (h // 2) + 512 * qq:PER * (h // 2) + 512 * (qq + 1)],
                    pb[0:64, 0:512], ALU.add), [(sT, co_t)], sV)
                corr_evac[n % 2] = sV.n
    corr_v = sV.n
    corr_t = sT.n
    if DEBUG:
        sdma(DYF, YF[:, :], [(sV, corr_v)])

    esA.close()
    FFW = sbp(es, "ffw", 128, 24576, BF16)
    X2F = sbp(es, "x2f", 128, 8 * PER, BF16)

    for k in range(8):
        gdma(FFW[:, 1024 * k:1024 * (k + 1)], OWT[128 * k:128 * (k + 1), :],
             [(sT, corr_t)])
    owt_w = sW.n

    if DEBUG:
        sdma(DOBR, OBR[:, :], [(sV, corr_v)])
    x2tok_wr = [0] * NCH
    x2f_tile = [0] * NCH
    wfe_tile = [0] * NCH
    h1t_free_io = 0
    du_free_io = 0
    dst_free_io = 0
    op_evac = {0: corr_v, 1: corr_v}
    pt_free2 = pt_free_v
    for tt in range(NCH):
        xb = XT0 if tt % 2 == 0 else XT1
        lx = sdma(xb[:, :], X[128 * tt:128 * (tt + 1), :],
                  [(sV, xe_free_v[tt % 2]), (sIO, x_chain_io)])
        x_chain_io = lx
        for half in range(2):
            pb = PY0 if half == 0 else PY1
            for k in range(8):
                O("tensor", lambda t, tt=tt, half=half, k=k, pb=pb: t.matmul(
                    pb[:, 0:512], YF[:, PER * k + 128 * tt:PER * k + 128 * (tt + 1)],
                    FFW[:, 1024 * k + 512 * half:1024 * k + 512 * (half + 1)],
                    start=(k == 0), stop=(k == 7)),
                  [(sV, corr_v), (sW, owt_w), (sV, op_evac[half])], sT)
        op_t = sT.n
        for half, pb in ((0, PY0), (1, PY1)):
            O("vector", lambda v, half=half, pb=pb: v.tensor_tensor(
                H1T[:, 512 * half:512 * (half + 1)], pb[:, 0:512],
                OBR[:, 512 * half:512 * (half + 1)], ALU.add),
              [(sT, op_t), (sIO, lx), (sIO, h1t_free_io), (sW, biasld_w)], sV)
            O("vector", lambda v, half=half, xb=xb: v.tensor_tensor(
                H1T[:, 512 * half:512 * (half + 1)], H1T[:, 512 * half:512 * (half + 1)],
                xb[:, 512 * half:512 * (half + 1)], ALU.add), [], sV)
            op_evac[half] = sV.n
        h1_v = sV.n
        if DEBUG:
            dh1w = sdma(DH1[128 * tt:128 * (tt + 1), :], H1T[:, :], [(sV, h1_v)])
            h1t_free_io = dh1w
        O("vector", lambda v: v.reduce_sum(STA[:, 0:1], H1T[:, :], axis=AXX),
          [(sV, h1_v), (sIO, du_free_io), (sIO, dst_free_io)], sV)
        O("vector", lambda v: v.scalar_tensor_tensor(OUTT1[:, :], H1T[:, :], 1.0,
                                                     H1T[:, :], ALU.mult, ALU.mult), [], sV)
        O("vector", lambda v: v.reduce_sum(STA[:, 1:2], OUTT1[:, :], axis=AXX), [], sV,
          drain=True)
        st1v = sV.n
        O("scalar", lambda s: s.activation(STA[:, 2:3], STA[:, 0:1], ACT.Identity,
                                           scale=1.0 / HS), [(sV, st1v)], sS)
        O("scalar", lambda s: s.activation(STA[:, 3:4], STA[:, 0:1], ACT.Square,
                                           scale=1.0 / HS), [], sS, drain=True)
        st1s = sS.n
        O("vector", lambda v: v.scalar_tensor_tensor(STA[:, 4:5], STA[:, 1:2], 1.0 / HS,
                                                     STA[:, 3:4], ALU.mult, ALU.subtract),
          [(sS, st1s)], sV, drain=True)
        va1 = sV.n
        O("scalar", lambda s: s.activation(STA[:, 6:7], STA[:, 4:5], ACT.Ln,
                                           bias=EPSC[:, 0:1]), [(sV, va1)], sS, drain=True)
        O("scalar", lambda s: s.activation(STA[:, 7:8], STA[:, 6:7], ACT.Exp, scale=-0.5),
          [], sS, drain=True)
        rs1 = sS.n
        O("vector", lambda v: v.tensor_scalar(OUTT0[:, :], H1T[:, :], STA[:, 2:3],
                                              STA[:, 7:8], ALU.subtract, ALU.mult),
          [(sS, rs1)], sV)
        O("vector", lambda v: v.tensor_tensor(OUTT0[:, :], OUTT0[:, :], LNGR[:, :],
                                              ALU.mult), [], sV)
        O("vector", lambda v: v.tensor_tensor(OUTT0[:, :], OUTT0[:, :], LNBR[:, :],
                                              ALU.add), [], sV)
        O("vector", lambda v, xb=xb: v.tensor_tensor(OUTT0[:, :], OUTT0[:, :], xb[:, :],
                                                     ALU.add), [], sV)
        u_v = sV.n
        if DEBUG:
            du_free_io = sdma(DU[128 * tt:128 * (tt + 1), :], OUTT0[:, :], [(sV, u_v)])
        xe_free_v[tt % 2] = u_v
        O("vector", lambda v: v.reduce_sum(STA[:, 8:9], OUTT0[:, :], axis=AXX),
          [(sV, u_v)], sV)
        O("vector", lambda v: v.scalar_tensor_tensor(OUTT1[:, :], OUTT0[:, :], 1.0,
                                                     OUTT0[:, :], ALU.mult, ALU.mult),
          [], sV)
        O("vector", lambda v: v.reduce_sum(STA[:, 9:10], OUTT1[:, :], axis=AXX), [], sV,
          drain=True)
        st2v = sV.n
        O("scalar", lambda s: s.activation(STA[:, 10:11], STA[:, 8:9], ACT.Identity,
                                           scale=1.0 / HS), [(sV, st2v)], sS)
        O("scalar", lambda s: s.activation(STA[:, 11:12], STA[:, 8:9], ACT.Square,
                                           scale=1.0 / HS), [], sS, drain=True)
        st2s = sS.n
        O("vector", lambda v: v.scalar_tensor_tensor(STA[:, 12:13], STA[:, 9:10], 1.0 / HS,
                                                     STA[:, 11:12], ALU.mult,
                                                     ALU.subtract), [(sS, st2s)], sV,
          drain=True)
        va2 = sV.n
        O("scalar", lambda s: s.activation(STA[:, 14:15], STA[:, 12:13], ACT.Ln,
                                           bias=EPSC[:, 0:1]), [(sV, va2)], sS, drain=True)
        O("scalar", lambda s: s.activation(STA[:, 15:16], STA[:, 14:15], ACT.Exp,
                                           scale=-0.5), [], sS, drain=True)
        rs2 = sS.n
        O("vector", lambda v: v.tensor_scalar(H1T[:, :], OUTT0[:, :], STA[:, 10:11],
                                              STA[:, 15:16], ALU.subtract, ALU.mult),
          [(sS, rs2)], sV)
        x2_v = sV.n
        if DEBUG:
            dst_free_io = sdma(DST2[:, 16 * tt:16 * (tt + 1)], STA[:, :], [(sV, x2_v)])
        x2tok_wr[tt] = sdma(X2TOK[128 * tt:128 * (tt + 1), :], H1T[:, :], [(sV, x2_v)])
        h1t_free_io = x2tok_wr[tt]
        O("vector", lambda v: v.tensor_copy(X2B[:, :], H1T[:, :]), [(sV, pt_free2)], sV)
        x2b_v = sV.n
        for j in range(8):
            O("tensor", lambda t, j=j: t.transpose(PT[:, 128 * j:128 * (j + 1)],
                                                   X2B[:, 128 * j:128 * (j + 1)],
                                                   IDT[:, :]), [(sV, x2b_v)], sT)
        tx_t = sT.n
        O("vector", lambda v, tt=tt: v.tensor_copy(
            X2F[:, :].rearrange("p (j c) -> p j c", j=8)[:, :, 128 * tt:128 * (tt + 1)],
            PT[:, 0:1024].rearrange("p (j c) -> p j c", j=8)), [(sT, tx_t)], sV)
        x2f_v = sV.n
        x2f_tile[tt] = x2f_v
        for k in range(8):
            O("tensor", lambda t, tt=tt, k=k: t.matmul(
                PU[:, 0:E], X2F[:, PER * k + 128 * tt:PER * k + 128 * (tt + 1)],
                WR8B[:, E * k:E * (k + 1)], start=(k == 0), stop=(k == 7)),
              [(sV, x2f_v)], sT)
        lg_t = sT.n
        O("vector", lambda v: v.tensor_tensor(LGT[:, :], PU[:, 0:E], BRR[:, :], ALU.add),
          [(sT, lg_t)], sV, drain=True)
        O("vector", lambda v: v.reduce_max(GTS[:, 0:1], LGT[:, :], axis=AXX), [], sV,
          drain=True)
        mx_v = sV.n
        O("scalar", lambda s: s.activation(GTS[:, 1:2], GTS[:, 0:1], ACT.Identity,
                                           scale=-1.0), [(sV, mx_v)], sS, drain=True)
        O("scalar", lambda s: s.activation(EX8[:, :], LGT[:, :], ACT.Exp,
                                           bias=GTS[:, 1:2]), [], sS, drain=True)
        ex_s = sS.n
        O("vector", lambda v: v.reduce_sum(GTS[:, 3:4], EX8[:, :], axis=AXX),
          [(sS, ex_s)], sV, drain=True)
        O("vector", lambda v, tt=tt: v.tensor_scalar(WTK[:, E * tt:E * (tt + 1)],
                                                     LGT[:, :], GTS[:, 0:1], None,
                                                     ALU.is_ge), [], sV)
        O("vector", lambda v: v.reciprocal(GTS[:, 4:5], GTS[:, 3:4]), [], sV, drain=True)
        O("vector", lambda v, tt=tt: v.tensor_scalar(WTK[:, E * tt:E * (tt + 1)],
                                                     WTK[:, E * tt:E * (tt + 1)],
                                                     GTS[:, 4:5], None, ALU.mult), [], sV,
          drain=True)
        if DEBUG:
            O("vector", lambda v, tt=tt: v.tensor_copy(DBGL[:, E * tt:E * (tt + 1)],
                                                       LGT[:, :]), [], sV)
            O("vector", lambda v, tt=tt: v.tensor_copy(DBGG[:, 8 * tt:8 * (tt + 1)],
                                                       GTS[:, :]), [], sV)
        O("vector", lambda v, tt=tt: v.tensor_copy(X2B[:, 0:E],
                                                   WTK[:, E * tt:E * (tt + 1)]), [], sV,
          drain=True)
        wtk_v = sV.n
        O("tensor", lambda t: t.transpose(PT[0:E, 0:128], X2B[:, 0:E], IDT[:, :]),
          [(sV, wtk_v)], sT)
        wt_t = sT.n
        O("vector", lambda v, tt=tt: v.tensor_copy(WFE[0:E, 128 * tt:128 * (tt + 1)],
                                                   PT[0:E, 0:128]), [(sT, wt_t)], sV)
        pt_free2 = sV.n
        wfe_tile[tt] = sV.n
    epiA_v = sV.n
    epiA_t = sT.n
    if DEBUG:
        sdma(DWTK, WTK[:, :], [(sV, epiA_v)])
        sdma(DWFE, WFE[:, :], [(sV, epiA_v)])
        sdma(DLGT, DBGL[:, :], [(sV, epiA_v)])
        sdma(DGTS, DBGG[:, :], [(sV, epiA_v)])

    esB.close()
    YACC = sbp(es, "yacc", 128, 8192, F32)
    HWB = sbp(es, "hwb", 128, 8192, BF16)
    HT2 = sbp(es, "ht2", 128, 4096, BF16)
    B1FT = sbp(es, "b1ft", 128, 256, F32)
    ldb1 = sdma(B1FT[:, :], B1FD, [])

    slots = [8192, 16384, 0]
    loads = []
    for half in range(2):
        for e in range(E):
            for dc in range(4):
                loads.append(("w1", e, dc))
                loads.append(("w2", e, dc))
    slot_prev_t = {8192: corr_t, 16384: corr_t, 0: epiA_t}
    wload_done = [0] * len(loads)
    load_consumer_t = [0] * len(loads)

    def emit_wload(idx):
        kind, e, dc = loads[idx]
        col0 = slots[idx % 3]
        prev_t = load_consumer_t[idx - 3] if idx >= 3 else slot_prev_t[col0]
        if kind == "w1":
            src = W1D[e, :, 1024 * dc:1024 * (dc + 1)].rearrange(
                "(k p) c -> p k c", p=128)
        else:
            src = W2D[e, 1024 * dc:1024 * (dc + 1), :].rearrange(
                "(m p) c -> p m c", p=128)
        dst = FFW[:, col0:col0 + 8192].rearrange("p (k c) -> p k c", c=1024)
        O("gpsimd", lambda g_, dst=dst, src=src: g_.dma_start(out=dst, in_=src),
          [(sT, prev_t)], sW, 16)
        return sW.n

    wload_done[0] = emit_wload(0)
    wload_done[1] = emit_wload(1)
    nextload = 2

    yps = [PY0, PY1, PS0, PS1]
    yev = {0: 0, 1: 0, 2: 0, 3: 0}
    hev = {0: 0, 1: 0}
    hcv = {0: 0, 1: 0}
    hw_used_t = {0: 0, 1: 0}
    xr_free = [0, 0]
    out_done = 0
    ln2_prev_s = 0
    for half in range(2):
        hoff = 1024 * half
        lxs = [0] * 9
        for tt in range(8):
            gtile = half * 8 + tt
            xb = XR0 if tt % 2 == 0 else XR1
            lxs[tt] = sdma(xb[:, :], X2TOK[128 * gtile:128 * (gtile + 1), :],
                           [(sV, xr_free[tt % 2]), (sIO, x2tok_wr[gtile])])
            if tt == 7:
                lxs[8] = sdma(SGT[:, 0:256], X2TOK[128 * gtile:128 * gtile + 128, 0:256],
                              [])
            for hh in range(2):
                bi = (2 * tt + hh) % 4
                pb = yps[bi]
                O("tensor", lambda t, gtile=gtile, hh=hh, pb=pb: t.matmul(
                    pb[:, 0:512], WFE[0:E, 128 * gtile:128 * (gtile + 1)],
                    B2B[0:E, 512 * hh:512 * (hh + 1)], start=True, stop=True),
                  [(sV, wfe_tile[gtile]), (sW, wsmall_done), (sV, yev[bi])], sT)
                b2_t = sT.n
                O("vector", lambda v, tt=tt, hh=hh, xb=xb, pb=pb: v.tensor_tensor(
                    YACC[:, 1024 * tt + 512 * hh:1024 * tt + 512 * (hh + 1)],
                    pb[:, 0:512], xb[:, 512 * hh:512 * (hh + 1)], ALU.add),
                  [(sT, b2_t), (sIO, lxs[tt] + 16), (sS, ln2_prev_s)], sV)
                yev[bi] = sV.n
            xr_free[tt % 2] = sV.n
        yinit_v = sV.n

        for e in range(E):
            for dc in range(4):
                li = 2 * (32 * half + 4 * e + dc)
                w1_rdy = wload_done[li]
                w2_rdy = wload_done[li + 1]
                s1 = slots[li % 3]
                s2 = slots[(li + 1) % 3]
                for gg in range(2):
                    hbuf = 4096 * (gg % 2)
                    for m in range(8):
                        pb = PP0 if m % 2 == 0 else PP1
                        for k in range(8):
                            O("tensor", lambda t, m=m, k=k, gg=gg, s1=s1, hoff=hoff,
                              pb=pb: t.matmul(
                                  pb[:, 0:512],
                                  FFW[:, s1 + 1024 * k + 128 * m:s1 + 1024 * k + 128 * (m + 1)],
                                  X2F[:, PER * k + hoff + 512 * gg:PER * k + hoff + 512 * (gg + 1)],
                                  start=(k == 0), stop=(k == 7)),
                              [(sW, w1_rdy), (sV, x2f_tile[half * 8 + 4 * gg + 3]),
                               (sS, hev[m % 2])], sT)
                        h_t = sT.n
                        O("scalar", lambda s, m=m, e=e, dc=dc, hbuf=hbuf, pb=pb:
                          s.activation(HWB[:, hbuf + 512 * m:hbuf + 512 * (m + 1)],
                                       pb[:, 0:512], ACT.Gelu,
                                       bias=B1FT[:, 32 * e + 8 * dc + m:32 * e + 8 * dc + m + 1]),
                          [(sT, h_t), (sIO, ldb1), (sT, hw_used_t[gg % 2])], sS)
                        hev[m % 2] = sS.n
                    hg_s = smargin()
                    for t4 in range(4):
                        tt = 4 * gg + t4
                        for hh in range(2):
                            bi = (2 * tt + hh) % 4
                            pb = yps[bi]
                            for m in range(8):
                                O("tensor", lambda t, m=m, hh=hh, t4=t4, hbuf=hbuf,
                                  s2=s2, pb=pb: t.matmul(
                                      pb[:, 0:512],
                                      HWB[:, hbuf + 512 * m + 128 * t4:hbuf + 512 * m + 128 * (t4 + 1)],
                                      FFW[:, s2 + 1024 * m + 512 * hh:s2 + 1024 * m + 512 * (hh + 1)],
                                      start=(m == 0), stop=(m == 7)),
                                  [(sS, hg_s), (sW, w2_rdy), (sV, yev[bi])], sT)
                            yy_t = sT.n
                            O("vector", lambda v, tt=tt, hh=hh, e=e, half=half, pb=pb:
                              v.scalar_tensor_tensor(
                                  YACC[:, 1024 * tt + 512 * hh:1024 * tt + 512 * (hh + 1)],
                                  pb[:, 0:512],
                                  WTK[:, E * (half * 8 + tt) + e:E * (half * 8 + tt) + e + 1],
                                  YACC[:, 1024 * tt + 512 * hh:1024 * tt + 512 * (hh + 1)],
                                  ALU.mult, ALU.add), [(sT, yy_t)], sV)
                            yev[bi] = sV.n
                    hw_used_t[gg % 2] = sT.n
                load_consumer_t[li] = sT.n
                load_consumer_t[li + 1] = sT.n
                if nextload < len(loads):
                    wload_done[nextload] = emit_wload(nextload)
                    nextload += 1
                if nextload < len(loads):
                    wload_done[nextload] = emit_wload(nextload)
                    nextload += 1
        ffn_v = sV.n
        if DEBUG and half == 0:
            sdma(DYA, YACC[:, :], [(sV, ffn_v)])
        for tt in range(8):
            gtile = half * 8 + tt
            ob = OUTT0 if tt % 2 == 0 else OUTT1
            O("vector", lambda v, tt=tt: v.reduce_sum(
                STA[:, 0:1], YACC[:, 1024 * tt:1024 * (tt + 1)], axis=AXX),
              [(sV, ffn_v), (sIO, out_done)], sV)
            O("vector", lambda v, tt=tt, ob=ob: v.scalar_tensor_tensor(
                ob[:, :], YACC[:, 1024 * tt:1024 * (tt + 1)], 1.0,
                YACC[:, 1024 * tt:1024 * (tt + 1)], ALU.mult, ALU.mult), [], sV)
            O("vector", lambda v, ob=ob: v.reduce_sum(STA[:, 1:2], ob[:, :], axis=AXX),
              [], sV, drain=True)
            stfv = sV.n
            O("scalar", lambda s: s.activation(STA[:, 2:3], STA[:, 0:1], ACT.Identity,
                                               scale=1.0 / HS), [(sV, stfv)], sS)
            O("scalar", lambda s: s.activation(STA[:, 3:4], STA[:, 0:1], ACT.Square,
                                               scale=1.0 / HS), [], sS, drain=True)
            stfs = sS.n
            O("vector", lambda v: v.scalar_tensor_tensor(STA[:, 4:5], STA[:, 1:2],
                                                         1.0 / HS, STA[:, 3:4],
                                                         ALU.mult, ALU.subtract),
              [(sS, stfs)], sV, drain=True)
            va = sV.n
            O("scalar", lambda s: s.activation(STA[:, 6:7], STA[:, 4:5], ACT.Ln,
                                               bias=EPSC[:, 0:1]), [(sV, va)], sS,
              drain=True)
            O("scalar", lambda s: s.activation(STA[:, 7:8], STA[:, 6:7], ACT.Exp,
                                               scale=-0.5), [], sS, drain=True)
            rs = sS.n
            O("vector", lambda v, tt=tt, ob=ob: v.tensor_scalar(
                ob[:, :], YACC[:, 1024 * tt:1024 * (tt + 1)], STA[:, 2:3], STA[:, 7:8],
                ALU.subtract, ALU.mult), [(sS, rs)], sV)
            o_v = sV.n
            out_done = sdma(OUT[128 * gtile:128 * (gtile + 1), :], ob[:, :], [(sV, o_v)])
        ln2_prev_s = sS.n

    final_io = sIO.n
    O("sync", lambda s: s.nop(), [(sIO, final_io)], None)

    global _LAST_SEMS
    _LAST_SEMS = {s.name: s.n for s in (sW, sIO, sT, sV, sS, sCC)}
    _LAST_SEMS["_ids"] = {s.name: s.h for s in (sW, sIO, sT, sV, sS, sCC)}
    with nc.Block() as block:
        bld.emit(block)
    es.close()
    return nc


_NC_CACHE = None


def kernel(**inputs):
    global _NC_CACHE
    cores = host_prepare(inputs)
    if _NC_CACHE is None:
        _NC_CACHE = build_nc()
    nc = _NC_CACHE
    res = run_bass_kernel_spmd(nc, cores, core_ids=list(range(8)))
    out = np.empty((B, T, HS), np.float32)
    for c in range(8):
        b = c // 4
        t0 = (c % 4) * PER
        out[b, t0:t0 + PER] = np.asarray(res.results[c]["out"])
    return out


# version 07 (final)
# speedup vs baseline: 1.0698x; 1.0005x over previous
import numpy as np
import ml_dtypes
from contextlib import ExitStack

import concourse.bass as bass
import concourse.mybir as mybir
from concourse.bass_utils import run_bass_kernel_spmd

F32 = mybir.dt.float32
BF16 = mybir.dt.bfloat16
ALU = mybir.AluOpType
ACT = mybir.ActivationFunctionType
AXX = mybir.AxisListType.X
NPBF = ml_dtypes.bfloat16

B, T, HS = 2, 8192, 1024
H, R, D = 16, 16, 64
PER = 2048
C = 128
NCH = PER // C
GRP = 512
NG = PER // GRP
E = 8
DFF = 4096
EPS = 1e-5
LN_SCALE = float(np.log(0.125))


class Ctr:
    __slots__ = ("h", "n", "name")

    def __init__(self, h, name):
        self.h = h
        self.n = 0
        self.name = name


class Builder:
    ENGS = ("tensor", "vector", "scalar", "gpsimd", "sync")

    def __init__(self, nc):
        self.nc = nc
        self.prog = {e: [] for e in self.ENGS}
        self.waited = {}

    def sem(self, es, name):
        return Ctr(es.enter_context(self.nc.semaphore(name)), name)

    def op(self, eng, fn, waits=(), inc=None, inck=1, drain=False):
        wl = []
        for ctr, val in waits:
            key = (eng, ctr.name)
            if val > 0 and self.waited.get(key, 0) < val:
                wl.append((ctr.h, val))
                self.waited[key] = val
        if inc is not None:
            inc.n += inck
        semh = inc.h if inc is not None else None

        if eng in ("vector", "scalar") and semh is not None:
            drain = True

        def closure(e, fn=fn, wl=wl, semh=semh, inck=inck, drain=drain):
            for h, v in wl:
                e.wait_ge(h, v)
            inst = fn(e)
            if drain:
                inst = e.drain()
            if semh is not None:
                if inst is None:
                    inst = e.nop()
                inst.then_inc(semh, inck)
        self.prog[eng].append(closure)

    def emit(self, block):
        for eng in self.ENGS:
            ops = self.prog[eng]
            if not ops:
                continue
            getattr(block, eng)(lambda e, ops=ops: [f(e) for f in ops])


def host_prepare(inputs):
    f = np.float32
    qw = np.asarray(inputs["L0_attn_q_w"], f)
    kw = np.asarray(inputs["L0_attn_k_w"], f)
    vw = np.asarray(inputs["L0_attn_v_w"], f)
    ow = np.asarray(inputs["L0_attn_out_w"], f)
    gw = np.asarray(inputs["L0_attn_g_w"], f)
    Wq = np.asarray(inputs["L0_attn_Wq"], f)
    Wk = np.asarray(inputs["L0_attn_Wk"], f)
    gb = np.asarray(inputs["L0_attn_g_b"], f)
    bdl = np.asarray(inputs["L0_attn_base_decay_logits"], f)
    ob = np.asarray(inputs["L0_attn_out_b"], f)
    lng = np.asarray(inputs["L0_attn_ln_g"], f)
    lnb = np.asarray(inputs["L0_attn_ln_b"], f)
    W1 = np.asarray(inputs["L0_ffn_W1"], f)
    b1 = np.asarray(inputs["L0_ffn_b1"], f)
    W2 = np.asarray(inputs["L0_ffn_W2"], f)
    b2 = np.asarray(inputs["L0_ffn_b2"], f)
    Wr = np.asarray(inputs["L0_ffn_Wr"], f)
    br = np.asarray(inputs["L0_ffn_br"], f)
    x = np.asarray(inputs["L0_attn_x"], f)
    cc = np.ascontiguousarray

    shared = {
        "qwT": cc(qw.T), "kwT": cc(kw.T), "vwT": cc(vw.T), "owT": cc(ow.T),
        "w1": cc(W1), "w2": cc(W2), "b2": cc(b2),
    }
    gwTp = np.zeros((HS, 512), f)
    for h in range(H):
        gwTp[:, 32 * h:32 * h + R] = gw[16 * h:16 * h + R, :].T
    shared["gwTp"] = gwTp
    wqbd = np.zeros((8, 128, 64), f)
    wkbd = np.zeros((8, 128, 64), f)
    for j in range(8):
        for u in range(2):
            h = 2 * j + u
            wqbd[j, 64 * u:64 * u + D, 32 * u:32 * u + R] = Wq[h].T
            wkbd[j, 64 * u:64 * u + D, 32 * u:32 * u + R] = Wk[h].T
    shared["wqbd"] = wqbd
    shared["wkbd"] = wkbd
    wr8 = np.zeros((128, 64), f)
    for k in range(8):
        wr8[:, 8 * k:8 * (k + 1)] = Wr[128 * k:128 * (k + 1), :]
    shared["wr8"] = wr8
    b1f = np.zeros((128, 256), f)
    for e in range(E):
        for dc in range(4):
            for m in range(8):
                b1f[:, e * 32 + dc * 8 + m] = b1[e, dc * 1024 + m * 128:dc * 1024 + (m + 1) * 128]
    shared["b1feat"] = b1f
    shared["brRep"] = np.broadcast_to(br, (128, E)).copy()
    shared["obRep"] = np.broadcast_to(ob, (128, HS)).copy()
    shared["lngRep"] = np.broadcast_to(lng, (128, HS)).copy()
    shared["lnbRep"] = np.broadcast_to(lnb, (128, HS)).copy()
    gbpadv = np.zeros((512,), f)
    for h in range(H):
        gbpadv[32 * h:32 * h + R] = gb[16 * h:16 * h + R]
    shared["gbPadT"] = np.broadcast_to(gbpadv, (128, 512)).copy()
    lgp = np.zeros((128, 4), f)
    gbp = np.zeros((128, 4), f)
    for i in range(4):
        for u in range(4):
            h = 4 * i + u
            lgp[32 * u:32 * u + R, i] = bdl[h]
            gbp[32 * u:32 * u + R, i] = gb[16 * h:16 * h + R]
    shared["lgp"] = lgp
    shared["gbPadF"] = gbp

    halfD = D // 2
    invf = (1.0 / 10000.0 ** (np.arange(halfD, dtype=f) / halfD)).astype(f)
    posg = np.arange(T, dtype=f)
    frq = posg[:, None] * invf[None, :]
    cosT = np.cos(frq).astype(f)
    sinT = np.sin(frq).astype(f)
    cext = np.zeros((T, D), f)
    sext = np.zeros((T, D), f)
    cext[:, 0::2] = cosT
    cext[:, 1::2] = cosT
    sext[:, 0::2] = -sinT
    sext[:, 1::2] = sinT
    swapm = np.zeros((128, 128), f)
    for i in range(0, 128, 2):
        swapm[i, i + 1] = 1.0
        swapm[i + 1, i] = 1.0
    ident = np.eye(128, dtype=f)
    lt = np.arange(C, dtype=f)
    maskst = (lt[None, :] >= lt[:, None]).astype(f)
    idx128 = np.broadcast_to(lt, (128, C)).copy()

    cores = []
    for c in range(8):
        b = c // 4
        t0 = (c % 4) * PER
        ce = np.zeros((128, PER), f)
        se = np.zeros((128, PER), f)
        ce[0:64] = cext[t0:t0 + PER].T
        ce[64:128] = ce[0:64]
        se[0:64] = sext[t0:t0 + PER].T
        se[64:128] = se[0:64]
        offs = np.full((4,), 1e9, f)
        p = c % 4
        for m in range(p):
            offs[m] = float(PER * (p - 1 - m))
        core = dict(shared)
        core.update({
            "x": cc(x[b, t0:t0 + PER]),
            "cext": ce.astype(NPBF), "sext": se.astype(NPBF),
            "swapm": swapm.astype(NPBF), "ident": ident.astype(NPBF),
            "maskst": maskst.astype(NPBF),
            "maskt4": np.tile(maskst, (1, 4)).astype(NPBF),
            "idx128": idx128,
            "offs": np.broadcast_to(offs, (128, 4)).copy(),
        })
        cores.append(core)
    return cores


DEBUG = False
_LAST_SEMS = None


def build_nc():
    nc = bass.Bass("TRN2", target_bir_lowering=False)
    es = ExitStack()
    esA = ExitStack()
    esB = ExitStack()

    def dram(name, shape, dt=F32, kind="ExternalInput"):
        return nc.dram_tensor(name, shape, dt, kind=kind).ap()

    X = dram("x", [PER, HS])
    QWT = dram("qwT", [HS, HS]); KWT = dram("kwT", [HS, HS])
    VWT = dram("vwT", [HS, HS]); OWT = dram("owT", [HS, HS])
    GWTP = dram("gwTp", [HS, 512])
    WQBD = dram("wqbd", [8, 128, 64]); WKBD = dram("wkbd", [8, 128, 64])
    W1D = dram("w1", [E, HS, DFF]); W2D = dram("w2", [E, DFF, HS])
    WR8 = dram("wr8", [128, 64]); B2D = dram("b2", [E, HS])
    B1FD = dram("b1feat", [128, 256])
    BRREP = dram("brRep", [128, E])
    OBREP = dram("obRep", [128, HS]); LNGREP = dram("lngRep", [128, HS])
    LNBREP = dram("lnbRep", [128, HS]); GBPADT = dram("gbPadT", [128, 512])
    LGP = dram("lgp", [128, 4]); GBPADF = dram("gbPadF", [128, 4])
    CEXT = dram("cext", [128, PER], BF16); SEXT = dram("sext", [128, PER], BF16)
    SWAPM = dram("swapm", [128, 128], BF16); IDENT = dram("ident", [128, 128], BF16)
    MASKST = dram("maskst", [128, 128], BF16)
    MASKT4 = dram("maskt4", [128, 512], BF16)
    IDX128 = dram("idx128", [128, C])
    OFFS = dram("offs", [128, 4])
    OUT = dram("out", [PER, HS], kind="ExternalOutput")
    if DEBUG:
        DQT = dram("dqt", [128, 4 * PER], BF16, kind="ExternalOutput")
        DYF = dram("dyf", [128, 8 * PER], BF16, kind="ExternalOutput")
        DMST = dram("dmst", [128, 256], F32, kind="ExternalOutput")
        DTAB = dram("dtab", [128, 4 * C * 3 + 512 + 4 * 3], F32, kind="ExternalOutput")
        DAT = dram("dat", [128, 4 * GRP], BF16, kind="ExternalOutput")
        DAK = dram("dak", [128, 4 * 512], BF16, kind="ExternalOutput")
        DVW = dram("dvw", [128, 4096], BF16, kind="ExternalOutput")
        DXF = dram("dxf", [128, 8 * GRP], BF16, kind="ExternalOutput")
        DH1 = dram("dh1", [PER, HS], F32, kind="ExternalOutput")
        DU = dram("du", [PER, HS], F32, kind="ExternalOutput")
        DST2 = dram("dst2", [128, 16 * NCH], F32, kind="ExternalOutput")
        DWTK = dram("dwtk", [128, NCH * E], F32, kind="ExternalOutput")
        DWFE = dram("dwfe", 8, [PER] if False else [8, PER], BF16, kind="ExternalOutput") if False else dram("dwfe", [8, PER], BF16, kind="ExternalOutput")
        DYA = dram("dya", [128, 8192], F32, kind="ExternalOutput")
        DLGT = dram("dlgt", [128, NCH * E], F32, kind="ExternalOutput")
        DGTS = dram("dgts", [128, NCH * 8], F32, kind="ExternalOutput")
        DMSTE = dram("dmste", [128, 1024], BF16, kind="ExternalOutput")
        DSGT = dram("dsgt", [128, 1024], F32, kind="ExternalOutput")
        DOBR = dram("dobr", [128, HS], BF16, kind="ExternalOutput")
        DSTA = dram("dsta", [128, 16 * NCH], F32, kind="ExternalOutput")
    X2TOK = nc.dram_tensor("x2tok", [PER, HS], F32,
                           kind="ExternalOutput" if DEBUG else "Internal").ap()
    CCIN = nc.dram_tensor("ccin", [128, 256], F32)
    CCOUT = nc.dram_tensor("ccout", [512, 256], F32)

    def sbp(stack, name, p, n, dt, side=None):
        kw = {} if side is None else {"side": side}
        return stack.enter_context(nc.sbuf_tensor(name, [p, n], dt, **kw))

    QT = sbp(es, "qt", 128, 4 * PER, BF16)
    XT0 = sbp(es, "xt0", 128, HS, F32)
    XT1 = sbp(es, "xt1", 128, HS, F32)
    H1T = sbp(es, "h1t", 128, HS, F32)
    OUTT0 = sbp(es, "outt0", 128, HS, F32)
    OUTT1 = sbp(es, "outt1", 128, HS, F32)
    X2B = sbp(es, "x2b", 128, HS, BF16)
    IDT = sbp(es, "idt", 128, 128, BF16)
    OBR = sbp(es, "obr", 128, HS, BF16)
    LNGR = sbp(es, "lngr", 128, HS, BF16)
    LNBR = sbp(es, "lnbr", 128, HS, BF16)
    BRR = sbp(es, "brr", 128, E, F32)
    B2B = sbp(es, "b2b", 8, HS, BF16)
    WR8B = sbp(es, "wr8b", 128, 64, BF16)
    WFE = sbp(es, "wfe", 8, PER, BF16)
    WTK = sbp(es, "wtk", 128, NCH * E, F32)
    STA = sbp(es, "sta", 128, 16, F32)
    LGT = sbp(es, "lgt", 128, E, F32)
    GTS = sbp(es, "gts", 128, 8, F32)
    LAMP = sbp(es, "lamp", 128, 4, F32)
    DECP = sbp(es, "decp", 128, 4, F32)
    DECC = sbp(es, "decc", 128, 4, F32)
    DPOW = sbp(es, "dpow", 128, 4 * NCH, F32)
    WCMB = sbp(es, "wcmb", 128, 16, F32)
    MST = sbp(es, "mst", 128, 256, F32)
    MEXP = sbp(es, "mexp", 128, 16 * 64, BF16)
    MSTE = sbp(es, "mste", 128, 16 * 64, BF16)
    SGT = sbp(es, "sgt", 128, 4 * 256, F32)
    XR0 = sbp(es, "xr0", 128, HS, F32)
    XR1 = sbp(es, "xr1", 128, HS, F32)
    LNSC = sbp(es, "lnsc", 128, 1, F32)
    EX8 = sbp(es, "ex8", 128, 8, F32)
    EPSC = sbp(es, "epsc", 128, 1, F32)
    DBGL = sbp(es, "dbgl", 128, NCH * E, F32)
    DBGG = sbp(es, "dbgg", 128, NCH * 8, F32)
    SPCV = sbp(es, "spcv", 128, 1, F32)

    YF = sbp(esB, "yf", 128, 8 * PER, BF16, side="right")

    WSL = [sbp(esA, "wsl0", 128, 8192, BF16, side="right"),
           sbp(esA, "wsl1", 128, 8192, BF16, side="right")]
    GWTB = sbp(esA, "gwtb", 128, 4096, BF16, side="right")
    BDQ = sbp(esA, "bdq", 128, 512, BF16, side="right")
    BDK = sbp(esA, "bdk", 128, 512, BF16, side="right")
    XFW = sbp(esA, "xfw", 128, 8 * GRP, BF16, side="right")
    VW = sbp(esA, "vw", 128, 4096, BF16, side="right")
    ATW = sbp(esA, "atw", 128, 4 * GRP, BF16, side="right")
    AKW = sbp(esA, "akw", 128, 4 * 512, BF16, side="right")
    KRW = sbp(esA, "krw", 128, 8 * GRP, BF16, side="right")
    GFW = sbp(esA, "gfw", 128, 4 * GRP, BF16, side="right")
    GTW = sbp(esA, "gtw", 128, 4 * 512, BF16, side="right")
    RPW = sbp(esA, "rpw", 128, 2 * GRP, BF16, side="right")
    SPW = sbp(esA, "spw", 128, 16 * C, BF16, side="right")
    QHW = sbp(esA, "qhw", 128, 4 * C, BF16, side="right")
    CEXW = sbp(esA, "cexw", 128, 2 * GRP, BF16, side="right")
    SEXW = sbp(esA, "sexw", 128, 2 * GRP, BF16, side="right")
    SWP = sbp(esA, "swp", 128, 128, BF16, side="right")
    MSK = sbp(esA, "msk", 128, 128, BF16, side="right")
    MSKT4 = sbp(esA, "mskt4", 128, 512, BF16, side="right")
    IDXT = sbp(esA, "idxt", 128, C, F32, side="right")
    GBT = sbp(esA, "gbt", 128, 512, F32, side="right")
    GBPF = sbp(esA, "gbpf", 128, 4, F32, side="right")
    LGPT = sbp(esA, "lgpt", 128, 4, F32, side="right")
    OFFT = sbp(esA, "offt", 128, 4, F32, side="right")
    PPOS = sbp(esA, "ppos", 128, 4 * C, F32, side="right")
    PNEG = sbp(esA, "pneg", 128, 4 * C, F32, side="right")
    PREV = sbp(esA, "prev", 128, 4 * C, F32, side="right")
    PRVT = sbp(esA, "prvt", 128, 512, F32, side="right")

    PP0 = es.enter_context(nc.psum_tensor("pp0", [128, 512], F32))
    PP1 = es.enter_context(nc.psum_tensor("pp1", [128, 512], F32))
    PS0 = es.enter_context(nc.psum_tensor("ps0", [128, 512], F32))
    PS1 = es.enter_context(nc.psum_tensor("ps1", [128, 512], F32))
    PY0 = es.enter_context(nc.psum_tensor("py0", [128, 512], F32))
    PY1 = es.enter_context(nc.psum_tensor("py1", [128, 512], F32))
    PU = es.enter_context(nc.psum_tensor("pu", [128, 512], F32))
    PT = es.enter_context(nc.psum_tensor("pt", [128, 1024], BF16))

    bld = Builder(nc)
    sW = bld.sem(es, "sW")
    sIO = bld.sem(es, "sIO")
    sT = bld.sem(es, "sT")
    sV = bld.sem(es, "sV")
    sS = bld.sem(es, "sS")
    sCC = bld.sem(es, "sCC")
    sG = bld.sem(es, "sG")
    sCN = bld.sem(es, "sCN")
    O = bld.op

    def acc_act(s, *a, **kw):
        s.activation(*a, **kw)
        return s.nop()

    def smargin():
        O("scalar", lambda s: s.activation(SPCV[:, :], LNSC[:, :], ACT.Identity), [], sS)
        return sS.n

    def gdma(dst, src, waits=()):
        O("gpsimd", lambda g, d=dst, s=src: g.dma_start(out=d, in_=s), waits, sW, 16)
        return sW.n

    def sdma(dst, src, waits=()):
        O("sync", lambda s_, d=dst, s=src: s_.dma_start(out=d, in_=s), waits, sIO, 16)
        return sIO.n

    def cdma(dst, src):
        O("sync", lambda s_, d=dst, s=src: s_.dma_start(out=d, in_=s), (), sCN, 16)
        return sCN.n

    cdma(SWP[:, :], SWAPM); cdma(IDT[:, :], IDENT); cdma(MSK[:, :], MASKST)
    cdma(MSKT4[:, :], MASKT4)
    cdma(IDXT[:, :], IDX128)
    cdma(GBT[:, :], GBPADT); cdma(BRR[:, :], BRREP)
    cdma(OFFT[:, :], OFFS); cdma(LGPT[:, :], LGP); cdma(GBPF[:, :], GBPADF)
    gdma(OBR[:, :], OBREP)
    gdma(LNGR[:, :], LNGREP)
    gdma(LNBR[:, :], LNBREP)
    biascast_v = 0
    biasld_w = sW.n

    WSRC = [QWT, KWT, VWT]

    def load_wslot(seq, waits):
        src = WSRC[seq % 3]
        sl = WSL[seq % 2]
        for k in range(8):
            gdma(sl[:, 1024 * k:1024 * (k + 1)], src[128 * k:128 * (k + 1), :], waits)
        return sW.n

    wrdy = {}
    wconsumed_t = {}
    for k in range(8):
        gdma(GWTB[:, 512 * k:512 * (k + 1)], GWTP[128 * k:128 * (k + 1), :])
        gdma(BDQ[:, 64 * k:64 * (k + 1)], WQBD[k, :, :])
        gdma(BDK[:, 64 * k:64 * (k + 1)], WKBD[k, :, :])
    gdma(WR8B[:, :], WR8)
    gdma(B2B[:, :], B2D)
    wsmall_done = sW.n
    wrdy[0] = load_wslot(0, [(sW, wsmall_done)])
    wrdy[1] = load_wslot(1, [(sW, wrdy[0])])

    O("vector", lambda v: v.memset(LNSC[:, :], LN_SCALE), [], sV)
    O("vector", lambda v: v.memset(EPSC[:, :], EPS), [], sV)
    O("vector", lambda v: v.memset(SPCV[:, :], 0.0), [], sV)
    lnsc_v = sV.n
    epsc_v = sV.n
    O("vector", lambda v: v.memset(MEXP[:, :], 0.0), [], sV)
    O("vector", lambda v: v.memset(MST[:, :], 0.0), [], sV)
    O("vector", lambda v: v.memset(MSTE[:, :], 0.0), [], sV)

    lam_in = sCN.n
    O("scalar", lambda s: s.activation(STA[:, 0:4], LGPT[:, :], ACT.Exp, scale=-1.0),
      [(sCN, lam_in)], sS, drain=True)
    O("vector", lambda v: v.tensor_scalar(STA[:, 4:8], STA[:, 0:4], 1.0, None, ALU.add),
      [(sS, sS.n)], sV, drain=True)
    O("scalar", lambda s: s.activation(STA[:, 8:12], STA[:, 4:8], ACT.Ln), [(sV, sV.n)], sS,
      drain=True)
    O("vector", lambda v: v.tensor_scalar(LAMP[:, :], STA[:, 8:12], -1.0, None, ALU.mult),
      [(sS, sS.n)], sV, drain=True)
    O("vector", lambda v: v.memset(SPCV[:, :], 0.0), [], sV)
    O("vector", lambda v: v.memset(SPCV[:, :], 0.0), [], sV)
    lam_v = sV.n
    O("scalar", lambda s: s.activation(DECP[:, :], LAMP[:, :], ACT.Exp), [(sV, lam_v)], sS)
    O("scalar", lambda s: s.activation(DECC[:, :], LAMP[:, :], ACT.Exp, scale=float(C)),
      [], sS)
    for i in range(4):
        O("vector", lambda v, i=i: v.tensor_scalar(PPOS[:, C * i:C * (i + 1)], IDXT[:, :],
                                                   LAMP[:, i:i + 1], None, ALU.mult),
          [(sV, lam_v)], sV, drain=True)
        sv = sV.n
        O("scalar", lambda s, i=i: s.activation(PPOS[:, C * i:C * (i + 1)],
                                                PPOS[:, C * i:C * (i + 1)], ACT.Exp),
          [(sV, sv)], sS)
        O("vector", lambda v, i=i: v.tensor_scalar(PNEG[:, C * i:C * (i + 1)], IDXT[:, :],
                                                   LAMP[:, i:i + 1], None, ALU.mult), [], sV,
          drain=True)
        sv = sV.n
        O("scalar", lambda s, i=i: s.activation(PNEG[:, C * i:C * (i + 1)],
                                                PNEG[:, C * i:C * (i + 1)], ACT.Exp,
                                                scale=-1.0, bias=LNSC[:, 0:1]),
          [(sV, sv), (sV, lnsc_v)], sS)
        O("vector", lambda v, i=i: v.tensor_scalar(STA[:, 12:13], LAMP[:, i:i + 1],
                                                   127.0, LN_SCALE, ALU.mult, ALU.add),
          [(sS, sS.n)], sV, drain=True)
        O("vector", lambda v, i=i: v.tensor_scalar(PREV[:, C * i:C * (i + 1)], IDXT[:, :],
                                                   LAMP[:, i:i + 1], None, ALU.mult), [], sV,
          drain=True)
        sv = sV.n
        O("scalar", lambda s, i=i: s.activation(PREV[:, C * i:C * (i + 1)],
                                                PREV[:, C * i:C * (i + 1)], ACT.Exp,
                                                scale=-1.0, bias=STA[:, 12:13]),
          [(sV, sv)], sS)
        for cc_ in range(NCH):
            O("vector", lambda v, i=i, cc_=cc_: v.tensor_scalar(
                DPOW[:, NCH * i + cc_:NCH * i + cc_ + 1], LAMP[:, i:i + 1],
                float(C * cc_ + 1), None, ALU.mult), [], sV,
              drain=(cc_ == NCH - 1))
        sv = sV.n
        O("scalar", lambda s, i=i: s.activation(DPOW[:, NCH * i:NCH * (i + 1)],
                                                DPOW[:, NCH * i:NCH * (i + 1)], ACT.Exp),
          [(sV, sv)], sS)
        O("vector", lambda v, i=i: v.tensor_scalar(WCMB[:, 4 * i:4 * (i + 1)], OFFT[:, :],
                                                   LAMP[:, i:i + 1], None, ALU.mult), [], sV,
          drain=True)
        sv = sV.n
        O("scalar", lambda s, i=i: s.activation(WCMB[:, 4 * i:4 * (i + 1)],
                                                WCMB[:, 4 * i:4 * (i + 1)], ACT.Exp),
          [(sV, sv)], sS)
    tab_s = sS.n

    for i in range(4):
        O("vector", lambda v, i=i: v.tensor_copy(QHW[:, C * i:C * (i + 1)],
                                                 PREV[:, C * i:C * (i + 1)]),
          [(sS, tab_s)], sV)
    prvc = sV.n
    for i in range(4):
        O("tensor", lambda t, i=i: t.transpose(PT[:, 128 * i:128 * (i + 1)],
                                               QHW[:, C * i:C * (i + 1)], IDT[:, :]),
          [(sV, prvc), (sCN, 160)], sT)
    trp = sT.n
    O("vector", lambda v: v.tensor_copy(PRVT[:, :].rearrange("p (i c) -> p i c", i=4),
                                        PT[:, 0:512].rearrange("p (i c) -> p i c", i=4)),
      [(sT, trp)], sV)
    prvt_v = sV.n

    win_free_v = 0
    pt_free_v = prvt_v
    xe_free_v = [biascast_v, biascast_v]
    spw_evac = {0: 0, 1: 0}
    py_evac = {0: 0, 1: 0}
    cex_ld = [0, 0]
    qhw_free_t = trp
    mexp_g = 0
    x_chain_io = 0

    for g in range(NG):
        g0 = g * GRP
        par = g % 2
        cex_ld[par] = sdma(CEXW[:, GRP * par:GRP * (par + 1)], CEXT[:, g0:g0 + GRP],
                           [(sV, win_free_v)])
        sdma(SEXW[:, GRP * par:GRP * (par + 1)], SEXT[:, g0:g0 + GRP], [])
        rope_ld = sIO.n
        CEXs = CEXW[:, GRP * par:GRP * (par + 1)]
        SEXs = SEXW[:, GRP * par:GRP * (par + 1)]

        for tt in range(4):
            xb = XT0 if tt % 2 == 0 else XT1
            lv = sdma(xb[:, :], X[g0 + 128 * tt:g0 + 128 * (tt + 1), :],
                      [(sV, xe_free_v[tt % 2]), (sIO, x_chain_io)])
            x_chain_io = lv
            O("vector", lambda v, xb=xb: v.tensor_copy(X2B[:, :], xb[:, :]),
              [(sIO, lv), (sV, pt_free_v)], sV)
            cv = sV.n
            xe_free_v[tt % 2] = cv
            for j in range(8):
                O("tensor", lambda t, j=j: t.transpose(PT[:, 128 * j:128 * (j + 1)],
                                                       X2B[:, 128 * j:128 * (j + 1)],
                                                       IDT[:, :]), [(sV, cv)], sT)
            tv = sT.n
            O("vector", lambda v, tt=tt: v.tensor_copy(
                XFW[:, :].rearrange("p (j c) -> p j c", j=8)[:, :, 128 * tt:128 * (tt + 1)],
                PT[:, 0:1024].rearrange("p (j c) -> p j c", j=8)),
              [(sT, tv), (sV, win_free_v)], sV)
            pt_free_v = sV.n
        xfw_v = sV.n

        rp = [PP0, PP1, PS0, PS1]
        for i in range(4):
            for k in range(8):
                O("tensor", lambda t, i=i, k=k: t.matmul(
                    rp[i][:, 0:GRP], GWTB[:, 512 * k + 128 * i:512 * k + 128 * (i + 1)],
                    XFW[:, GRP * k:GRP * (k + 1)], start=(k == 0), stop=(k == 7)),
                  [(sV, xfw_v), (sW, wsmall_done)], sT)
        gf_t = sT.n
        for i in range(4):
            O("scalar", lambda s, i=i: s.activation(GFW[:, GRP * i:GRP * (i + 1)],
                                                    rp[i][:, 0:GRP], ACT.Sigmoid,
                                                    bias=GBPF[:, i:i + 1]),
              [(sT, gf_t), (sCN, lam_in)], sS)
        gf_s = sS.n
        for tt in range(4):
            pb = rp[tt]
            for k in range(8):
                O("tensor", lambda t, tt=tt, k=k, pb=pb: t.matmul(
                    pb[:, 0:512],
                    XFW[:, GRP * k + 128 * tt:GRP * k + 128 * (tt + 1)],
                    GWTB[:, 512 * k:512 * (k + 1)],
                    start=(k == 0), stop=(k == 7)), [(sS, gf_s)], sT)
            gt_t = sT.n
            O("vector", lambda v, tt=tt, pb=pb: v.tensor_tensor(
                GTW[:, 512 * tt:512 * (tt + 1)], pb[:, 0:512], GBT[:, :], ALU.add),
              [(sT, gt_t)], sV)
            sv = sV.n
            O("scalar", lambda s, tt=tt: s.activation(GTW[:, 512 * tt:512 * (tt + 1)],
                                                      GTW[:, 512 * tt:512 * (tt + 1)],
                                                      ACT.Sigmoid), [(sV, sv)], sS)
        gt_s = sS.n

        qsl = WSL[(3 * g) % 2]
        q_rdy = wrdy[3 * g]
        for j in range(8):
            pj = PY0 if j % 2 == 0 else PY1
            for k in range(8):
                O("tensor", lambda t, j=j, k=k, pj=pj, qsl=qsl: t.matmul(
                    pj[:, 0:GRP], qsl[:, 1024 * k + 128 * j:1024 * k + 128 * (j + 1)],
                    XFW[:, GRP * k:GRP * (k + 1)], start=(k == 0), stop=(k == 7)),
                  [(sS, gt_s), (sW, q_rdy), (sV, py_evac[j % 2])], sT)
            pj_t = sT.n
            O("vector", lambda v, pj=pj: v.tensor_copy(RPW[:, 0:GRP], pj[:, 0:GRP]),
              [(sT, pj_t)], sV)
            raw_v = sV.n
            py_evac[j % 2] = raw_v
            O("tensor", lambda t: t.matmul(PU[:, 0:GRP], SWP[:, :], RPW[:, 0:GRP],
                                           start=True, stop=True), [(sV, raw_v)], sT)
            sw_t = sT.n
            O("vector", lambda v, CEXs=CEXs: v.tensor_tensor(
                RPW[:, GRP:2 * GRP], RPW[:, 0:GRP], CEXs, ALU.mult),
              [(sIO, rope_ld)], sV)
            O("vector", lambda v, SEXs=SEXs: v.tensor_tensor(
                RPW[:, 0:GRP], PU[:, 0:GRP], SEXs, ALU.mult), [(sT, sw_t)], sV)
            O("vector", lambda v: v.tensor_tensor(RPW[:, GRP:2 * GRP], RPW[:, GRP:2 * GRP],
                                                  RPW[:, 0:GRP], ALU.add), [], sV)
            rope_v = sV.n
            pbank = rp[j // 2]
            O("tensor", lambda t, j=j, pbank=pbank: t.matmul(
                pbank[64 * (j % 2):64 * (j % 2) + 64, 0:GRP],
                BDQ[:, 64 * j:64 * (j + 1)], RPW[:, GRP:2 * GRP],
                start=True, stop=True), [(sV, rope_v)], sT)
        qr_t = sT.n
        wconsumed_t[3 * g] = qr_t
        if 3 * g + 2 < 3 * NG:
            wrdy[3 * g + 2] = load_wslot(3 * g + 2, [(sT, wconsumed_t.get(3 * g, 0))])
        for i in range(4):
            for wc in range(4):
                O("vector", lambda v, i=i, wc=wc, g0=g0: v.tensor_tensor(
                    QT[:, PER * i + g0 + C * wc:PER * i + g0 + C * (wc + 1)],
                    rp[i][:, C * wc:C * (wc + 1)], PPOS[:, C * i:C * (i + 1)], ALU.mult),
                  [(sT, qr_t)], sV)
        qt_v = sV.n

        ksl = WSL[(3 * g + 1) % 2]
        k_rdy = wrdy[3 * g + 1]
        for j in range(8):
            pj = PY0 if j % 2 == 0 else PY1
            for k in range(8):
                O("tensor", lambda t, j=j, k=k, pj=pj, ksl=ksl: t.matmul(
                    pj[:, 0:GRP], ksl[:, 1024 * k + 128 * j:1024 * k + 128 * (j + 1)],
                    XFW[:, GRP * k:GRP * (k + 1)], start=(k == 0), stop=(k == 7)),
                  [(sV, qt_v), (sW, k_rdy), (sV, py_evac[j % 2])], sT)
            pj_t = sT.n
            O("vector", lambda v, pj=pj: v.tensor_copy(RPW[:, 0:GRP], pj[:, 0:GRP]),
              [(sT, pj_t)], sV)
            raw_v = sV.n
            py_evac[j % 2] = raw_v
            O("tensor", lambda t: t.matmul(PU[:, 0:GRP], SWP[:, :], RPW[:, 0:GRP],
                                           start=True, stop=True), [(sV, raw_v)], sT)
            sw_t = sT.n
            O("vector", lambda v, j=j, CEXs=CEXs: v.tensor_tensor(
                KRW[:, GRP * j:GRP * (j + 1)], RPW[:, 0:GRP], CEXs, ALU.mult), [], sV)
            O("vector", lambda v, SEXs=SEXs: v.tensor_tensor(
                RPW[:, 0:GRP], PU[:, 0:GRP], SEXs, ALU.mult), [(sT, sw_t)], sV)
            O("vector", lambda v, j=j: v.tensor_tensor(KRW[:, GRP * j:GRP * (j + 1)],
                                                       KRW[:, GRP * j:GRP * (j + 1)],
                                                       RPW[:, 0:GRP], ALU.add), [], sV)
        kro_v = sV.n
        wconsumed_t[3 * g + 1] = sT.n
        if 3 * g + 3 < 3 * NG:
            wrdy[3 * g + 3] = load_wslot(3 * g + 3, [(sT, wconsumed_t.get(3 * g + 1, 0))])
        for j in range(8):
            pbank = rp[j // 2]
            O("tensor", lambda t, j=j, pbank=pbank: t.matmul(
                pbank[64 * (j % 2):64 * (j % 2) + 64, 0:GRP],
                BDK[:, 64 * j:64 * (j + 1)], KRW[:, GRP * j:GRP * (j + 1)],
                start=True, stop=True), [(sV, kro_v)], sT)
        krf_t = sT.n
        for i in range(4):
            for wc in range(4):
                O("vector", lambda v, i=i, wc=wc: v.tensor_tensor(
                    SPW[:, 0:C], rp[i][:, C * wc:C * (wc + 1)],
                    GFW[:, GRP * i + C * wc:GRP * i + C * (wc + 1)], ALU.mult),
                  [(sT, krf_t)], sV)
                O("vector", lambda v, i=i, wc=wc: v.tensor_tensor(
                    ATW[:, GRP * i + C * wc:GRP * i + C * (wc + 1)],
                    SPW[:, 0:C], PNEG[:, C * i:C * (i + 1)], ALU.mult), [], sV)
        atw_v = sV.n
        for tt in range(4):
            pb = PY0 if tt % 2 == 0 else PY1
            for j in range(8):
                O("tensor", lambda t, tt=tt, j=j, pb=pb: t.matmul(
                    pb[:, 64 * j:64 * (j + 1)],
                    KRW[:, GRP * j + 128 * tt:GRP * j + 128 * (tt + 1)],
                    BDK[:, 64 * j:64 * (j + 1)], start=True, stop=True),
                  [(sV, atw_v), (sV, py_evac[tt % 2])], sT)
            kt_t = sT.n
            O("vector", lambda v, tt=tt, pb=pb: v.tensor_tensor(
                SPW[:, 0:512], pb[:, 0:512], GTW[:, 512 * tt:512 * (tt + 1)], ALU.mult),
              [(sT, kt_t), (sV, prvt_v)], sV)
            O("vector", lambda v, tt=tt: v.tensor_tensor(
                AKW[:, 512 * tt:512 * (tt + 1)], SPW[:, 0:512], PRVT[:, :], ALU.mult),
              [], sV)
            py_evac[tt % 2] = sV.n
        akw_v = sV.n

        vsl = WSL[(3 * g + 2) % 2]
        v_rdy = wrdy[3 * g + 2]
        for tt in range(4):
            for hh in range(2):
                pb = PY0 if hh == 0 else PY1
                for k in range(8):
                    O("tensor", lambda t, tt=tt, hh=hh, k=k, pb=pb, vsl=vsl: t.matmul(
                        pb[:, 0:512],
                        XFW[:, GRP * k + 128 * tt:GRP * k + 128 * (tt + 1)],
                        vsl[:, 1024 * k + 512 * hh:1024 * k + 512 * (hh + 1)],
                        start=(k == 0), stop=(k == 7)),
                      [(sV, akw_v), (sW, v_rdy), (sV, py_evac[hh])], sT)
                vt_t = sT.n
                O("vector", lambda v, tt=tt, hh=hh, pb=pb: v.tensor_copy(
                    VW[:, 1024 * tt + 512 * hh:1024 * tt + 512 * (hh + 1)], pb[:, 0:512]),
                  [(sT, vt_t)], sV)
                py_evac[hh] = sV.n
        vw_v = sV.n
        if DEBUG and g == 0:
            sdma(DAT, ATW[:, :], [(sV, atw_v)])
            sdma(DAK, AKW[:, :], [(sV, akw_v)])
            sdma(DVW, VW[:, :], [(sV, vw_v)])
            sdma(DXF, XFW[:, :], [(sV, xfw_v)])
        wconsumed_t[3 * g + 2] = sT.n

        for wc in range(4):
            for i in range(4):
                O("gpsimd", lambda g_, i=i, wc=wc, g0=g0: g_.tensor_scalar(
                    QHW[:, C * i:C * (i + 1)],
                    QT[:, PER * i + g0 + C * wc:PER * i + g0 + C * (wc + 1)],
                    DECP[:, i:i + 1], 1.0, ALU.mult, ALU.mult),
                  [(sV, vw_v), (sT, qhw_free_t)], sG)
            qh_g = sG.n
            for u in range(4):
                pb = [PS0, PS1, PY0, PY1][u]
                bfree = spw_evac[u] if u < 2 else py_evac[u - 2]
                for kk in range(4):
                    h = u + 4 * kk
                    i = h // 4
                    O("tensor", lambda t, i=i, u=u, kk=kk, wc=wc, g0=g0, pb=pb: t.matmul(
                        pb[:, C * kk:C * (kk + 1)],
                        ATW[32 * u:32 * u + 32, GRP * i + C * wc:GRP * i + C * (wc + 1)],
                        QT[32 * u:32 * u + 32,
                           PER * i + g0 + C * wc:PER * i + g0 + C * (wc + 1)],
                        start=True, stop=True, tile_position=(32 * u, 0)),
                      [(sG, qh_g), (sV, vw_v), (sV, bfree)], sT)
                sc_t = sT.n
                O("vector", lambda v, u=u, pb=pb: v.tensor_tensor(
                    SPW[:, 4 * C * u:4 * C * (u + 1)], pb[:, 0:4 * C], MSKT4[:, :],
                    ALU.mult), [(sT, sc_t)], sV)
            sc_v = sV.n
            for j in range(8):
                pb = PY0 if j % 2 == 0 else PY1
                for hh in range(2):
                    h = 2 * j + hh
                    slot = 4 * (h % 4) + h // 4
                    O("tensor", lambda t, h=h, hh=hh, wc=wc, pb=pb, slot=slot: t.matmul(
                        pb[64 * hh:64 * hh + 64, 0:C],
                        VW[:, 1024 * wc + 64 * h:1024 * wc + 64 * (h + 1)],
                        SPW[:, C * slot:C * (slot + 1)], start=True, stop=False),
                      [(sV, sc_v), (sV, py_evac[j % 2]), (sG, mexp_g)], sT)
                    O("tensor", lambda t, h=h, hh=hh, pb=pb: t.matmul(
                        pb[64 * hh:64 * hh + 64, 0:C], MEXP[:, 64 * h:64 * (h + 1)],
                        QHW[:, C * (h // 4):C * (h // 4 + 1)], start=False, stop=True),
                      [], sT)
                y_t = sT.n
                O("vector", lambda v, j=j, g0=g0, wc=wc, pb=pb: v.tensor_copy(
                    YF[:, PER * j + g0 + C * wc:PER * j + g0 + C * (wc + 1)],
                    pb[:, 0:C]), [(sT, y_t)], sV)
                py_evac[j % 2] = sV.n
            y_v = sV.n
            qhw_free_t = sT.n
            for qg in range(4):
                pb = PS0 if qg < 2 else PS1
                O("tensor", lambda t, qg=qg, wc=wc, pb=pb: t.matmul(
                    pb[:, 256 * (qg % 2):256 * (qg % 2) + 256],
                    AKW[:, 512 * wc + 128 * qg:512 * wc + 128 * (qg + 1)],
                    VW[:, 1024 * wc + 256 * qg:1024 * wc + 256 * (qg + 1)],
                    start=True, stop=True), [(sV, y_v)], sT)
            u_t = sT.n
            for h in range(H):
                qg, u = h // 4, h % 4
                pb = PS0 if qg < 2 else PS1
                O("vector", lambda v, qg=qg, u=u, pb=pb: v.scalar_tensor_tensor(
                    MST[32 * u:32 * u + 32, 64 * qg:64 * (qg + 1)],
                    MST[32 * u:32 * u + 32, 64 * qg:64 * (qg + 1)],
                    DECC[32 * u:32 * u + 32, qg:qg + 1],
                    pb[32 * u:32 * u + 32,
                       256 * (qg % 2) + 64 * u:256 * (qg % 2) + 64 * (u + 1)],
                    ALU.mult, ALU.add), [(sT, u_t), (sG, mexp_g)], sV)
            mst_v = sV.n
            for h in range(H):
                qg, u = h // 4, h % 4
                O("gpsimd", lambda g_, h=h, qg=qg, u=u: g_.tensor_copy(
                    MEXP[32 * u:32 * u + 32, 64 * h:64 * (h + 1)],
                    MST[32 * u:32 * u + 32, 64 * qg:64 * (qg + 1)]),
                  [(sV, mst_v), (sT, qhw_free_t)], sG)
            mexp_g = sG.n
            spw_evac[0] = mst_v
            spw_evac[1] = mst_v
            win_free_v = mst_v
        if 3 * g + 4 < 3 * NG:
            wrdy[3 * g + 4] = load_wslot(3 * g + 4, [(sT, wconsumed_t[3 * g + 2])])
    scan_v = sV.n
    scan_t = sT.n
    if DEBUG:
        sdma(DQT, QT[:, :], [(sV, scan_v)])
        sdma(DMST, MST[:, :], [(sV, scan_v)])
        sdma(DTAB[:, 0:4 * C], PPOS[:, :], [(sS, tab_s)])
        sdma(DTAB[:, 4 * C:8 * C], PNEG[:, :], [])
        sdma(DTAB[:, 8 * C:12 * C], PREV[:, :], [])
        sdma(DTAB[:, 12 * C:12 * C + 512], PRVT[:, :], [(sV, prvt_v)])
        sdma(DTAB[:, 12 * C + 512:12 * C + 516], LAMP[:, :], [(sV, scan_v)])
        sdma(DTAB[:, 12 * C + 516:12 * C + 520], DECP[:, :], [])
        sdma(DTAB[:, 12 * C + 520:12 * C + 524], DECC[:, :], [])

    st_out = sdma(CCIN.ap(), MST[:, :], [(sV, scan_v)])
    O("gpsimd", lambda g_: g_.collective_compute(
        "AllGather", ALU.bypass, replica_groups=[[0, 1, 2, 3], [4, 5, 6, 7]],
        ins=[CCIN.ap().opt()], outs=[CCOUT.ap().opt()]),
      [(sIO, st_out)], sCC)
    cc_n = sCC.n
    for m in range(4):
        sdma(SGT[:, 256 * m:256 * (m + 1)], CCOUT.ap()[128 * m:128 * (m + 1), :],
             [(sCC, cc_n)])
    sg_io = sIO.n
    O("vector", lambda v: v.memset(MST[:, :], 0.0), [(sIO, sg_io)], sV)
    for m in range(4):
        for i in range(4):
            O("vector", lambda v, i=i, m=m: v.scalar_tensor_tensor(
                MST[:, 64 * i:64 * (i + 1)],
                SGT[:, 256 * m + 64 * i:256 * m + 64 * (i + 1)],
                WCMB[:, 4 * i + m:4 * i + m + 1],
                MST[:, 64 * i:64 * (i + 1)], ALU.mult, ALU.add), [], sV, drain=True)
    O("vector", lambda v: v.memset(SPCV[:, :], 0.0), [], sV, drain=True)
    for h in range(H):
        qg, u = h // 4, h % 4
        O("vector", lambda v, h=h, qg=qg, u=u: v.tensor_copy(
            MSTE[32 * u:32 * u + 32, 64 * h:64 * (h + 1)],
            MST[32 * u:32 * u + 32, 64 * qg:64 * (qg + 1)]), [], sV)
    mste_v = sV.n
    if DEBUG:
        sdma(DMSTE, MSTE[:, :], [(sV, mste_v)])
        sdma(DSGT, SGT[:, :], [(sV, mste_v)])

    QGW = KRW
    corr_evac = {0: 0, 1: 0}
    for i in range(4):
        for cc_ in range(NCH):
            O("vector", lambda v, i=i, cc_=cc_: v.tensor_scalar(
                QGW[:, C * cc_:C * (cc_ + 1)],
                QT[:, PER * i + C * cc_:PER * i + C * (cc_ + 1)],
                DPOW[:, NCH * i + cc_:NCH * i + cc_ + 1], None, ALU.mult),
              [(sV, mste_v)], sV)
        qg_v = sV.n
        for u in range(4):
            h = 4 * i + u
            for qq in range(4):
                n = u * 4 + qq
                pb = PY0 if n % 2 == 0 else PY1
                O("tensor", lambda t, h=h, qq=qq, pb=pb: t.matmul(
                    pb[0:64, 0:512], MSTE[:, 64 * h:64 * (h + 1)],
                    QGW[:, 512 * qq:512 * (qq + 1)], start=True, stop=True),
                  [(sV, qg_v), (sT, scan_t), (sV, corr_evac[n % 2])], sT)
                co_t = sT.n
                O("vector", lambda v, h=h, qq=qq, pb=pb: v.tensor_tensor(
                    YF[64 * (h % 2):64 * (h % 2) + 64,
                       PER * (h // 2) + 512 * qq:PER * (h // 2) + 512 * (qq + 1)],
                    YF[64 * (h % 2):64 * (h % 2) + 64,
                       PER * (h // 2) + 512 * qq:PER * (h // 2) + 512 * (qq + 1)],
                    pb[0:64, 0:512], ALU.add), [(sT, co_t)], sV)
                corr_evac[n % 2] = sV.n
    corr_v = sV.n
    corr_t = sT.n
    if DEBUG:
        sdma(DYF, YF[:, :], [(sV, corr_v)])

    esA.close()
    FFW = sbp(es, "ffw", 128, 24576, BF16)
    X2F = sbp(es, "x2f", 128, 8 * PER, BF16)

    for k in range(8):
        gdma(FFW[:, 1024 * k:1024 * (k + 1)], OWT[128 * k:128 * (k + 1), :],
             [(sT, corr_t)])
    owt_w = sW.n

    if DEBUG:
        sdma(DOBR, OBR[:, :], [(sV, corr_v)])
    x2tok_wr = [0] * NCH
    x2f_tile = [0] * NCH
    wfe_tile = [0] * NCH
    h1t_free_io = 0
    du_free_io = 0
    dst_free_io = 0
    op_evac = {0: corr_v, 1: corr_v}
    pt_free2 = pt_free_v
    for tt in range(NCH):
        xb = XT0 if tt % 2 == 0 else XT1
        lx = sdma(xb[:, :], X[128 * tt:128 * (tt + 1), :],
                  [(sV, xe_free_v[tt % 2]), (sIO, x_chain_io)])
        x_chain_io = lx
        for half in range(2):
            pb = PY0 if half == 0 else PY1
            for k in range(8):
                O("tensor", lambda t, tt=tt, half=half, k=k, pb=pb: t.matmul(
                    pb[:, 0:512], YF[:, PER * k + 128 * tt:PER * k + 128 * (tt + 1)],
                    FFW[:, 1024 * k + 512 * half:1024 * k + 512 * (half + 1)],
                    start=(k == 0), stop=(k == 7)),
                  [(sV, corr_v), (sW, owt_w), (sV, op_evac[half])], sT)
        op_t = sT.n
        for half, pb in ((0, PY0), (1, PY1)):
            O("vector", lambda v, half=half, pb=pb: v.tensor_tensor(
                H1T[:, 512 * half:512 * (half + 1)], pb[:, 0:512],
                OBR[:, 512 * half:512 * (half + 1)], ALU.add),
              [(sT, op_t), (sIO, lx), (sIO, h1t_free_io), (sW, biasld_w)], sV)
            O("vector", lambda v, half=half, xb=xb: v.tensor_tensor(
                H1T[:, 512 * half:512 * (half + 1)], H1T[:, 512 * half:512 * (half + 1)],
                xb[:, 512 * half:512 * (half + 1)], ALU.add), [], sV)
            op_evac[half] = sV.n
        h1_v = sV.n
        if DEBUG:
            dh1w = sdma(DH1[128 * tt:128 * (tt + 1), :], H1T[:, :], [(sV, h1_v)])
            h1t_free_io = dh1w
        O("vector", lambda v: v.reduce_sum(STA[:, 0:1], H1T[:, :], axis=AXX),
          [(sV, h1_v), (sIO, du_free_io), (sIO, dst_free_io)], sV)
        O("vector", lambda v: v.scalar_tensor_tensor(OUTT1[:, :], H1T[:, :], 1.0,
                                                     H1T[:, :], ALU.mult, ALU.mult), [], sV)
        O("vector", lambda v: v.reduce_sum(STA[:, 1:2], OUTT1[:, :], axis=AXX), [], sV,
          drain=True)
        st1v = sV.n
        O("scalar", lambda s: s.activation(STA[:, 2:3], STA[:, 0:1], ACT.Identity,
                                           scale=1.0 / HS), [(sV, st1v)], sS)
        O("scalar", lambda s: s.activation(STA[:, 3:4], STA[:, 0:1], ACT.Square,
                                           scale=1.0 / HS), [], sS, drain=True)
        st1s = sS.n
        O("vector", lambda v: v.scalar_tensor_tensor(STA[:, 4:5], STA[:, 1:2], 1.0 / HS,
                                                     STA[:, 3:4], ALU.mult, ALU.subtract),
          [(sS, st1s)], sV, drain=True)
        va1 = sV.n
        O("scalar", lambda s: s.activation(STA[:, 6:7], STA[:, 4:5], ACT.Ln,
                                           bias=EPSC[:, 0:1]), [(sV, va1)], sS, drain=True)
        O("scalar", lambda s: s.activation(STA[:, 7:8], STA[:, 6:7], ACT.Exp, scale=-0.5),
          [], sS, drain=True)
        rs1 = sS.n
        O("vector", lambda v: v.tensor_scalar(OUTT0[:, :], H1T[:, :], STA[:, 2:3],
                                              STA[:, 7:8], ALU.subtract, ALU.mult),
          [(sS, rs1)], sV)
        O("vector", lambda v: v.tensor_tensor(OUTT0[:, :], OUTT0[:, :], LNGR[:, :],
                                              ALU.mult), [], sV)
        O("vector", lambda v: v.tensor_tensor(OUTT0[:, :], OUTT0[:, :], LNBR[:, :],
                                              ALU.add), [], sV)
        O("vector", lambda v, xb=xb: v.tensor_tensor(OUTT0[:, :], OUTT0[:, :], xb[:, :],
                                                     ALU.add), [], sV)
        u_v = sV.n
        if DEBUG:
            du_free_io = sdma(DU[128 * tt:128 * (tt + 1), :], OUTT0[:, :], [(sV, u_v)])
        xe_free_v[tt % 2] = u_v
        O("vector", lambda v: v.reduce_sum(STA[:, 8:9], OUTT0[:, :], axis=AXX),
          [(sV, u_v)], sV)
        O("vector", lambda v: v.scalar_tensor_tensor(OUTT1[:, :], OUTT0[:, :], 1.0,
                                                     OUTT0[:, :], ALU.mult, ALU.mult),
          [], sV)
        O("vector", lambda v: v.reduce_sum(STA[:, 9:10], OUTT1[:, :], axis=AXX), [], sV,
          drain=True)
        st2v = sV.n
        O("scalar", lambda s: s.activation(STA[:, 10:11], STA[:, 8:9], ACT.Identity,
                                           scale=1.0 / HS), [(sV, st2v)], sS)
        O("scalar", lambda s: s.activation(STA[:, 11:12], STA[:, 8:9], ACT.Square,
                                           scale=1.0 / HS), [], sS, drain=True)
        st2s = sS.n
        O("vector", lambda v: v.scalar_tensor_tensor(STA[:, 12:13], STA[:, 9:10], 1.0 / HS,
                                                     STA[:, 11:12], ALU.mult,
                                                     ALU.subtract), [(sS, st2s)], sV,
          drain=True)
        va2 = sV.n
        O("scalar", lambda s: s.activation(STA[:, 14:15], STA[:, 12:13], ACT.Ln,
                                           bias=EPSC[:, 0:1]), [(sV, va2)], sS, drain=True)
        O("scalar", lambda s: s.activation(STA[:, 15:16], STA[:, 14:15], ACT.Exp,
                                           scale=-0.5), [], sS, drain=True)
        rs2 = sS.n
        O("vector", lambda v: v.tensor_scalar(H1T[:, :], OUTT0[:, :], STA[:, 10:11],
                                              STA[:, 15:16], ALU.subtract, ALU.mult),
          [(sS, rs2)], sV)
        x2_v = sV.n
        if DEBUG:
            dst_free_io = sdma(DST2[:, 16 * tt:16 * (tt + 1)], STA[:, :], [(sV, x2_v)])
        x2tok_wr[tt] = sdma(X2TOK[128 * tt:128 * (tt + 1), :], H1T[:, :], [(sV, x2_v)])
        h1t_free_io = x2tok_wr[tt]
        O("vector", lambda v: v.tensor_copy(X2B[:, :], H1T[:, :]), [(sV, pt_free2)], sV)
        x2b_v = sV.n
        for j in range(8):
            O("tensor", lambda t, j=j: t.transpose(PT[:, 128 * j:128 * (j + 1)],
                                                   X2B[:, 128 * j:128 * (j + 1)],
                                                   IDT[:, :]), [(sV, x2b_v)], sT)
        tx_t = sT.n
        O("vector", lambda v, tt=tt: v.tensor_copy(
            X2F[:, :].rearrange("p (j c) -> p j c", j=8)[:, :, 128 * tt:128 * (tt + 1)],
            PT[:, 0:1024].rearrange("p (j c) -> p j c", j=8)), [(sT, tx_t)], sV)
        x2f_v = sV.n
        x2f_tile[tt] = x2f_v
        for k in range(8):
            O("tensor", lambda t, tt=tt, k=k: t.matmul(
                PU[:, 0:E], X2F[:, PER * k + 128 * tt:PER * k + 128 * (tt + 1)],
                WR8B[:, E * k:E * (k + 1)], start=(k == 0), stop=(k == 7)),
              [(sV, x2f_v)], sT)
        lg_t = sT.n
        O("vector", lambda v: v.tensor_tensor(LGT[:, :], PU[:, 0:E], BRR[:, :], ALU.add),
          [(sT, lg_t)], sV, drain=True)
        O("vector", lambda v: v.reduce_max(GTS[:, 0:1], LGT[:, :], axis=AXX), [], sV,
          drain=True)
        mx_v = sV.n
        O("scalar", lambda s: s.activation(GTS[:, 1:2], GTS[:, 0:1], ACT.Identity,
                                           scale=-1.0), [(sV, mx_v)], sS, drain=True)
        O("scalar", lambda s: s.activation(EX8[:, :], LGT[:, :], ACT.Exp,
                                           bias=GTS[:, 1:2]), [], sS, drain=True)
        ex_s = sS.n
        O("vector", lambda v: v.reduce_sum(GTS[:, 3:4], EX8[:, :], axis=AXX),
          [(sS, ex_s)], sV, drain=True)
        O("vector", lambda v, tt=tt: v.tensor_scalar(WTK[:, E * tt:E * (tt + 1)],
                                                     LGT[:, :], GTS[:, 0:1], None,
                                                     ALU.is_ge), [], sV)
        O("vector", lambda v: v.reciprocal(GTS[:, 4:5], GTS[:, 3:4]), [], sV, drain=True)
        O("vector", lambda v, tt=tt: v.tensor_scalar(WTK[:, E * tt:E * (tt + 1)],
                                                     WTK[:, E * tt:E * (tt + 1)],
                                                     GTS[:, 4:5], None, ALU.mult), [], sV,
          drain=True)
        if DEBUG:
            O("vector", lambda v, tt=tt: v.tensor_copy(DBGL[:, E * tt:E * (tt + 1)],
                                                       LGT[:, :]), [], sV)
            O("vector", lambda v, tt=tt: v.tensor_copy(DBGG[:, 8 * tt:8 * (tt + 1)],
                                                       GTS[:, :]), [], sV)
        O("vector", lambda v, tt=tt: v.tensor_copy(X2B[:, 0:E],
                                                   WTK[:, E * tt:E * (tt + 1)]), [], sV,
          drain=True)
        wtk_v = sV.n
        O("tensor", lambda t: t.transpose(PT[0:E, 0:128], X2B[:, 0:E], IDT[:, :]),
          [(sV, wtk_v)], sT)
        wt_t = sT.n
        O("vector", lambda v, tt=tt: v.tensor_copy(WFE[0:E, 128 * tt:128 * (tt + 1)],
                                                   PT[0:E, 0:128]), [(sT, wt_t)], sV)
        pt_free2 = sV.n
        wfe_tile[tt] = sV.n
    epiA_v = sV.n
    epiA_t = sT.n
    if DEBUG:
        sdma(DWTK, WTK[:, :], [(sV, epiA_v)])
        sdma(DWFE, WFE[:, :], [(sV, epiA_v)])
        sdma(DLGT, DBGL[:, :], [(sV, epiA_v)])
        sdma(DGTS, DBGG[:, :], [(sV, epiA_v)])

    esB.close()
    YACC = sbp(es, "yacc", 128, 8192, F32)
    HWB = sbp(es, "hwb", 128, 8192, BF16)
    HT2 = sbp(es, "ht2", 128, 4096, BF16)
    B1FT = sbp(es, "b1ft", 128, 256, F32)
    ldb1 = sdma(B1FT[:, :], B1FD, [])

    slots = [8192, 16384, 0]
    loads = []
    for half in range(2):
        for e in range(E):
            for dc in range(4):
                loads.append(("w1", e, dc))
                loads.append(("w2", e, dc))
    slot_prev_t = {8192: corr_t, 16384: corr_t, 0: epiA_t}
    wload_done = [0] * len(loads)
    load_consumer_t = [0] * len(loads)

    def emit_wload(idx):
        kind, e, dc = loads[idx]
        col0 = slots[idx % 3]
        prev_t = load_consumer_t[idx - 3] if idx >= 3 else slot_prev_t[col0]
        if kind == "w1":
            src = W1D[e, :, 1024 * dc:1024 * (dc + 1)].rearrange(
                "(k p) c -> p k c", p=128)
        else:
            src = W2D[e, 1024 * dc:1024 * (dc + 1), :].rearrange(
                "(m p) c -> p m c", p=128)
        dst = FFW[:, col0:col0 + 8192].rearrange("p (k c) -> p k c", c=1024)
        O("gpsimd", lambda g_, dst=dst, src=src: g_.dma_start(out=dst, in_=src),
          [(sT, prev_t)], sW, 16)
        return sW.n

    wload_done[0] = emit_wload(0)
    wload_done[1] = emit_wload(1)
    nextload = 2

    yps = [PY0, PY1, PS0, PS1]
    yev = {0: 0, 1: 0, 2: 0, 3: 0}
    hev = {0: 0, 1: 0}
    hcv = {0: 0, 1: 0}
    hw_used_t = {0: 0, 1: 0}
    xr_free = [0, 0]
    out_done = 0
    ln2_prev_s = 0
    for half in range(2):
        hoff = 1024 * half
        lxs = [0] * 9
        for tt in range(8):
            gtile = half * 8 + tt
            xb = XR0 if tt % 2 == 0 else XR1
            lxs[tt] = sdma(xb[:, :], X2TOK[128 * gtile:128 * (gtile + 1), :],
                           [(sV, xr_free[tt % 2]), (sIO, x2tok_wr[gtile])])
            if tt == 7:
                lxs[8] = sdma(SGT[:, 0:256], X2TOK[128 * gtile:128 * gtile + 128, 0:256],
                              [])
            for hh in range(2):
                bi = (2 * tt + hh) % 4
                pb = yps[bi]
                O("tensor", lambda t, gtile=gtile, hh=hh, pb=pb: t.matmul(
                    pb[:, 0:512], WFE[0:E, 128 * gtile:128 * (gtile + 1)],
                    B2B[0:E, 512 * hh:512 * (hh + 1)], start=True, stop=True),
                  [(sV, wfe_tile[gtile]), (sW, wsmall_done), (sV, yev[bi])], sT)
                b2_t = sT.n
                O("vector", lambda v, tt=tt, hh=hh, xb=xb, pb=pb: v.tensor_tensor(
                    YACC[:, 1024 * tt + 512 * hh:1024 * tt + 512 * (hh + 1)],
                    pb[:, 0:512], xb[:, 512 * hh:512 * (hh + 1)], ALU.add),
                  [(sT, b2_t), (sIO, lxs[tt] + 16), (sS, ln2_prev_s)], sV)
                yev[bi] = sV.n
            xr_free[tt % 2] = sV.n
        yinit_v = sV.n

        for e in range(E):
            for dc in range(4):
                li = 2 * (32 * half + 4 * e + dc)
                w1_rdy = wload_done[li]
                w2_rdy = wload_done[li + 1]
                s1 = slots[li % 3]
                s2 = slots[(li + 1) % 3]
                for gg in range(2):
                    hbuf = 4096 * (gg % 2)
                    for m in range(8):
                        pb = PP0 if m % 2 == 0 else PP1
                        for k in range(8):
                            O("tensor", lambda t, m=m, k=k, gg=gg, s1=s1, hoff=hoff,
                              pb=pb: t.matmul(
                                  pb[:, 0:512],
                                  FFW[:, s1 + 1024 * k + 128 * m:s1 + 1024 * k + 128 * (m + 1)],
                                  X2F[:, PER * k + hoff + 512 * gg:PER * k + hoff + 512 * (gg + 1)],
                                  start=(k == 0), stop=(k == 7)),
                              [(sW, w1_rdy), (sV, x2f_tile[half * 8 + 4 * gg + 3]),
                               (sS, hev[m % 2])], sT)
                        h_t = sT.n
                        O("scalar", lambda s, m=m, e=e, dc=dc, hbuf=hbuf, pb=pb:
                          s.activation(HWB[:, hbuf + 512 * m:hbuf + 512 * (m + 1)],
                                       pb[:, 0:512], ACT.Gelu,
                                       bias=B1FT[:, 32 * e + 8 * dc + m:32 * e + 8 * dc + m + 1]),
                          [(sT, h_t), (sIO, ldb1), (sT, hw_used_t[gg % 2])], sS)
                        hev[m % 2] = sS.n
                    hg_s = smargin()
                    for t4 in range(4):
                        tt = 4 * gg + t4
                        for hh in range(2):
                            bi = (2 * tt + hh) % 4
                            pb = yps[bi]
                            for m in range(8):
                                O("tensor", lambda t, m=m, hh=hh, t4=t4, hbuf=hbuf,
                                  s2=s2, pb=pb: t.matmul(
                                      pb[:, 0:512],
                                      HWB[:, hbuf + 512 * m + 128 * t4:hbuf + 512 * m + 128 * (t4 + 1)],
                                      FFW[:, s2 + 1024 * m + 512 * hh:s2 + 1024 * m + 512 * (hh + 1)],
                                      start=(m == 0), stop=(m == 7)),
                                  [(sS, hg_s), (sW, w2_rdy), (sV, yev[bi])], sT)
                            yy_t = sT.n
                            O("vector", lambda v, tt=tt, hh=hh, e=e, half=half, pb=pb:
                              v.scalar_tensor_tensor(
                                  YACC[:, 1024 * tt + 512 * hh:1024 * tt + 512 * (hh + 1)],
                                  pb[:, 0:512],
                                  WTK[:, E * (half * 8 + tt) + e:E * (half * 8 + tt) + e + 1],
                                  YACC[:, 1024 * tt + 512 * hh:1024 * tt + 512 * (hh + 1)],
                                  ALU.mult, ALU.add), [(sT, yy_t)], sV)
                            yev[bi] = sV.n
                    hw_used_t[gg % 2] = sT.n
                load_consumer_t[li] = sT.n
                load_consumer_t[li + 1] = sT.n
                if nextload < len(loads):
                    wload_done[nextload] = emit_wload(nextload)
                    nextload += 1
                if nextload < len(loads):
                    wload_done[nextload] = emit_wload(nextload)
                    nextload += 1
        ffn_v = sV.n
        if DEBUG and half == 0:
            sdma(DYA, YACC[:, :], [(sV, ffn_v)])
        for tt in range(8):
            gtile = half * 8 + tt
            ob = OUTT0 if tt % 2 == 0 else OUTT1
            O("vector", lambda v, tt=tt: v.reduce_sum(
                STA[:, 0:1], YACC[:, 1024 * tt:1024 * (tt + 1)], axis=AXX),
              [(sV, ffn_v), (sIO, out_done)], sV)
            O("vector", lambda v, tt=tt, ob=ob: v.scalar_tensor_tensor(
                ob[:, :], YACC[:, 1024 * tt:1024 * (tt + 1)], 1.0,
                YACC[:, 1024 * tt:1024 * (tt + 1)], ALU.mult, ALU.mult), [], sV)
            O("vector", lambda v, ob=ob: v.reduce_sum(STA[:, 1:2], ob[:, :], axis=AXX),
              [], sV, drain=True)
            stfv = sV.n
            O("scalar", lambda s: s.activation(STA[:, 2:3], STA[:, 0:1], ACT.Identity,
                                               scale=1.0 / HS), [(sV, stfv)], sS)
            O("scalar", lambda s: s.activation(STA[:, 3:4], STA[:, 0:1], ACT.Square,
                                               scale=1.0 / HS), [], sS, drain=True)
            stfs = sS.n
            O("vector", lambda v: v.scalar_tensor_tensor(STA[:, 4:5], STA[:, 1:2],
                                                         1.0 / HS, STA[:, 3:4],
                                                         ALU.mult, ALU.subtract),
              [(sS, stfs)], sV, drain=True)
            va = sV.n
            O("scalar", lambda s: s.activation(STA[:, 6:7], STA[:, 4:5], ACT.Ln,
                                               bias=EPSC[:, 0:1]), [(sV, va)], sS,
              drain=True)
            O("scalar", lambda s: s.activation(STA[:, 7:8], STA[:, 6:7], ACT.Exp,
                                               scale=-0.5), [], sS, drain=True)
            rs = sS.n
            O("vector", lambda v, tt=tt, ob=ob: v.tensor_scalar(
                ob[:, :], YACC[:, 1024 * tt:1024 * (tt + 1)], STA[:, 2:3], STA[:, 7:8],
                ALU.subtract, ALU.mult), [(sS, rs)], sV)
            o_v = sV.n
            out_done = sdma(OUT[128 * gtile:128 * (gtile + 1), :], ob[:, :], [(sV, o_v)])
        ln2_prev_s = sS.n

    final_io = sIO.n
    O("sync", lambda s: s.nop(), [(sIO, final_io)], None)

    global _LAST_SEMS
    _LAST_SEMS = {s.name: s.n for s in (sW, sIO, sT, sV, sS, sCC)}
    _LAST_SEMS["_ids"] = {s.name: s.h for s in (sW, sIO, sT, sV, sS, sCC)}
    with nc.Block() as block:
        bld.emit(block)
    es.close()
    return nc


_NC_CACHE = None


def kernel(**inputs):
    global _NC_CACHE
    cores = host_prepare(inputs)
    if _NC_CACHE is None:
        _NC_CACHE = build_nc()
    nc = _NC_CACHE
    res = run_bass_kernel_spmd(nc, cores, core_ids=list(range(8)))
    out = np.empty((B, T, HS), np.float32)
    for c in range(8):
        b = c // 4
        t0 = (c % 4) * PER
        out[b, t0:t0 + PER] = np.asarray(res.results[c]["out"])
    return out
